# version 12
# baseline (speedup 1.0000x reference)
import contextlib
import numpy as np
import ml_dtypes
import concourse.bass as bass
import concourse.mybir as mybir
from concourse.bass_utils import run_bass_kernel_spmd

F32 = mybir.dt.float32
BF16 = mybir.dt.bfloat16
AF = mybir.ActivationFunctionType
AX = mybir.AxisListType
OP = mybir.AluOpType
bf16 = ml_dtypes.bfloat16

NCORES = 8
B, N, C = 2, 8192, 1024
TLOC = 2048
NT = TLOC // 128
EPS = 1e-5


class Sems:

    def __init__(self, nc, stack, prefix=""):
        self.nc, self.stack = nc, stack
        self.prefix = prefix
        self.h = {}
        self.n = {}

    def new(self, *names):
        for nm in names:
            self.h[nm] = self.stack.enter_context(
                self.nc.semaphore(self.prefix + nm))
            self.n[nm] = 0

    def inc(self, inst, nm, k=1):
        inst.then_inc(self.h[nm], k)
        self.n[nm] += k
        return self.n[nm]

    def wait(self, eng, nm, val):
        assert 0 <= val <= self.n[nm], f"wait on future {nm}: {val} > {self.n[nm]}"
        if val > 0:
            eng.wait_ge(self.h[nm], val)

    def waitf(self, eng, nm, val):
        if val > 0:
            eng.wait_ge(self.h[nm], val)
            m = getattr(self, "maxw", None)
            if m is None:
                m = self.maxw = {}
            m[nm] = max(m.get(nm, 0), val)

    def check_futures(self):
        for nm, val in getattr(self, "maxw", {}).items():
            assert val <= self.n[nm], f"unmet future wait {nm}: {val} > {self.n[nm]}"

    def chain(self, eng, inst, nm):
        inst.then_inc(self.h[nm], 1)
        self.n[nm] += 1
        eng.wait_ge(self.h[nm], self.n[nm])


def build_ln1():
    nc = bass.Bass("TRN2", target_bir_lowering=False)
    x = nc.dram_tensor("x", [TLOC, C], F32, kind="ExternalInput").ap()
    ident = nc.dram_tensor("ident", [128, 128], BF16, kind="ExternalInput").ap()
    xn0 = nc.dram_tensor("xn0", [TLOC, C], F32, kind="ExternalOutput").ap()
    xn0T = nc.dram_tensor("xn0T", [C, TLOC], BF16, kind="ExternalOutput").ap()
    xlnbT = nc.dram_tensor("xlnbT", [C, TLOC], BF16, kind="ExternalOutput").ap()

    with contextlib.ExitStack() as stack:
        E = stack.enter_context
        sx = [E(nc.sbuf_tensor(f"sx{j}", [128, C], F32)) for j in range(2)]
        sxc = E(nc.sbuf_tensor("sxc", [128, C], F32))
        sxc2 = E(nc.sbuf_tensor("sxc2", [128, C], F32))
        scrap = E(nc.sbuf_tensor("scrap", [128, C], F32))
        sxn0 = [E(nc.sbuf_tensor(f"sxn0_{j}", [128, C], F32)) for j in range(2)]
        sxn0b = E(nc.sbuf_tensor("sxn0b", [128, C], BF16))
        sxlnb = E(nc.sbuf_tensor("sxlnb", [128, C], BF16))
        sident = E(nc.sbuf_tensor("sident", [128, 128], BF16))
        st1 = E(nc.sbuf_tensor("st1", [128, 1], F32))
        smu = E(nc.sbuf_tensor("smu", [128, 1], F32))
        sss = E(nc.sbuf_tensor("sss", [128, 1], F32))
        svar = E(nc.sbuf_tensor("svar", [128, 1], F32))
        slog = E(nc.sbuf_tensor("slog", [128, 1], F32))
        srstd = E(nc.sbuf_tensor("srstd", [128, 1], F32))
        smu2 = E(nc.sbuf_tensor("smu2", [128, 1], F32))
        sss2 = E(nc.sbuf_tensor("sss2", [128, 1], F32))
        svar2 = E(nc.sbuf_tensor("svar2", [128, 1], F32))
        srstd2 = E(nc.sbuf_tensor("srstd2", [128, 1], F32))
        sT0 = [E(nc.sbuf_tensor(f"sT0_{k}", [128, TLOC], BF16)) for k in range(8)]
        sT1 = [E(nc.sbuf_tensor(f"sT1_{k}", [128, TLOC], BF16)) for k in range(8)]
        pt = [E(nc.psum_tensor(f"pt{j}", [128, 512], BF16)) for j in range(4)]

        S = Sems(nc, stack)
        S.new("ld", "v", "a", "t", "st", "vs", "as_")
        block = E(nc.Block())

        @block.sync
        def _(sync):
            sync.dma_start(out=sident[:], in_=ident[:, :]).then_inc(S.h["ld"], 16)
            for i in range(NT):
                if i >= 2:
                    S.waitf(sync, "v", 6 * (i - 2) + 1)
                sync.dma_start(out=sx[i % 2][:], in_=x[i * 128:(i + 1) * 128, :]
                               ).then_inc(S.h["ld"], 16)
        @block.gpsimd
        def _(g):
            for i in range(NT):
                S.waitf(g, "v", 6 * i + 3)
                g.dma_start(out=xn0[i * 128:(i + 1) * 128, :], in_=sxn0[i % 2][:]
                            ).then_inc(S.h["st"], 16)
            S.waitf(g, "v", 6 * NT)
            for k in range(8):
                g.dma_start(out=xn0T[k * 128:(k + 1) * 128, :], in_=sT0[k][:]
                            ).then_inc(S.h["st"], 16)
                g.dma_start(out=xlnbT[k * 128:(k + 1) * 128, :], in_=sT1[k][:]
                            ).then_inc(S.h["st"], 16)
            g.wait_ge(S.h["st"], 16 * (NT + 16))

        @block.vector
        def _(v):
            for i in range(NT):
                xt = sx[i % 2]
                S.waitf(v, "ld", 16 * (i + 2))
                if i >= 2:
                    S.waitf(v, "st", 16 * (i - 1))
                S.chain(v, v.reduce_sum(st1[:], xt[:], axis=AX.X), "vs")
                S.chain(v, v.tensor_scalar_mul(smu[:], st1[:], 1.0 / C), "vs")
                S.inc(v.tensor_single_scalar(sxc[:], xt[:], smu[:, 0:1], OP.subtract),
                      "v")
                S.waitf(v, "a", 4 * i + 1)
                S.inc(v.tensor_scalar(svar[:], sss[:], 1.0 / C, EPS, OP.mult, OP.add),
                      "v")
                S.waitf(v, "a", 4 * i + 2)
                v.tensor_single_scalar(sxn0[i % 2][:], sxc[:], srstd[:, 0:1], OP.mult)
                v.tensor_copy(sxn0b[:], sxn0[i % 2][:])
                S.chain(v, v.reduce_sum(st1[:], sxn0[i % 2][:], axis=AX.X), "vs")
                S.chain(v, v.tensor_scalar_mul(smu2[:], st1[:], 1.0 / C), "vs")
                S.inc(v.tensor_single_scalar(sxc2[:], sxn0[i % 2][:], smu2[:, 0:1],
                                             OP.subtract), "v")
                S.waitf(v, "a", 4 * i + 3)
                S.inc(v.tensor_scalar(svar2[:], sss2[:], 1.0 / C, EPS, OP.mult, OP.add),
                      "v")
                S.waitf(v, "a", 4 * i + 4)
                S.inc(v.tensor_single_scalar(sxlnb[:], sxc2[:], srstd2[:, 0:1], OP.mult),
                      "v")
                for g in range(4):
                    S.waitf(v, "t", 4 * i + g + 1)
                    tgt = sT0 if g < 2 else sT1
                    for j in range(4):
                        k = (g % 2) * 4 + j
                        v.tensor_copy(tgt[k][:, i * 128:(i + 1) * 128],
                                      pt[g][:, j * 128:(j + 1) * 128])
                S.inc(v.engine_nop(), "v")

        @block.scalar
        def _(a):
            for i in range(NT):
                S.waitf(a, "v", 6 * i + 1)
                S.inc(a.activation(scrap[:], sxc[:], AF.Square, accum_out=sss[:]),
                      "a")
                S.waitf(a, "v", 6 * i + 2)
                S.chain(a, a.activation(slog[:], svar[:], AF.Ln), "as_")
                S.inc(a.activation(srstd[:], slog[:], AF.Exp, scale=-0.5), "a")
                S.waitf(a, "v", 6 * i + 3)
                S.inc(a.activation(scrap[:], sxc2[:], AF.Square, accum_out=sss2[:]),
                      "a")
                S.waitf(a, "v", 6 * i + 4)
                S.chain(a, a.activation(slog[:], svar2[:], AF.Ln), "as_")
                S.inc(a.activation(srstd2[:], slog[:], AF.Exp, scale=-0.5), "a")

        @block.tensor
        def _(t):
            for i in range(NT):
                S.waitf(t, "ld", 16)
                if i >= 1:
                    S.waitf(t, "v", 6 * (i - 1) + 6)
                S.waitf(t, "v", 6 * i + 5)
                for g in range(4):
                    src = sxn0b if g < 2 else sxlnb
                    ins = None
                    for j in range(4):
                        k = (g % 2) * 4 + j
                        ins = t.transpose(pt[g][:, j * 128:(j + 1) * 128],
                                          src[:, k * 128:(k + 1) * 128], sident[:])
                    S.inc(ins, "t")

    return nc


ATTN0_INPUTS = ["xnT", "qkvwT", "qn_w", "qn_b", "kn_w", "kn_b", "projwT",
                "proj_b", "identb", "identf", "onesf"]


def attn0_drams(nc, dbg=None, kind="ExternalInput", okind="ExternalOutput"):
    H, D_ = 16, 64
    NB = N
    d = {}
    if kind:
        d["xnT"] = nc.dram_tensor("xnT", [C, NB], BF16, kind=kind).ap()
    d["qkvwT"] = nc.dram_tensor("qkvwT", [C, 3 * C], F32, kind="ExternalInput").ap()
    d["qn_w"] = nc.dram_tensor("qn_w", [1, D_], F32, kind="ExternalInput").ap()
    d["qn_b"] = nc.dram_tensor("qn_b", [1, D_], F32, kind="ExternalInput").ap()
    d["kn_w"] = nc.dram_tensor("kn_w", [1, D_], F32, kind="ExternalInput").ap()
    d["kn_b"] = nc.dram_tensor("kn_b", [1, D_], F32, kind="ExternalInput").ap()
    d["projwT"] = nc.dram_tensor("projwT", [C, C], F32, kind="ExternalInput").ap()
    d["proj_b"] = nc.dram_tensor("proj_b", [1, C], F32, kind="ExternalInput").ap()
    if okind:
        d["s0"] = nc.dram_tensor("s0", [TLOC, C], F32, kind=okind).ap()
    if dbg:
        d["dbg_k"] = nc.dram_tensor("dbg_k", [4 * 64, NB], BF16, kind="ExternalOutput").ap()
        d["dbg_q"] = nc.dram_tensor("dbg_q", [4 * 64, TLOC], BF16, kind="ExternalOutput").ap()
        d["dbg_v"] = nc.dram_tensor("dbg_v", [128, NB // 128 * 4 * 65], BF16,
                                    kind="ExternalOutput").ap()
        d["dbg_m"] = nc.dram_tensor("dbg_m", [1, 4 * TLOC], BF16, kind="ExternalOutput").ap()
        d["dbg_ot"] = nc.dram_tensor("dbg_ot", [128, 2 * TLOC], BF16,
                                     kind="ExternalOutput").ap()
    return d


def emit_attn0(nc, block, S, stack, DR, dbg=None):
    H, D = 16, 64
    SCALE = D ** -0.5
    NB = N
    NJ = NB // 128
    NI = TLOC // 128
    NG, HPG = 4, 4
    xnT, qkvwT = DR["xnT"], DR["qkvwT"]
    qn_w, qn_b, kn_w, kn_b = DR["qn_w"], DR["qn_b"], DR["kn_w"], DR["kn_b"]
    projwT, proj_b = DR["projwT"], DR["proj_b"]
    identb, identf, onesf = DR["identb"], DR["identf"], DR["onesf"]
    s0 = DR["s0"]
    if dbg:
        dbg_k, dbg_q, dbg_v = DR["dbg_k"], DR["dbg_q"], DR["dbg_v"]
        dbg_m, dbg_ot = DR["dbg_m"], DR["dbg_ot"]
    if True:
        E = stack.enter_context

        sxn = [E(nc.sbuf_tensor(f"sxn{j}", [128, 1024], BF16)) for j in range(2)]
        sqkvw = E(nc.sbuf_tensor("sqkvw", [128, 8192], BF16))
        skh = E(nc.sbuf_tensor("skh", [128, 256], BF16))
        sqh = E(nc.sbuf_tensor("sqh", [128, 256], BF16))
        sxcL = E(nc.sbuf_tensor("sxcL", [128, 256], F32))
        scrapL = E(nc.sbuf_tensor("scrapL", [128, 256], F32))
        swfL = E(nc.sbuf_tensor("swfL", [128, 256], F32))
        ssum4 = E(nc.sbuf_tensor("ssum4", [128, 4], F32))
        smu4 = E(nc.sbuf_tensor("smu4", [128, 4], F32))
        sss4 = E(nc.sbuf_tensor("sss4", [128, 4], F32))
        svar4 = E(nc.sbuf_tensor("svar4", [128, 4], F32))
        slog4 = E(nc.sbuf_tensor("slog4", [128, 4], F32))
        srstd4 = E(nc.sbuf_tensor("srstd4", [128, 4], F32))
        scm = E(nc.sbuf_tensor("scm", [64, 1], F32))
        scs = E(nc.sbuf_tensor("scs", [64, 1], F32))
        swq = E(nc.sbuf_tensor("swq", [128, D], F32))
        sbq = E(nc.sbuf_tensor("sbq", [128, D], F32))
        swk = E(nc.sbuf_tensor("swk", [128, D], F32))
        sbk = E(nc.sbuf_tensor("sbk", [128, D], F32))
        sKB = [E(nc.sbuf_tensor(f"sKB{h}", [128, NB], BF16)) for h in range(HPG)]
        sQB = [E(nc.sbuf_tensor(f"sQB{h}", [128, TLOC], BF16)) for h in range(HPG)]
        sV = E(nc.sbuf_tensor("sV", [128, NJ * HPG * 65], BF16))
        srm = E(nc.sbuf_tensor("srm", [128, 16], F32))
        sm = E(nc.sbuf_tensor("sm", [128, 1], F32))
        sm0 = E(nc.sbuf_tensor("sm0", [1, HPG * TLOC], BF16))
        sE = [E(nc.sbuf_tensor(f"sE{j}", [128, 512], BF16)) for j in range(3)]
        sOF = E(nc.sbuf_tensor("sOF", [65, 512], F32))
        sden = E(nc.sbuf_tensor("sden", [65, 512], F32))
        sOnes64 = E(nc.sbuf_tensor("sOnes64", [65, 64], F32))
        sOT = E(nc.sbuf_tensor("sOT", [128, 8 * TLOC], BF16))
        sbrep = E(nc.sbuf_tensor("sbrep", [128, C], F32))
        ss0 = [E(nc.sbuf_tensor(f"ss0_{j}", [128, C], F32)) for j in range(2)]
        sidb = E(nc.sbuf_tensor("sidb", [128, 128], BF16))
        sidf = E(nc.sbuf_tensor("sidf", [128, 128], F32))
        sonesf = E(nc.sbuf_tensor("sonesf", [1, 128], F32))
        sraw = E(nc.sbuf_tensor("sraw", [1, 128], F32))

        pbank = [E(nc.psum_tensor(f"pbank{j}", [128, 512], F32)) for j in range(3)]
        pS2 = E(nc.psum_tensor("pS2", [128, 512], F32))
        pO = E(nc.psum_tensor("pO", [65, 512], F32))
        pTb = [E(nc.psum_tensor(f"pTb{j}", [128, 512], BF16)) for j in range(2)]
        pRep = E(nc.psum_tensor("pRep", [128, 512], F32))
        pBr = pRep

        S.new("ld", "ld0", "ld1", "lds", "gld", "st", "tmm", "ttr", "tS", "tPV", "tRep",
              "vst", "ast", "vfree", "vkq", "vtb", "vmA", "vOF", "vdma", "vrep",
              "vs", "as_", "aexp")

        @block.sync
        def _(sync):
            S.inc(sync.dma_start(out=sidb[:], in_=identb[:, :]), "lds", 16)
            S.inc(sync.dma_start(out=sidf[:], in_=identf[:, :]), "lds", 16)
            S.inc(sync.dma_start(out=sonesf[:], in_=onesf[:, :]), "lds", 16)

        @block.vector
        def _(v):
            v.memset(sOnes64[64:65, :], 1.0)
            v.memset(sm0[:], 0.0)
            v.memset(sV[:].rearrange("p (j c) -> p j c", c=65)[:, :, 64:65], 1.0)
            for hh in range(HPG):
                S.chain(v, v.memset(sKB[hh][64:128, :], 0.0), "vs")
                v.memset(sQB[hh][64:128, :], 0.0)
                S.inc(v.memset(sKB[hh][64:65, :], 1.0), "vrep")
        reps = [(qn_w, swq, SCALE), (qn_b, sbq, SCALE),
                (kn_w, swk, 1.0), (kn_b, sbk, 1.0)]
        for idx, (src, dst, scale) in enumerate(reps):
            @block.sync
            def _(sync, src=src, idx=idx):
                if idx > 0:
                    S.wait(sync, "tRep", idx)
                S.inc(sync.dma_start(out=sraw[0:1, 0:D], in_=src[0:1, :]
                               ), "lds", 16)

            @block.tensor
            def _(t, idx=idx):
                S.wait(t, "lds", S.n["lds"])
                if idx > 0:
                    S.wait(t, "vrep", S.n["vrep"])
                S.inc(t.matmul(pBr[:, 0:D], sonesf[:, :], sraw[0:1, 0:D],
                         start=True, stop=True), "tRep", 1)

            @block.vector
            def _(v, dst=dst, scale=scale):
                S.wait(v, "tRep", S.n["tRep"])
                S.inc(v.tensor_scalar_mul(dst[:], pBr[:, 0:D], scale), "vrep")

        for kk in range(8):
            @block.sync
            def _(sync, kk=kk):
                S.wait(sync, "tRep", S.n["tRep"])
                S.inc(sync.dma_start(out=sraw[0:1, :],
                               in_=proj_b[0:1, kk * 128:(kk + 1) * 128]
                               ), "lds", 16)

            @block.tensor
            def _(t, kk=kk):
                S.wait(t, "lds", S.n["lds"])
                S.wait(t, "vrep", S.n["vrep"])
                S.inc(t.matmul(pBr[:, 0:128], sonesf[:, :], sraw[0:1, :],
                         start=True, stop=True), "tRep", 1)

            @block.vector
            def _(v, kk=kk):
                S.wait(v, "tRep", S.n["tRep"])
                S.inc(v.tensor_copy(sbrep[:, kk * 128:(kk + 1) * 128],
                                    pBr[:, 0:128]), "vrep")

        def segln_pre(v, psum_ap):
            a3 = psum_ap.rearrange("p (s d) -> p s d", s=4)
            S.chain(v, v.reduce_sum(ssum4[:], a3, axis=AX.X), "vs")
            S.chain(v, v.tensor_scalar_mul(smu4[:], ssum4[:], 1.0 / D), "vs")
            mu_bc = bass.AP(smu4.ap().tensor, 0, [[4, 128], [1, 4], [0, D]])
            S.chain(v, v.tensor_tensor(sxcL[:].rearrange("p (s d) -> p s d", s=4),
                                       a3, mu_bc, OP.subtract), "vs")
            psum_free = ("vs", S.n["vs"])
            S.chain(v, v.tensor_mul(scrapL[:], sxcL[:], sxcL[:]), "vs")
            S.chain(v, v.reduce_sum(
                sss4[:], scrapL[:].rearrange("p (s d) -> p s d", s=4),
                axis=AX.X), "vs")
            S.inc(v.tensor_scalar(svar4[:], sss4[:], 1.0 / D, EPS,
                                  OP.mult, OP.add), "vst")
            return psum_free

        def segln_act(a):
            S.wait(a, "vst", S.n["vst"])
            S.chain(a, a.activation(slog4[:], svar4[:], AF.Ln), "as_")
            S.inc(a.activation(srstd4[:], slog4[:], AF.Exp, scale=-0.5), "ast")

        def segln_post(v, wrep, brep, out_bf):
            S.wait(v, "ast", S.n["ast"])
            rstd_bc = bass.AP(srstd4.ap().tensor, 0, [[4, 128], [1, 4], [0, D]])
            w_bc = bass.AP(wrep.ap().tensor, 0, [[D, 128], [0, 4], [1, D]])
            S.chain(v, v.tensor_tensor(swfL[:].rearrange("p (s d) -> p s d", s=4),
                                       rstd_bc, w_bc, OP.mult), "vs")
            S.chain(v, v.tensor_mul(scrapL[:], sxcL[:], swfL[:]), "vs")
            b_bc = bass.AP(brep.ap().tensor, 0, [[D, 128], [0, 4], [1, D]])
            return v.tensor_tensor(out_bf.rearrange("p (s d) -> p s d", s=4),
                                   scrapL[:].rearrange("p (s d) -> p s d", s=4),
                                   b_bc, OP.add)

        NG_run = 1 if dbg else NG
        for g in range(NG_run):
            tmm_base = S.n["tmm"]
            tpv_base_g = S.n["tPV"]

            @block.gpsimd
            def _(gp, g=g):
                S.wait(gp, "tmm", tmm_base)
                S.wait(gp, "gld", S.n["gld"])
                for part in range(3):
                    cols = qkvwT[:, part * C + g * 256: part * C + g * 256 + 256]
                    S.inc(gp.dma_start(
                        out=sqkvw[:, 0:6144].rearrange("p (k r) -> p k r", k=8
                                                       )[:, :, part * 256:(part + 1) * 256],
                        in_=cols.rearrange("(k p) r -> p k r", p=128)
                    ), "gld", 16)
            ld_w_done = S.n["gld"]

            tmm_all, tmm_kv = {}, {}
            vfree_kv, vfree_q = {}, {}
            vkq_t, vtb_t, ttr_t = {}, {}, {}
            ld_t = {}

            for j in range(NJ + 1):
                local = j < NI
                if j < NJ:
                    @block.sync
                    def _(sync, j=j, g=g):
                        if j >= 2:
                            S.wait(sync, "tmm", tmm_all[j - 2])
                        elif g > 0:
                            S.wait(sync, "tmm", tmm_base)
                        S.inc(sync.dma_start(
                            out=sxn[j % 2][:].rearrange("p (k n) -> p k n", k=8),
                            in_=xnT[:, j * 128:(j + 1) * 128].rearrange(
                                "(k p) n -> p k n", p=128)
                        ), "ld" + str(j % 2), 16)
                        ld_t[j] = S.n["ld" + str(j % 2)]

                    @block.tensor
                    def _(t, j=j, g=g, local=local):
                        S.wait(t, "ld" + str(j % 2), ld_t[j])
                        S.wait(t, "gld", ld_w_done)
                        if j >= 2:
                            S.wait(t, "vfree", vfree_kv[j - 2])
                        elif g > 0:
                            S.wait(t, "aexp", aexp_base_next)
                        if local and j >= 1:
                            S.wait(t, vfree_q[j - 1][0], vfree_q[j - 1][1])
                        last = None
                        for k in range(8):
                            lhs = sxn[j % 2][:, k * 128:(k + 1) * 128]
                            last = t.matmul(pbank[j % 2][:], lhs,
                                            sqkvw[:, k * 768 + 256:k * 768 + 768],
                                            start=(k == 0), stop=(k == 7))
                            if local:
                                last = t.matmul(pbank[2][:, 0:256], lhs,
                                                sqkvw[:, k * 768:k * 768 + 256],
                                                start=(k == 0), stop=(k == 7))
                        S.inc(last, "tmm")
                        tmm_kv[j] = tmm_all[j] = S.n["tmm"]

                if j >= 1:
                    @block.tensor
                    def _(t, j=j):
                        jj = j - 1
                        S.wait(t, "vkq", vkq_t[jj])
                        if jj >= 2:
                            S.wait(t, "vtb", vtb_t[jj - 2])
                        pT = pTb[jj % 2]
                        last = None
                        for hh in range(HPG):
                            last = t.transpose(pT[0:64, hh * 128:(hh + 1) * 128],
                                               skh[:, hh * 64:(hh + 1) * 64], sidb[:])
                        if jj < NI:
                            for hh in range(HPG):
                                last = t.transpose(
                                    pT[64:128, hh * 128:(hh + 1) * 128],
                                    sqh[:, hh * 64:(hh + 1) * 64], sidb[:],
                                    tile_position=(0, 64))
                        S.inc(last, "ttr")
                        ttr_t[jj] = S.n["ttr"]

                if j < NJ:
                    @block.vector
                    def _(v, j=j, g=g, local=local):
                        S.wait(v, "tmm", tmm_kv[j])
                        if j == 0 and g > 0:
                            S.wait(v, "tPV", tpv_base_g)
                        if j >= 1:
                            S.wait(v, "ttr", ttr_t[j - 1])
                        segln_pre(v, pbank[j % 2][:, 0:256])
                        last = None
                        for hh in range(HPG):
                            last = v.tensor_copy(
                                sV[:, (j * HPG + hh) * 65:(j * HPG + hh) * 65 + 64],
                                pbank[j % 2][:, 256 + hh * 64:256 + (hh + 1) * 64])
                        S.inc(last, "vfree")
                        vfree_kv[j] = S.n["vfree"]

                    @block.scalar
                    def _(a, j=j):
                        segln_act(a)

                    @block.vector
                    def _(v, j=j):
                        S.inc(segln_post(v, swk, sbk, skh[:]), "vkq")

                    if local:
                        @block.vector
                        def _(v, j=j):
                            vfree_q[j] = segln_pre(v, pbank[2][:, 0:256])

                        @block.scalar
                        def _(a, j=j):
                            segln_act(a)

                        @block.vector
                        def _(v, j=j):
                            S.inc(segln_post(v, swq, sbq, sqh[:]), "vkq")
                    vkq_t[j] = S.n["vkq"]

                if j >= 1:
                    @block.vector
                    def _(v, j=j):
                        jj = j - 1
                        S.wait(v, "ttr", ttr_t[jj])
                        pT = pTb[jj % 2]
                        for hh in range(HPG):
                            last = v.tensor_copy(sKB[hh][0:64, jj * 128:(jj + 1) * 128],
                                                 pT[0:64, hh * 128:(hh + 1) * 128])
                        if jj < NI:
                            for hh in range(HPG):
                                last = v.tensor_copy(
                                    sQB[hh][0:64, jj * 128:(jj + 1) * 128],
                                    pT[64:128, hh * 128:(hh + 1) * 128])
                        S.inc(last, "vtb")
                        vtb_t[jj] = S.n["vtb"]

            @block.vector
            def _(v, g=g):
                S.wait(v, "vtb", S.n["vtb"])
                for hh in range(HPG):
                    S.chain(v, v.reduce_sum(scm[:], sKB[hh][0:64, :], axis=AX.X),
                            "vs")
                    S.chain(v, v.tensor_scalar_mul(scs[:], scm[:], -1.0 / NB), "vs")
                    S.inc(v.tensor_single_scalar(sKB[hh][0:64, :], sKB[hh][0:64, :],
                                                 scs[:, 0:1], OP.add), "vkq")
            vcenter_done = S.n["vkq"]

            if dbg == "p0":
                @block.gpsimd
                def _(gp):
                    S.wait(gp, "vkq", S.n["vkq"])
                    S.wait(gp, "vtb", S.n["vtb"])
                    for hh in range(HPG):
                        S.inc(gp.dma_start(out=dbg_k[hh * 64:(hh + 1) * 64, :],
                                           in_=sKB[hh][0:64, :]), "st", 16)
                        S.inc(gp.dma_start(out=dbg_q[hh * 64:(hh + 1) * 64, :],
                                           in_=sQB[hh][0:64, :]), "st", 16)
                    S.inc(gp.dma_start(out=dbg_v[:, :], in_=sV[:]), "st", 16)
                    gp.wait_ge(S.h["st"], S.n["st"])
                S.check_futures()
                return

            vfA_base = S.n["vfree"]
            vdma_base_g = S.n["vdma"]
            tSa = {}
            tSb = {}
            tRepA = {}
            pSB = [pbank[2], pS2]

            def emit_quarter(hh, ph, g):
                haveA = False
                haveB = ph >= 1 or hh > 0
                if haveB:
                    bhh, big = (hh, ph - 1) if ph >= 1 else (hh - 1, 3)
                    combo = bhh * 4 + big
                    pv_base = S.n["tPV"]
                    vOF_base = S.n["vOF"]
                    ax_base = S.n["aexp"]
                    vdma_q = S.n["vdma"]
                vmA_q = S.n["vmA"]

                @block.tensor
                def _(t, hh=hh, ph=ph, g=g):
                    if haveB:
                        S.wait(t, "vdma", vdma_q)
                        if combo > 0 or g > 0:
                            S.wait(t, "vOF", vOF_base)
                        if combo == 0:
                            S.wait(t, "vfree", vfA_base)
                            S.wait(t, "vkq", vcenter_done)
                    if haveA and hh == 0 and ph == 0:
                        S.wait(t, "vkq", vcenter_done)
                    for k in range(64):
                        if haveA:
                            it = ph * 4 + k // 16
                            jg = k % 16
                            idx = (hh * NI + it) * 16 + jg
                            if idx >= 2:
                                S.waitf(t, "vfree", vfA_base + idx - 1)
                            else:
                                S.wait(t, "vfree", vfA_base)
                            S.inc(t.matmul(
                                pbank[jg % 2][:],
                                sQB[hh][0:128, it * 128:(it + 1) * 128],
                                sKB[hh][0:128, jg * 512:(jg + 1) * 512],
                                start=True, stop=True), "tS")
                            tSa[idx] = S.n["tS"]
                            if k % 16 == 15:
                                S.waitf(t, "vmA", vmA_q + (k // 16) * 2 + 1)
                                S.wait(t, "vOF", S.n["vOF"])
                                S.inc(t.transpose(pRep[0:1, 0:128], sm[:],
                                                  sidf[:]), "tRep")
                                tRepA[(hh, it)] = S.n["tRep"]
                        if haveB:
                            j = k
                            S.waitf(t, "aexp", ax_base + max(j - 1, 0))
                            S.inc(t.matmul(
                                pSB[j % 2][:],
                                sKB[bhh][0:128, j * 128:(j + 1) * 128],
                                sQB[bhh][0:128, big * 512:(big + 1) * 512],
                                start=True, stop=True), "tS")
                            tSb[(combo, j)] = S.n["tS"]
                            if j >= 2:
                                S.inc(t.matmul(
                                    pO[:],
                                    sV[:, ((j - 2) * HPG + bhh) * 65:
                                       ((j - 2) * HPG + bhh) * 65 + 65],
                                    sE[(j - 2) % 3][:],
                                    start=(j == 2), stop=False), "tPV", 1)
                    if haveB:
                        for jt in (NJ - 2, NJ - 1):
                            S.waitf(t, "aexp", ax_base + jt + 1)
                            S.inc(t.matmul(
                                pO[:],
                                sV[:, (jt * HPG + bhh) * 65:
                                   (jt * HPG + bhh) * 65 + 65],
                                sE[jt % 3][:],
                                start=False, stop=(jt == NJ - 1)), "tPV", 1)

                if haveA:
                    @block.vector
                    def _(v, hh=hh, ph=ph):
                        if hh == 0 and ph == 0:
                            S.wait(v, "vdma", vdma_base_g)
                        for tt in range(4):
                            it = ph * 4 + tt
                            for jg in range(16):
                                idx = (hh * NI + it) * 16 + jg
                                S.wait(v, "tS", tSa[idx])
                                S.inc(v.reduce_max(srm[:, jg:jg + 1],
                                                   pbank[jg % 2][:],
                                                   axis=AX.X), "vfree")
                            S.wait(v, "vfree", S.n["vfree"])
                            S.chain(v, v.reduce_max(sm[:], srm[:], axis=AX.X),
                                    "vs")
                            S.inc(v.engine_nop(), "vmA")
                            S.wait(v, "tRep", tRepA[(hh, it)])
                            sm0sl = sm0[0:1, hh * TLOC + it * 128:
                                        hh * TLOC + (it + 1) * 128]
                            S.inc(v.tensor_tensor(
                                sm0sl, sm0sl, pRep[0:1, 0:128],
                                OP.subtract), "vmA")

                    @block.gpsimd
                    def _(gp, hh=hh, ph=ph):
                        S.wait(gp, "vmA", S.n["vmA"])
                        S.wait(gp, "vdma", S.n["vdma"])
                        S.inc(gp.dma_start(
                            out=sQB[hh][64:65, ph * 512:(ph + 1) * 512],
                            in_=sm0[0:1, hh * TLOC + ph * 512:
                                    hh * TLOC + (ph + 1) * 512]), "vdma", 16)

                if haveB:
                    @block.scalar
                    def _(a, pv_base=pv_base, combo=combo):
                        for j in range(NJ):
                            S.wait(a, "tS", tSb[(combo, j)])
                            if j >= 3:
                                S.wait(a, "tPV", pv_base + j - 2)
                            else:
                                S.wait(a, "tPV", pv_base)
                            S.inc(a.activation(sE[j % 3][:], pSB[j % 2][:],
                                               AF.Exp), "aexp")

                    @block.vector
                    def _(v):
                        S.wait(v, "tPV", S.n["tPV"])
                        S.wait(v, "tRep", S.n["tRep"])
                        S.inc(v.tensor_copy(sOF[:], pO[:]), "vOF")

                    @block.scalar
                    def _(a):
                        S.wait(a, "vOF", S.n["vOF"])
                        S.chain(a, a.activation(sden[64:65, :], sOF[64:65, :],
                                                AF.Ln), "as_")
                        S.inc(a.activation(sOF[64:65, :], sden[64:65, :],
                                           AF.Exp, scale=-1.0), "ast")

                    @block.tensor
                    def _(t):
                        S.wait(t, "ast", S.n["ast"])
                        S.wait(t, "vmA", S.n["vmA"])
                        S.inc(t.matmul(pRep[0:64, :], sOnes64[64:65, :],
                                       sOF[64:65, :], start=True, stop=True),
                              "tRep")

                    @block.vector
                    def _(v, combo=combo, g=g):
                        S.wait(v, "tRep", S.n["tRep"])
                        h_abs = g * HPG + combo // 4
                        ig2 = combo % 4
                        p, r0 = h_abs // 2, (h_abs % 2) * 64
                        S.inc(v.tensor_tensor(
                            sOT[r0:r0 + 64,
                                p * TLOC + ig2 * 512:p * TLOC + (ig2 + 1) * 512],
                            sOF[0:64, :], pRep[0:64, :], OP.mult), "vOF")

            for hh in range(HPG):
                for ph in range(5):
                    if ph == 4 and hh < HPG - 1:
                        break
                    if hh == 0 and ph == 0:
                        continue
                    emit_quarter(hh, ph, g)
            aexp_base_next = S.n["aexp"]
            if dbg == "passB":
                @block.gpsimd
                def _(gp):
                    S.wait(gp, "vOF", S.n["vOF"])
                    S.inc(gp.dma_start(out=dbg_ot[:, :], in_=sOT[:, 0:2 * TLOC]),
                          "st", 16)
                    gp.wait_ge(S.h["st"], S.n["st"])
                S.check_futures()
                return

        @block.gpsimd
        def _(gp):
            S.wait(gp, "tS", S.n["tS"])
            S.wait(gp, "gld", S.n["gld"])
            S.inc(gp.dma_start(
                out=sqkvw[:].rearrange("p (k r) -> p k r", k=8),
                in_=projwT[:, :].rearrange("(k p) r -> p k r", p=128)), "gld", 16)
        ld_pw = S.n["gld"]

        vproj = {}
        for it in range(NI):
            @block.tensor
            def _(t, it=it):
                S.wait(t, "gld", ld_pw)
                S.wait(t, "vOF", S.n["vOF"])
                S.wait(t, "vrep", S.n["vrep"])
                if it >= 1:
                    S.wait(t, "vfree", vproj[it - 1])
                for half in range(2):
                    last = None
                    for p in range(8):
                        last = t.matmul(
                            pbank[half][:],
                            sOT[:, p * TLOC + it * 128:p * TLOC + (it + 1) * 128],
                            sqkvw[:, p * 1024 + half * 512:
                                  p * 1024 + half * 512 + 512],
                            start=(p == 0), stop=(p == 7))
                    S.inc(last, "tS")

            tbase = S.n["tS"]

            @block.vector
            def _(v, it=it, tbase=tbase):
                if it >= 2:
                    S.wait(v, "st", 16 * (it - 1))
                for half in range(2):
                    S.wait(v, "tS", tbase - 1 + half)
                    v.tensor_tensor(ss0[it % 2][:, half * 512:(half + 1) * 512],
                                    pbank[half][:],
                                    sbrep[:, half * 512:(half + 1) * 512], OP.add)
                S.inc(v.engine_nop(), "vfree")
                vproj[it] = S.n["vfree"]

            @block.gpsimd
            def _(gp, it=it):
                S.wait(gp, "vfree", vproj[it])
                S.wait(gp, "st", S.n["st"])
                S.inc(gp.dma_start(out=s0[it * 128:(it + 1) * 128, :], in_=ss0[it % 2][:]
                             ), "st", 16)

        @block.gpsimd
        def _(gp):
            gp.wait_ge(S.h["st"], 16 * NI)

        S.check_futures()




def build_attn0(dbg=None):
    nc = bass.Bass("TRN2", target_bir_lowering=False)
    with contextlib.ExitStack() as stack:
        D = attn0_drams(nc, dbg)
        D["identb"] = nc.dram_tensor("identb", [128, 128], BF16,
                                     kind="ExternalInput").ap()
        D["identf"] = nc.dram_tensor("identf", [128, 128], F32,
                                     kind="ExternalInput").ap()
        D["onesf"] = nc.dram_tensor("onesf", [1, 128], F32,
                                    kind="ExternalInput").ap()
        S = Sems(nc, stack)
        block = stack.enter_context(nc.Block())
        emit_attn0(nc, block, S, stack, D, dbg)
    return nc

def run_attn0(xnT_by_core, inp, dbg=None):
    nc = build_attn0(dbg)
    qkvwT = np.ascontiguousarray(inp["L0_attn_qkv_w"].T)
    projwT = np.ascontiguousarray(inp["L0_attn_proj_w"].T)
    base = {
        "qkvwT": qkvwT, "projwT": projwT,
        "qn_w": inp["L0_attn_q_norm_w"].reshape(1, -1),
        "qn_b": inp["L0_attn_q_norm_b"].reshape(1, -1),
        "kn_w": inp["L0_attn_k_norm_w"].reshape(1, -1),
        "kn_b": inp["L0_attn_k_norm_b"].reshape(1, -1),
        "proj_b": inp["L0_attn_proj_b"].reshape(1, -1),
        "identb": np.eye(128, dtype=bf16),
        "identf": np.eye(128, dtype=np.float32),
        "onesf": np.ones((1, 128), dtype=np.float32),
    }
    in_maps = [dict(base, xnT=xnT_by_core[c]) for c in range(NCORES)]
    res = run_bass_kernel_spmd(nc, in_maps, core_ids=list(range(NCORES)))
    return res.results


def moe0_drams(nc, kind="ExternalInput", okind="ExternalOutput"):
    F, NE = 4096, 8
    d = {}
    if kind:
        d["xlnbT"] = nc.dram_tensor("xlnbT", [C, TLOC], BF16, kind=kind).ap()
        d["xn0"] = nc.dram_tensor("xn0", [TLOC, C], F32, kind=kind).ap()
    d["cen"] = nc.dram_tensor("cen", [NE, C], F32, kind="ExternalInput").ap()
    d["lngR"] = nc.dram_tensor("lngR", [128, NE * 8], F32, kind="ExternalInput").ap()
    d["lnbR"] = nc.dram_tensor("lnbR", [128, NE * 8], F32, kind="ExternalInput").ap()
    d["w1"] = nc.dram_tensor("w1", [NE, C, F], F32, kind="ExternalInput").ap()
    d["b1R"] = nc.dram_tensor("b1R", [128, NE * 32], F32, kind="ExternalInput").ap()
    d["w2"] = nc.dram_tensor("w2", [NE, F, C], F32, kind="ExternalInput").ap()
    d["b2"] = nc.dram_tensor("b2", [NE, C], F32, kind="ExternalInput").ap()
    if okind:
        d["g0"] = nc.dram_tensor("g0", [TLOC, C], F32, kind=okind).ap()
    return d


def emit_moe0(nc, block, S, stack, DR):
    F = 4096
    NE = 8
    xlnbT, xn0 = DR["xlnbT"], DR["xn0"]
    cen, lngR, lnbR = DR["cen"], DR["lngR"], DR["lnbR"]
    w1, b1R, w2, b2 = DR["w1"], DR["b1R"], DR["w2"], DR["b2"]
    identf, onesf, g0 = DR["identf"], DR["onesf"], DR["g0"]
    NI_ = TLOC // 128
    if True:
        E = stack.enter_context

        sW1 = E(nc.sbuf_tensor("sW1", [128, 8 * F], BF16))
        sW2b = [E(nc.sbuf_tensor(f"sW2b{j}", [128, 8 * C], BF16)) for j in range(2)]
        sHT = E(nc.sbuf_tensor("sHT", [128, 32 * 512], BF16))
        sFA = E(nc.sbuf_tensor("sFA", [128, NI_ * C], BF16))
        sXE = E(nc.sbuf_tensor("sXE", [128, 8 * 512], BF16))
        sXBf = [E(nc.sbuf_tensor("sXBf0", [128, 8 * 512], BF16))] * 2
        scen1 = E(nc.sbuf_tensor("scen1", [1, C], F32))
        sonesf = E(nc.sbuf_tensor("sonesf", [1, 128], F32))
        sLNG = E(nc.sbuf_tensor("sLNG", [128, 64], F32))
        sLNB = E(nc.sbuf_tensor("sLNB", [128, 64], F32))
        sB1 = E(nc.sbuf_tensor("sB1", [128, 256], F32))
        sB2 = E(nc.sbuf_tensor("sB2", [8, C], BF16))
        sOH = E(nc.sbuf_tensor("sOH", [128, NI_ * 8], F32))
        sOHT = E(nc.sbuf_tensor("sOHT", [8, TLOC], BF16))
        sAL = E(nc.sbuf_tensor("sAL", [128, NI_], F32))
        sAmax = E(nc.sbuf_tensor("sAmax", [128, 1], F32))
        sIdf = E(nc.sbuf_tensor("sIdf", [128, 128], F32))
        sX0 = [E(nc.sbuf_tensor(f"sX0{j}", [128, C], F32)) for j in range(2)]
        sG = [E(nc.sbuf_tensor(f"sG{j}", [128, C], F32)) for j in range(2)]
        sCR = sG[0]
        sscr = sG[1]
        pB = [E(nc.psum_tensor(f"pB{j}", [128, 512], F32)) for j in range(8)]

        S.new("lds", "gld", "gld2", "gw1", "st", "ta", "th", "ty", "tb",
              "va", "vfa", "vy", "vg", "aal", "ah", "ax", "vs", "as_")
        ah_t, xb_g, vh_t, w2l_t, ty_t = {}, {}, {}, {}, {}
        bfa_t, vg_t, ag_t = {}, {}, {}

        @block.sync
        def _(sync):
            S.inc(sync.dma_start(out=sIdf[:], in_=identf[:, :]), "lds", 16)
            S.inc(sync.dma_start(out=sLNG[:], in_=lngR[:, :]), "lds", 16)
            S.inc(sync.dma_start(out=sLNB[:], in_=lnbR[:, :]), "lds", 16)
            S.inc(sync.dma_start(out=sB1[:], in_=b1R[:, :]), "lds", 16)

        @block.gpsimd
        def _(gp):
            S.inc(gp.dma_start(out=sB2[:], in_=b2[:, :]), "gld", 16)

        @block.sync
        def _(sync):
            S.inc(sync.dma_start(out=sonesf[:], in_=onesf[:, :]), "lds", 16)
        gld_m0 = S.n["lds"]

        scenT = E(nc.sbuf_tensor("scenT", [128, 64], F32))
        sXT = [E(nc.sbuf_tensor(f"sXT{j}", [128, 128], F32)) for j in range(2)]

        @block.sync
        def _(sync):
            S.wait(sync, "lds", S.n["lds"])
            for e_ in range(NE):
                S.inc(sync.dma_start(
                    out=bass.AP(scenT.ap().tensor, e_, [[64, 128], [8, 8], [1, 1]]),
                    in_=bass.AP(cen.tensor, e_ * 1024, [[1, 128], [128, 8], [1, 1]]),
                    allow_slow_non_contiguous=True), "lds", 16)
        lds_cen = S.n["lds"]

        tr_t = {}
        ev_t = {}
        mm_t = {}
        for it in range(NI_):
            @block.sync
            def _(sync, it=it):
                S.wait(sync, "lds", S.n["lds"])
                if it >= 2:
                    S.wait(sync, "ta", tr_t[(it - 2) * 8 + 7])
                S.inc(sync.dma_start(out=sX0[it % 2][:],
                                     in_=xn0[it * 128:(it + 1) * 128, :]),
                      "lds", 16)
            lds_x = S.n["lds"]

            for kc in range(8):
                idx = it * 8 + kc

                @block.tensor
                def _(t, it=it, kc=kc, idx=idx, lds_x=lds_x):
                    if kc == 0:
                        S.wait(t, "lds", lds_x)
                    if idx >= 2:
                        S.wait(t, "va", ev_t[idx - 2])
                    S.inc(t.transpose(pB[idx % 2][:, 0:128],
                                      sX0[it % 2][:, kc * 128:(kc + 1) * 128],
                                      sIdf[:]), "ta")
                    tr_t[idx] = S.n["ta"]
                    if idx >= 1:
                        pidx = idx - 1
                        pit, pkc = divmod(pidx, 8)
                        S.wait(t, "va", ev_t[pidx])
                        S.inc(t.matmul(pB[2][:, pit * 8:(pit + 1) * 8],
                                       sXT[pidx % 2][:],
                                       scenT[:, pkc * 8:(pkc + 1) * 8],
                                       start=(pkc == 0), stop=(pkc == 7)),
                              "tb")
                        mm_t[pidx] = S.n["tb"]

                @block.vector
                def _(v, idx=idx):
                    S.wait(v, "ta", tr_t[idx])
                    if idx >= 2:
                        S.wait(v, "tb", mm_t[idx - 2])
                    S.inc(v.tensor_copy(sXT[idx % 2][:], pB[idx % 2][:, 0:128]),
                          "va")
                    ev_t[idx] = S.n["va"]

        last_idx = NI_ * 8 - 1

        @block.tensor
        def _(t):
            pit, pkc = divmod(last_idx, 8)
            S.wait(t, "va", ev_t[last_idx])
            S.inc(t.matmul(pB[2][:, pit * 8:(pit + 1) * 8],
                           sXT[last_idx % 2][:],
                           scenT[:, pkc * 8:(pkc + 1) * 8],
                           start=(pkc == 0), stop=(pkc == 7)), "tb")
            mm_t[last_idx] = S.n["tb"]

        @block.vector
        def _(v):
            S.wait(v, "tb", mm_t[last_idx])
            S.inc(v.tensor_copy(sOH[:, 0:NI_ * 8], pB[2][:, 0:NI_ * 8]), "va")

        for it in range(NI_):
            @block.vector
            def _(v, it=it):
                S.wait(v, "va", S.n["va"])
                S.wait(v, "aal", S.n["aal"])
                S.chain(v, v.reduce_max(sAmax[:], sOH[:, it * 8:(it + 1) * 8],
                                        axis=AX.X), "vs")
                S.inc(v.tensor_single_scalar(sOH[:, it * 8:(it + 1) * 8],
                                             sOH[:, it * 8:(it + 1) * 8],
                                             sAmax[:, 0:1], OP.is_ge), "va")

            @block.scalar
            def _(a, it=it):
                S.wait(a, "va", S.n["va"])
                S.inc(a.activation(sAL[:, it:it + 1], sAmax[:, 0:1], AF.Sigmoid),
                      "aal")

            @block.tensor
            def _(t, it=it):
                S.wait(t, "va", S.n["va"])
                S.inc(t.transpose(pB[3][0:8, 0:128], sOH[:, it * 8:(it + 1) * 8],
                                  sIdf[:]), "tb")

            @block.vector
            def _(v, it=it):
                S.wait(v, "tb", S.n["tb"])
                S.inc(v.tensor_copy(sOHT[0:8, it * 128:(it + 1) * 128],
                                    pB[3][0:8, 0:128]), "va")

        m0_done_va = S.n["va"]
        m0_done_ta = S.n["ta"]

        for it in range(NI_):
            @block.tensor
            def _(t, it=it):
                S.wait(t, "gld", 16)
                S.wait(t, "va", m0_done_va)
                if it >= 2:
                    S.wait(t, "vfa", bfa_t[it - 2])
                last = None
                for half in range(2):
                    last = t.matmul(pB[4 + (it * 2 + half) % 4][:],
                                    sOHT[0:8, it * 128:(it + 1) * 128],
                                    sB2[0:8, half * 512:(half + 1) * 512],
                                    start=True, stop=True)
                S.inc(last, "tb")

            @block.vector
            def _(v, it=it):
                S.wait(v, "tb", S.n["tb"])
                last = None
                for half in range(2):
                    last = v.tensor_copy(
                        sFA[:, it * C + half * 512:it * C + (half + 1) * 512],
                        pB[4 + (it * 2 + half) % 4][:])
                S.inc(last, "vfa")
                bfa_t[it] = S.n["vfa"]

        for e in range(NE):
            @block.gpsimd
            def _(gp, e=e):
                if e > 0:
                    S.wait(gp, "th", S.n["th"])
                for fh in range(2):
                    S.inc(gp.dma_start(
                        out=sW1[:].rearrange("p (k f) -> p k f", k=8
                                             )[:, :, fh * 2048:(fh + 1) * 2048],
                        in_=w1[e][:, fh * 2048:(fh + 1) * 2048].rearrange(
                            "(k p) f -> p k f", p=128)), "gw1", 16)
            gld_w1 = S.n["gw1"]
            gld_w1h = {0: gld_w1 - 16, 1: gld_w1}

            for ig in range(4):
                gidx = e * 4 + ig

                @block.sync
                def _(sync, e=e, ig=ig, gidx=gidx):
                    S.wait(sync, "lds", S.n["lds"])
                    if gidx > 0:
                        S.wait(sync, "ah", ah_t[(gidx - 1, 7)])
                    S.inc(sync.dma_start(
                        out=sXBf[gidx % 2][:].rearrange("p (k n) -> p k n", k=8),
                        in_=xlnbT[:, ig * 512:(ig + 1) * 512].rearrange(
                            "(k p) n -> p k n", p=128)), "lds", 16)
                lds_xb = S.n["lds"]

                for kc in range(8):
                    @block.vector
                    def _(v, e=e, ig=ig, kc=kc, gidx=gidx, lds_xb=lds_xb):
                        S.wait(v, "lds", lds_xb)
                        if (e, ig) != (0, 0) and kc == 0:
                            S.wait(v, "th", S.n["th"])
                        b_bc = bass.AP(sLNB.ap().tensor, e * 8 + kc,
                                       [[64, 128], [0, 512]])
                        S.inc(v.scalar_tensor_tensor(
                            sXE[:, kc * 512:(kc + 1) * 512],
                            sXBf[gidx % 2][:, kc * 512:(kc + 1) * 512],
                            sLNG[:, e * 8 + kc:e * 8 + kc + 1], b_bc,
                            OP.mult, OP.add), "ah")
                        ah_t[(gidx, kc)] = S.n["ah"]

                for ft in range(32):
                    @block.tensor
                    def _(t, e=e, ig=ig, ft=ft):
                        if ft == 0:
                            S.wait(t, "gw1", gld_w1h[0])
                            S.wait(t, "ah", ah_t[(e * 4 + ig, 7)])
                            if (e, ig) == (0, 0):
                                S.wait(t, "va", m0_done_va)
                            else:
                                S.wait(t, "vy", S.n["vy"])
                        if ft == 16:
                            S.wait(t, "gw1", gld_w1h[1])
                        if ft >= 2:
                            S.wait(t, "ax", vh_t[(e, ig, ft - 2)])
                        last = None
                        for k in range(8):
                            last = t.matmul(
                                pB[ft % 2][:],
                                sW1[:, k * F + ft * 128:k * F + (ft + 1) * 128],
                                sXE[:, k * 512:(k + 1) * 512],
                                start=(k == 0), stop=(k == 7))
                        S.inc(last, "th")

                    @block.scalar
                    def _(a, e=e, ig=ig, ft=ft):
                        S.wait(a, "th", S.n["th"])
                        if (e, ig) == (0, 0) and ft == 0:
                            S.wait(a, "ta", m0_done_ta)
                        S.inc(a.activation(
                            sHT[:, ft * 512:(ft + 1) * 512], pB[ft % 2][:],
                            AF.Relu,
                            bias=sB1[:, e * 32 + ft:e * 32 + ft + 1]), "ax")
                        vh_t[(e, ig, ft)] = S.n["ax"]

                ty_base = S.n["ty"]

                @block.gpsimd
                def _(gp, e=e, ig=ig, ty_base=ty_base):
                    for wb in range(4):
                        S.wait(gp, "gld2", S.n["gld2"])
                        if wb == 0:
                            S.waitf(gp, "ty", max(ty_base - 8, 0))
                        elif wb == 1:
                            S.wait(gp, "ty", ty_base)
                        else:
                            S.waitf(gp, "ty", ty_base + 8 * (wb - 1))
                        S.inc(gp.dma_start(
                            out=sW2b[wb % 2][:].rearrange("p (k c) -> p k c", k=8),
                            in_=w2[e, wb * 1024:(wb + 1) * 1024, :].rearrange(
                                "(k p) c -> p k c", p=128)), "gld2", 16)
                        w2l_t[(e, ig, wb)] = S.n["gld2"]

                @block.tensor
                def _(t, e=e, ig=ig):
                    for ft in range(32):
                        wb = ft // 8
                        if ft == 0:
                            S.wait(t, "ax", vh_t[(e, ig, 31)])
                            if (e, ig) == (0, 0):
                                S.wait(t, "vfa", S.n["vfa"])
                        if ft % 8 == 0:
                            S.wait(t, "gld2", w2l_t[(e, ig, wb)])
                        last = None
                        for sub in range(8):
                            it, half = divmod(sub, 2)
                            last = t.matmul(
                                pB[sub][:],
                                sHT[:, ft * 512 + it * 128:
                                    ft * 512 + (it + 1) * 128],
                                sW2b[wb % 2][:, (ft % 8) * C + half * 512:
                                             (ft % 8) * C + (half + 1) * 512],
                                start=(ft == 0), stop=(ft == 31))
                        S.inc(last, "ty")
                        ty_t[(e, ig, ft)] = S.n["ty"]

                @block.vector
                def _(v, e=e, ig=ig):
                    S.wait(v, "ty", ty_t[(e, ig, 31)])
                    last = None
                    for sub in range(8):
                        it, half = divmod(sub, 2)
                        it_abs = ig * 4 + it
                        sl = sFA[:, it_abs * C + half * 512:
                                 it_abs * C + (half + 1) * 512]
                        last = v.scalar_tensor_tensor(
                            sl, pB[sub][:],
                            sOH[:, it_abs * 8 + e:it_abs * 8 + e + 1],
                            sl, OP.mult, OP.add)
                    S.inc(last, "vy")

        my_done = S.n["vy"]

        for it in range(NI_):
            @block.sync
            def _(sync, it=it):
                S.wait(sync, "lds", S.n["lds"])
                if it >= 2:
                    S.wait(sync, "vg", vg_t[it - 2])
                S.inc(sync.dma_start(out=sX0[it % 2][:],
                                     in_=xn0[it * 128:(it + 1) * 128, :]), "lds", 16)

            @block.vector
            def _(v, it=it):
                S.wait(v, "lds", S.n["lds"])
                S.wait(v, "vy", my_done)
                S.wait(v, "aal", S.n["aal"])
                if it >= 1:
                    S.wait(v, "ax", ag_t[it - 1])
                sTv = sXE.ap().bitcast(F32)[:, 0:C]
                S.inc(v.scalar_tensor_tensor(
                    sTv, sFA[:, it * C:(it + 1) * C], sAL[:, it:it + 1],
                    sX0[it % 2][:], OP.mult, OP.add), "vg")
                vg_t[it] = S.n["vg"]

            @block.scalar
            def _(a, it=it):
                S.wait(a, "vg", vg_t[it])
                if it >= 2:
                    S.wait(a, "st", 16 * (it - 1))
                sTv = sXE.ap().bitcast(F32)[:, 0:C]
                S.inc(a.activation(sG[it % 2][:], sTv, AF.Silu), "ax")
                ag_t[it] = S.n["ax"]

            @block.gpsimd
            def _(gp, it=it):
                S.wait(gp, "ax", ag_t[it])
                S.wait(gp, "st", S.n["st"])
                S.inc(gp.dma_start(out=g0[it * 128:(it + 1) * 128, :],
                                   in_=sG[it % 2][:]), "st", 16)

        @block.gpsimd
        def _(gp):
            gp.wait_ge(S.h["st"], S.n["st"])

        S.check_futures()




def build_moe0():
    nc = bass.Bass("TRN2", target_bir_lowering=False)
    with contextlib.ExitStack() as stack:
        D = moe0_drams(nc)
        D["identf"] = nc.dram_tensor("identf", [128, 128], F32,
                                     kind="ExternalInput").ap()
        D["onesf"] = nc.dram_tensor("onesf", [1, 128], F32,
                                    kind="ExternalInput").ap()
        S = Sems(nc, stack)
        block = stack.enter_context(nc.Block())
        emit_moe0(nc, block, S, stack, D)
    return nc

def run_moe0(inp, xn0, xlnbT_by_core):
    nc = build_moe0()
    base = {
        "cen": inp["L0_ffn_centroids"],
        "onesf": np.ones((1, 128), dtype=np.float32),
        "lngR": np.ascontiguousarray(
            inp["L0_ffn_ln_g"].reshape(8, 8, 128).transpose(2, 0, 1).reshape(128, 64)),
        "lnbR": np.ascontiguousarray(
            inp["L0_ffn_ln_b"].reshape(8, 8, 128).transpose(2, 0, 1).reshape(128, 64)),
        "w1": inp["L0_ffn_w1"],
        "b1R": np.ascontiguousarray(
            inp["L0_ffn_b1"].reshape(8, 32, 128).transpose(2, 0, 1).reshape(128, 256)),
        "w2": inp["L0_ffn_w2"],
        "b2": inp["L0_ffn_b2"],
        "identf": np.eye(128, dtype=np.float32),
    }
    xf = xn0.reshape(-1, C)
    in_maps = [dict(base, xn0=xf[c * TLOC:(c + 1) * TLOC],
                    xlnbT=xlnbT_by_core[c])
               for c in range(NCORES)]
    res = run_bass_kernel_spmd(nc, in_maps, core_ids=list(range(NCORES)))
    return res.results


def comb1_drams(nc, kind="ExternalInput", okind="ExternalOutput"):
    d = {}
    d["x"] = nc.dram_tensor("x", [TLOC, C], F32, kind=kind).ap()
    d["s0c"] = nc.dram_tensor("s0c", [TLOC, C], F32, kind=kind).ap()
    d["g0c"] = nc.dram_tensor("g0c", [TLOC, C], F32, kind=kind).ap()
    d["x1"] = nc.dram_tensor("x1", [TLOC, C], F32, kind=okind).ap()
    d["xn1T"] = nc.dram_tensor("xn1T", [C, TLOC], BF16, kind=okind).ap()
    return d


def emit_comb1(nc, block, S, stack, DR):
    x, s0, g0 = DR["x"], DR["s0c"], DR["g0c"]
    ident = DR["identb"]
    x1, xn1T = DR["x1"], DR["xn1T"]
    if True:
        E = stack.enter_context

        sx = [E(nc.sbuf_tensor(f"sx{j}", [128, C], F32)) for j in range(2)]
        ss = [E(nc.sbuf_tensor(f"ss{j}", [128, C], F32)) for j in range(2)]
        sg = [E(nc.sbuf_tensor(f"sg{j}", [128, C], F32)) for j in range(2)]
        sx1 = [E(nc.sbuf_tensor(f"sx1{j}", [128, C], F32)) for j in range(2)]
        sxc = E(nc.sbuf_tensor("sxc", [128, C], F32))
        scrap = E(nc.sbuf_tensor("scrap", [128, C], F32))
        sxnb = E(nc.sbuf_tensor("sxnb", [128, C], BF16))
        sident = E(nc.sbuf_tensor("sident", [128, 128], BF16))
        st1 = E(nc.sbuf_tensor("st1", [128, 1], F32))
        smu = E(nc.sbuf_tensor("smu", [128, 1], F32))
        sss = E(nc.sbuf_tensor("sss", [128, 1], F32))
        svar = E(nc.sbuf_tensor("svar", [128, 1], F32))
        slog = E(nc.sbuf_tensor("slog", [128, 1], F32))
        srstd = E(nc.sbuf_tensor("srstd", [128, 1], F32))
        sT0 = [E(nc.sbuf_tensor(f"sT0_{k}", [128, TLOC], BF16)) for k in range(8)]
        pt = [E(nc.psum_tensor(f"pt{j}", [128, 512], BF16)) for j in range(2)]

        S.new("ld", "ldi", "v", "a", "t", "st", "vs", "as_")
        v_t, a_t, t_t, e_t = {}, {}, {}, {}

        @block.sync
        def _(sync):
            S.inc(sync.dma_start(out=sident[:], in_=ident[:, :]), "ldi", 16)
            sync.wait_ge(S.h["ldi"], 16)

        for i in range(NT):
            @block.sync
            def _(sync, i=i):
                S.wait(sync, "ld", S.n["ld"])
                if i >= 2:
                    S.wait(sync, "v", v_t[i - 2])
                S.inc(sync.dma_start(out=sx[i % 2][:],
                                     in_=x[i * 128:(i + 1) * 128, :]), "ld", 16)
                S.inc(sync.dma_start(out=ss[i % 2][:],
                                     in_=s0[i * 128:(i + 1) * 128, :]), "ld", 16)
                S.inc(sync.dma_start(out=sg[i % 2][:],
                                     in_=g0[i * 128:(i + 1) * 128, :]), "ld", 16)
            ld_i = S.n["ld"]

            @block.vector
            def _(v, i=i, ld_i=ld_i):
                S.wait(v, "ld", ld_i)
                if i >= 2:
                    S.wait(v, "st", 16 * (i - 1))
                S.chain(v, v.tensor_mul(scrap[:], ss[i % 2][:], sg[i % 2][:]), "vs")
                S.inc(v.tensor_add(sx1[i % 2][:], sx[i % 2][:], scrap[:]), "v")
                S.wait(v, "v", S.n["v"])
                S.chain(v, v.reduce_sum(st1[:], sx1[i % 2][:], axis=AX.X), "vs")
                S.chain(v, v.tensor_scalar_mul(smu[:], st1[:], 1.0 / C), "vs")
                S.inc(v.tensor_single_scalar(sxc[:], sx1[i % 2][:], smu[:, 0:1],
                                             OP.subtract), "v")
                v_t[i] = S.n["v"]

            @block.gpsimd
            def _(gp, i=i):
                S.wait(gp, "st", S.n["st"])
                S.wait(gp, "v", v_t[i])
                S.inc(gp.dma_start(out=x1[i * 128:(i + 1) * 128, :],
                                   in_=sx1[i % 2][:]), "st", 16)

            @block.scalar
            def _(a, i=i):
                S.wait(a, "v", v_t[i])
                S.inc(a.activation(scrap[:], sxc[:], AF.Square, accum_out=sss[:]),
                      "a")

            @block.vector
            def _(v, i=i):
                S.wait(v, "a", S.n["a"])
                S.inc(v.tensor_scalar(svar[:], sss[:], 1.0 / C, EPS,
                                      OP.mult, OP.add), "v")

            @block.scalar
            def _(a, i=i):
                S.wait(a, "v", S.n["v"])
                S.chain(a, a.activation(slog[:], svar[:], AF.Ln), "as_")
                S.inc(a.activation(srstd[:], slog[:], AF.Exp, scale=-0.5), "a")
            a_t[i] = S.n["a"]

            @block.vector
            def _(v, i=i):
                S.wait(v, "a", a_t[i])
                if i >= 1:
                    S.wait(v, "t", t_t[i - 1])
                S.inc(v.tensor_single_scalar(sxnb[:], sxc[:], srstd[:, 0:1],
                                             OP.mult), "v")
                vb_i = S.n["v"]
                v_t[(i, "b")] = vb_i

            @block.tensor
            def _(t, i=i):
                S.wait(t, "ldi", 16)
                S.wait(t, "v", v_t[(i, "b")])
                if i >= 2:
                    S.wait(t, "v", e_t[i - 2])
                last = None
                for k in range(4):
                    last = t.transpose(pt[i % 2][:, k * 128:(k + 1) * 128],
                                       sxnb[:, k * 128:(k + 1) * 128], sident[:])
                S.inc(last, "t")
                t_t[(i, 0)] = S.n["t"]

            @block.vector
            def _(v, i=i):
                S.wait(v, "t", t_t[(i, 0)])
                last = None
                for k in range(4):
                    last = v.tensor_copy(sT0[k][:, i * 128:(i + 1) * 128],
                                         pt[i % 2][:, k * 128:(k + 1) * 128])
                S.inc(last, "v")
                e_t[(i, 0)] = S.n["v"]

            @block.tensor
            def _(t, i=i):
                S.wait(t, "v", e_t[(i, 0)])
                last = None
                for k in range(4, 8):
                    last = t.transpose(pt[i % 2][:, (k - 4) * 128:(k - 3) * 128],
                                       sxnb[:, k * 128:(k + 1) * 128], sident[:])
                S.inc(last, "t")
                t_t[i] = S.n["t"]

            @block.vector
            def _(v, i=i):
                S.wait(v, "t", t_t[i])
                last = None
                for k in range(4, 8):
                    last = v.tensor_copy(sT0[k][:, i * 128:(i + 1) * 128],
                                         pt[i % 2][:, (k - 4) * 128:(k - 3) * 128])
                S.inc(last, "v")
                e_t[i] = S.n["v"]

        @block.gpsimd
        def _(gp):
            S.wait(gp, "v", e_t[NT - 1])
            S.wait(gp, "st", S.n["st"])
            for k in range(8):
                S.inc(gp.dma_start(out=xn1T[k * 128:(k + 1) * 128, :],
                                   in_=sT0[k][:]), "st", 16)
            gp.wait_ge(S.h["st"], S.n["st"])

        S.check_futures()




def build_comb1():
    nc = bass.Bass("TRN2", target_bir_lowering=False)
    with contextlib.ExitStack() as stack:
        D = comb1_drams(nc)
        D["identb"] = nc.dram_tensor("identb", [128, 128], BF16,
                                     kind="ExternalInput").ap()
        S = Sems(nc, stack)
        block = stack.enter_context(nc.Block())
        emit_comb1(nc, block, S, stack, D)
    return nc

def attn1_drams(nc, kind="ExternalInput", okind="ExternalOutput", fused=False):
    H1, QEXT = 8, 3072
    d = {}
    if not fused:
        d["xn1Te"] = nc.dram_tensor("xn1Te", [C, QEXT], BF16, kind=kind).ap()
    d["WQ"] = nc.dram_tensor("WQ", [C, H1 * C], F32, kind="ExternalInput").ap()
    d["WK"] = nc.dram_tensor("WK", [C, H1 * C], F32, kind="ExternalInput").ap()
    d["WV"] = nc.dram_tensor("WV", [C, H1 * C], F32, kind="ExternalInput").ap()
    d["cosT"] = nc.dram_tensor("cosT", [64, QEXT], F32, kind="ExternalInput").ap()
    d["sinS"] = nc.dram_tensor("sinS", [64, QEXT], F32, kind="ExternalInput").ap()
    d["dec11"] = nc.dram_tensor("dec11", [128, 11 * 128], F32,
                                kind="ExternalInput").ap()
    d["gnwR"] = nc.dram_tensor("gnwR", [128, C], F32, kind="ExternalInput").ap()
    d["gnbR"] = nc.dram_tensor("gnbR", [128, C], F32, kind="ExternalInput").ap()
    if okind:
        d["s1"] = nc.dram_tensor("s1", [TLOC, C], F32, kind=okind).ap()
    d["s1pre"] = nc.dram_tensor("s1pre", [TLOC, C], F32, kind="Internal").ap()
    return d


def emit_attn1(nc, block, S, stack, DR):
    H1 = 8
    QEXT = 3072
    NSL = 12
    xn1srcs = DR["xn1srcs"]
    WQ, WK, WV = DR["WQ"], DR["WK"], DR["WV"]
    cosT, sinS, dec11 = DR["cosT"], DR["sinS"], DR["dec11"]
    gnwR, gnbR = DR["gnwR"], DR["gnbR"]
    s1, s1pre = DR["s1"], DR["s1pre"]
    if True:
        E = stack.enter_context

        sWq = E(nc.sbuf_tensor("sWq", [128, 8 * C], BF16))
        sWk = E(nc.sbuf_tensor("sWk", [128, 8 * C], BF16))
        sWv = E(nc.sbuf_tensor("sWv", [128, 8 * C], BF16))
        sKT = E(nc.sbuf_tensor("sKT", [128, 8 * TLOC], BF16))
        sQT = E(nc.sbuf_tensor("sQT", [128, NSL * 1024], BF16))
        sV = E(nc.sbuf_tensor("sV", [128, NSL * 1024], BF16))
        sCos = E(nc.sbuf_tensor("sCos", [64, QEXT], F32))
        sSin = E(nc.sbuf_tensor("sSin", [64, QEXT], F32))
        sDec = E(nc.sbuf_tensor("sDec", [128, 11 * 128], F32))
        sXn = [E(nc.sbuf_tensor(f"sXn{j}", [128, 8 * 512], BF16)) for j in range(2)]
        sAm = [E(nc.sbuf_tensor(f"sAm{j}", [128, 256], BF16)) for j in range(2)]
        sStg = [E(nc.sbuf_tensor(f"sStg{j}", [128, C], F32)) for j in range(2)]
        srp = E(nc.sbuf_tensor("srp", [64, 512], F32))
        st1_ = E(nc.sbuf_tensor("st1_", [64, 512], F32))
        st2_ = E(nc.sbuf_tensor("st2_", [64, 512], F32))
        sMu = E(nc.sbuf_tensor("sMu", [128, 16], F32))
        sSm = E(nc.sbuf_tensor("sSm", [128, 16], F32))
        sVr = E(nc.sbuf_tensor("sVr", [128, 16], F32))
        sLg = E(nc.sbuf_tensor("sLg", [128, 16], F32))
        sRs = E(nc.sbuf_tensor("sRs", [128, 16], F32))
        sGw = E(nc.sbuf_tensor("sGw", [128, C], F32))
        sGb = E(nc.sbuf_tensor("sGb", [128, C], F32))
        pPr = [E(nc.psum_tensor(f"pPr{j}", [128, 512], F32)) for j in range(2)]
        pA = [E(nc.psum_tensor(f"pA{j}", [128, 256], F32)) for j in range(2)]
        pOutb = [E(nc.psum_tensor(f"pOut{j}", [128, 512], F32)) for j in range(4)]

        S.new("lds", "gw", "st", "sta", "tp", "ta", "tav",
              "vp", "vm", "vev", "aq", "vs", "as_", "ag", "vg")

        @block.sync
        def _(sync):
            S.inc(sync.dma_start(out=sCos[:], in_=cosT[:, :]), "lds", 16)
            S.inc(sync.dma_start(out=sSin[:], in_=sinS[:, :]), "lds", 16)
            S.inc(sync.dma_start(out=sDec[:], in_=dec11[:, :]), "lds", 16)
            S.inc(sync.dma_start(out=sGw[:], in_=gnwR[:, :]), "lds", 16)
            S.inc(sync.dma_start(out=sGb[:], in_=gnbR[:, :]), "lds", 16)
            sync.wait_ge(S.h["lds"], S.n["lds"])

        gi_ctr = [0]
        xn_t = {}
        qrdy, vrdy = {}, {}
        am_t, av_t = {}, {}
        aq_hist = []

        def emit_proj_group(h, kind, g, head_first):
            gi = gi_ctr[0]
            gi_ctr[0] += 1
            buf = gi % 2

            @block.sync
            def _(sync, kind=kind, g=g, buf=buf, gi=gi):
                S.wait(sync, "lds", S.n["lds"])
                if gi >= 2:
                    S.wait(sync, "tp", xn_t[gi - 2])
                srcap, colb = xn1srcs(g)
                S.inc(sync.dma_start(
                    out=sXn[buf][:].rearrange("p (k n) -> p k n", k=8),
                    in_=srcap[:, colb:colb + 512].rearrange(
                        "(k p) n -> p k n", p=128)), "lds", 16)
            lds_g = S.n["lds"]

            if kind == "v":
                for tt in range(4):
                    qt = g * 4 + tt
                    sl = qt % NSL
                    for half in range(2):
                        u = (2 * tt + half) % 2

                        @block.tensor
                        def _(t, h=h, tt=tt, half=half, u=u, buf=buf,
                              lds_g=lds_g):
                            S.wait(t, "lds", lds_g)
                            S.wait(t, "gw", wld_t[h])
                            if len(aq_hist) >= 2:
                                S.wait(t, "aq", aq_hist[-2])
                            last = None
                            for kc in range(8):
                                last = t.matmul(
                                    pPr[u][:],
                                    sXn[buf][:, kc * 512 + tt * 128:
                                             kc * 512 + (tt + 1) * 128],
                                    sWv[:, kc * 1024 + half * 512:
                                        kc * 1024 + (half + 1) * 512],
                                    start=(kc == 0), stop=(kc == 7))
                            S.inc(last, "tp")

                        @block.scalar
                        def _(a, sl=sl, half=half, u=u, head_first=head_first):
                            S.wait(a, "tp", S.n["tp"])
                            if head_first[0]:
                                S.wait(a, "ta", S.n["ta"])
                                S.wait(a, "tav", S.n["tav"])
                                head_first[0] = False
                            S.inc(a.activation(
                                sV[:, sl * 1024 + half * 512:
                                    sl * 1024 + (half + 1) * 512],
                                pPr[u][:], AF.Silu), "aq")
                            aq_hist.append(S.n["aq"])
                    vrdy[qt] = S.n["aq"]
                xn_t[gi] = S.n["tp"]
                return

            wsel = sWk if kind == "k" else sWq
            sl0 = (g * 4) % NSL
            for ct in range(8):
                @block.tensor
                def _(t, h=h, ct=ct, buf=buf, lds_g=lds_g):
                    S.wait(t, "lds", lds_g)
                    S.wait(t, "gw", wld_t[h])
                    if len(aq_hist) >= 2:
                        S.wait(t, "aq", aq_hist[-2])
                    last = None
                    for kc in range(8):
                        last = t.matmul(
                            pPr[ct % 2][:],
                            wsel[:, kc * 1024 + ct * 128:kc * 1024 + (ct + 1) * 128],
                            sXn[buf][:, kc * 512:(kc + 1) * 512],
                            start=(kc == 0), stop=(kc == 7))
                    S.inc(last, "tp")

                if ct == 0:
                    @block.vector
                    def _(v, g=g, ct=ct):
                        S.wait(v, "tp", S.n["tp"])
                        pos = g * 512
                        S.chain(v, v.stream_shuffle(
                            srp[:], pPr[ct % 2][0:64, :],
                            [i ^ 1 for i in range(32)]), "vs")
                        S.chain(v, v.tensor_mul(st1_[:], pPr[ct % 2][0:64, :],
                                                sCos[:, pos:pos + 512]), "vs")
                        S.chain(v, v.tensor_mul(st2_[:], srp[:],
                                                sSin[:, pos:pos + 512]), "vs")
                        S.inc(v.tensor_add(st1_[:], st1_[:], st2_[:]), "vp")

                    @block.scalar
                    def _(a, kind=kind, g=g, ct=ct, sl0=sl0,
                          head_first=head_first):
                        S.wait(a, "vp", S.n["vp"])
                        if head_first[0]:
                            S.wait(a, "ta", S.n["ta"])
                            S.wait(a, "tav", S.n["tav"])
                            head_first[0] = False
                        if kind == "k":
                            a.activation(sKT[0:64, g * 512:(g + 1) * 512],
                                         st1_[:], AF.Silu)
                            S.inc(a.activation(
                                sKT[64:128, ct * TLOC + g * 512:
                                    ct * TLOC + (g + 1) * 512],
                                pPr[ct % 2][64:128, :], AF.Silu), "aq")
                        else:
                            a.activation(
                                bass.AP(sQT.ap().tensor, sl0 * 1024,
                                        [[NSL * 1024, 64], [1024, 4], [1, 128]]),
                                st1_[:].rearrange("p (s n) -> p s n", s=4),
                                AF.Silu)
                            S.inc(a.activation(
                                bass.AP(sQT.ap().tensor,
                                        64 * NSL * 1024 + sl0 * 1024,
                                        [[NSL * 1024, 64], [1024, 4], [1, 128]]),
                                pPr[ct % 2][64:128, :].rearrange(
                                    "p (s n) -> p s n", s=4), AF.Silu), "aq")
                        aq_hist.append(S.n["aq"])
                else:
                    @block.scalar
                    def _(a, kind=kind, g=g, ct=ct, sl0=sl0):
                        S.wait(a, "tp", S.n["tp"])
                        if kind == "k":
                            S.inc(a.activation(
                                sKT[:, ct * TLOC + g * 512:
                                    ct * TLOC + (g + 1) * 512],
                                pPr[ct % 2][:], AF.Silu), "aq")
                        else:
                            S.inc(a.activation(
                                bass.AP(sQT.ap().tensor, sl0 * 1024 + ct * 128,
                                        [[NSL * 1024, 128], [1024, 4], [1, 128]]),
                                pPr[ct % 2][:].rearrange("p (s n) -> p s n", s=4),
                                AF.Silu), "aq")
                        aq_hist.append(S.n["aq"])
            if kind == "q":
                for tt in range(4):
                    qrdy[g * 4 + tt] = S.n["aq"]
            else:
                krdy[g] = S.n["aq"]
            xn_t[gi] = S.n["tp"]

        wld_t, krdy = {}, {}
        sta_after = {}

        for h in range(H1):
            head_first = [h > 0]

            @block.gpsimd
            def _(gp, h=h):
                S.wait(gp, "gw", S.n["gw"])
                if h > 0:
                    S.wait(gp, "tp", S.n["tp"])
                for (wd, ws) in ((WQ, sWq), (WK, sWk), (WV, sWv)):
                    S.inc(gp.dma_start(
                        out=ws[:].rearrange("p (k c) -> p k c", k=8),
                        in_=wd[:, h * C:(h + 1) * C].rearrange(
                            "(k p) c -> p k c", p=128)), "gw", 16)
            wld_t[h] = S.n["gw"]

            for g in range(4):
                emit_proj_group(h, "k", g, head_first)
            for g in range(3):
                emit_proj_group(h, "q", g, head_first)
                emit_proj_group(h, "v", g, head_first)
            for kg in range(8):
                if kg in (2, 4, 6):
                    g = kg // 2 + 2
                    emit_proj_group(h, "q", g, head_first)
                    emit_proj_group(h, "v", g, head_first)
                qts = list(range(2 * kg, min(2 * kg + 10, 24)))
                for qi, qt in enumerate(qts):
                    @block.tensor
                    def _(t, h=h, kg=kg, qt=qt, qi=qi):
                        S.wait(t, "aq", qrdy[qt])
                        S.wait(t, "aq", krdy[3])
                        if (h, kg, qi - 2) in am_t:
                            S.wait(t, "vm", am_t[(h, kg, qi - 2)])
                        elif qi < 2:
                            S.wait(t, "vm", S.n["vm"])
                        last = None
                        for kc in range(8):
                            last = t.matmul(
                                pA[qi % 2][:],
                                sQT[:, (qt % NSL) * 1024 + kc * 128:
                                    (qt % NSL) * 1024 + (kc + 1) * 128],
                                sKT[:, kc * TLOC + kg * 256:
                                    kc * TLOC + (kg + 1) * 256],
                                start=(kc == 0), stop=(kc == 7))
                        S.inc(last, "ta")

                    @block.vector
                    def _(v, h=h, kg=kg, qt=qt, qi=qi):
                        S.wait(v, "ta", S.n["ta"])
                        if (h, kg, qi - 2) in av_t:
                            S.wait(v, "tav", av_t[(h, kg, qi - 2)])
                        elif qi < 2:
                            S.wait(v, "tav", S.n["tav"])
                        o0 = qt - 2 * kg
                        mask = bass.AP(sDec.ap().tensor, (o0 + 1) * 128,
                                       [[11 * 128, 128], [-128, 2], [1, 128]])
                        S.inc(v.tensor_tensor(
                            sAm[qi % 2][:].rearrange("p (j n) -> p j n", j=2),
                            pA[qi % 2][:].rearrange("p (j n) -> p j n", j=2),
                            mask, OP.mult), "vm")
                        am_t[(h, kg, qi)] = S.n["vm"]

                    @block.tensor
                    def _(t, h=h, kg=kg, qt=qt, qi=qi, nq=len(qts)):
                        S.wait(t, "vm", am_t[(h, kg, qi)])
                        S.wait(t, "aq", vrdy[qt])
                        if qi == 0:
                            S.wait(t, "vev", S.n["vev"])
                        last = None
                        for j in range(2):
                            for half in range(2):
                                last = t.matmul(
                                    pOutb[j * 2 + half][:],
                                    sAm[qi % 2][:, j * 128:(j + 1) * 128],
                                    sV[:, (qt % NSL) * 1024 + half * 512:
                                        (qt % NSL) * 1024 + (half + 1) * 512],
                                    start=(qi == 0), stop=(qi == nq - 1))
                        S.inc(last, "tav")
                        av_t[(h, kg, qi)] = S.n["tav"]

                @block.vector
                def _(v, h=h, kg=kg):
                    S.wait(v, "tav", S.n["tav"])
                    prev = (h, kg - 1) if kg > 0 else (h - 1, 7)
                    if prev in sta_after:
                        S.wait(v, "sta", sta_after[prev])
                    last = None
                    for j in range(2):
                        for half in range(2):
                            last = v.tensor_copy(
                                sStg[j][:, half * 512:(half + 1) * 512],
                                pOutb[j * 2 + half][:])
                    S.inc(last, "vev")

                @block.gpsimd
                def _(gp, h=h, kg=kg):
                    S.wait(gp, "sta", S.n["sta"])
                    S.wait(gp, "vev", S.n["vev"])
                    for j in range(2):
                        kt = 2 * kg + j
                        S.inc(gp.dma_start(
                            out=s1pre[kt * 128:(kt + 1) * 128, :],
                            in_=sStg[j][:],
                            accum_op=(OP.add if h > 0 else OP.bypass)), "sta", 16)
                sta_after[(h, kg)] = S.n["sta"]

        sta_all = S.n["sta"]

        gn_v = {}
        for it in range(NT):
            @block.sync
            def _(sync, it=it):
                S.wait(sync, "lds", S.n["lds"])
                S.wait(sync, "sta", sta_all)
                if it >= 2:
                    S.wait(sync, "vg", gn_v[it - 2])
                    S.wait(sync, "st", 16 * (it - 1))
                S.inc(sync.dma_start(out=sStg[it % 2][:],
                                     in_=s1pre[it * 128:(it + 1) * 128, :]),
                      "lds", 16)
            lds_it = S.n["lds"]

            @block.vector
            def _(v, it=it, lds_it=lds_it):
                S.wait(v, "lds", lds_it)
                if it == 0:
                    S.wait(v, "tp", S.n["tp"])
                x3 = sStg[it % 2][:].rearrange("p (s d) -> p s d", s=16)
                S.chain(v, v.reduce_sum(sSm[:], x3, axis=AX.X), "vs")
                S.chain(v, v.tensor_scalar_mul(sMu[:], sSm[:], 1.0 / 64), "vs")
                mu_bc = bass.AP(sMu.ap().tensor, 0, [[16, 128], [1, 16], [0, 64]])
                sc = sXn[0].ap().bitcast(F32)[:, 0:C]
                S.chain(v, v.tensor_tensor(sc.rearrange("p (s d) -> p s d", s=16),
                                           x3, mu_bc, OP.subtract), "vs")
                sc2 = sXn[1].ap().bitcast(F32)[:, 0:C]
                S.chain(v, v.tensor_mul(sc2, sc, sc), "vs")
                S.chain(v, v.reduce_sum(sVr[:],
                                        sc2.rearrange("p (s d) -> p s d", s=16),
                                        axis=AX.X), "vs")
                S.inc(v.tensor_scalar(sVr[:], sVr[:], 1.0 / 64, EPS,
                                      OP.mult, OP.add), "vg")

            @block.scalar
            def _(a, it=it):
                S.wait(a, "vg", S.n["vg"])
                S.chain(a, a.activation(sLg[:], sVr[:], AF.Ln), "as_")
                S.inc(a.activation(sRs[:], sLg[:], AF.Exp, scale=-0.5), "ag")

            @block.vector
            def _(v, it=it):
                S.wait(v, "ag", S.n["ag"])
                sc = sXn[0].ap().bitcast(F32)[:, 0:C]
                rs_bc = bass.AP(sRs.ap().tensor, 0, [[16, 128], [1, 16], [0, 64]])
                sc2 = sXn[1].ap().bitcast(F32)[:, 0:C]
                S.chain(v, v.tensor_tensor(sc2.rearrange("p (s d) -> p s d", s=16),
                                           sc.rearrange("p (s d) -> p s d", s=16),
                                           rs_bc, OP.mult), "vs")
                S.chain(v, v.tensor_mul(sc, sc2, sGw[:]), "vs")
                S.inc(v.tensor_add(sStg[it % 2][:], sc, sGb[:]), "vg")
                gn_v[it] = S.n["vg"]

            @block.gpsimd
            def _(gp, it=it):
                S.wait(gp, "st", S.n["st"])
                S.wait(gp, "vg", gn_v[it])
                S.inc(gp.dma_start(out=s1[it * 128:(it + 1) * 128, :],
                                   in_=sStg[it % 2][:]), "st", 16)

        @block.gpsimd
        def _(gp):
            gp.wait_ge(S.h["st"], S.n["st"])

        S.check_futures()




def build_attn1():
    nc = bass.Bass("TRN2", target_bir_lowering=False)
    with contextlib.ExitStack() as stack:
        D = attn1_drams(nc)
        D["xn1srcs"] = lambda g: (D["xn1Te"], g * 512)
        S = Sems(nc, stack)
        block = stack.enter_context(nc.Block())
        emit_attn1(nc, block, S, stack, D)
    return nc

FFN1_WNAMES = ["encw", "encbR", "s2cw", "s2cb", "rw1", "rb1", "rw2", "rb2",
               "ew1", "eb1R", "ew2", "eb2r", "c2sw", "c2sbR", "decw", "decb",
               "lngR1", "lnbR1"]


def ffn1_drams(nc, kind="ExternalInput", okind="ExternalOutput"):
    F2, MD, NE2, EH = 2048, 64, 8, 1024
    d = {}
    if kind:
        d["xn1Tf"] = nc.dram_tensor("xn1Tf", [C, TLOC], BF16, kind=kind).ap()
        d["x1f"] = nc.dram_tensor("x1f", [TLOC, C], F32, kind=kind).ap()
        d["s1f"] = nc.dram_tensor("s1f", [TLOC, C], F32, kind=kind).ap()
    d["encw"] = nc.dram_tensor("encw", [C, F2], F32, kind="ExternalInput").ap()
    d["encbR"] = nc.dram_tensor("encbR", [128, 16], F32, kind="ExternalInput").ap()
    d["s2cw"] = nc.dram_tensor("s2cw", [F2, MD], F32, kind="ExternalInput").ap()
    d["s2cb"] = nc.dram_tensor("s2cb", [MD, 1], F32, kind="ExternalInput").ap()
    d["rw1"] = nc.dram_tensor("rw1", [MD, 64], F32, kind="ExternalInput").ap()
    d["rb1"] = nc.dram_tensor("rb1", [64, 1], F32, kind="ExternalInput").ap()
    d["rw2"] = nc.dram_tensor("rw2", [64, NE2], F32, kind="ExternalInput").ap()
    d["rb2"] = nc.dram_tensor("rb2", [1, NE2], F32, kind="ExternalInput").ap()
    d["ew1"] = nc.dram_tensor("ew1", [NE2, MD, EH], F32, kind="ExternalInput").ap()
    d["eb1R"] = nc.dram_tensor("eb1R", [128, NE2 * 8], F32, kind="ExternalInput").ap()
    d["ew2"] = nc.dram_tensor("ew2", [NE2, EH, MD], F32, kind="ExternalInput").ap()
    d["eb2r"] = nc.dram_tensor("eb2r", [1, NE2 * MD], F32, kind="ExternalInput").ap()
    d["c2sw"] = nc.dram_tensor("c2sw", [MD, F2], F32, kind="ExternalInput").ap()
    d["c2sbR"] = nc.dram_tensor("c2sbR", [128, 16], F32, kind="ExternalInput").ap()
    d["decw"] = nc.dram_tensor("decw", [F2, C], F32, kind="ExternalInput").ap()
    d["decb"] = nc.dram_tensor("decb", [1, C], F32, kind="ExternalInput").ap()
    d["lngR1"] = nc.dram_tensor("lngR1", [128, C], F32, kind="ExternalInput").ap()
    d["lnbR1"] = nc.dram_tensor("lnbR1", [128, C], F32, kind="ExternalInput").ap()
    d["x2"] = nc.dram_tensor("x2", [TLOC, C], F32, kind=okind).ap()
    return d


def emit_ffn1(nc, block, S, stack, DR):
    F2, MD, NE2, EH = 2048, 64, 8, 1024
    xn1T, x1, s1 = DR["xn1Tf"], DR["x1f"], DR["s1f"]
    encw, encbR, s2cw, s2cb = DR["encw"], DR["encbR"], DR["s2cw"], DR["s2cb"]
    rw1, rb1, rw2, rb2 = DR["rw1"], DR["rb1"], DR["rw2"], DR["rb2"]
    ew1, eb1R, ew2, eb2r = DR["ew1"], DR["eb1R"], DR["ew2"], DR["eb2r"]
    c2sw, c2sbR, decw, decb = DR["c2sw"], DR["c2sbR"], DR["decw"], DR["decb"]
    lngR, lnbR = DR["lngR1"], DR["lnbR1"]
    identb, onesf, onesb = DR["identb"], DR["onesf"], DR["onesb"]
    x2 = DR["x2"]
    if True:
        E = stack.enter_context

        sWenc = E(nc.sbuf_tensor("sWenc", [128, 8 * F2], BF16))
        sWs2c = E(nc.sbuf_tensor("sWs2c", [128, 16 * MD], BF16))
        sWr1 = E(nc.sbuf_tensor("sWr1", [64, 64], BF16))
        sWr2 = E(nc.sbuf_tensor("sWr2", [64, NE2], BF16))
        sRb2 = E(nc.sbuf_tensor("sRb2", [1, NE2], BF16))
        sE1 = E(nc.sbuf_tensor("sE1", [64, NE2 * EH], BF16))
        sE2 = E(nc.sbuf_tensor("sE2", [128, 8 * NE2 * MD], BF16))
        sEb2 = E(nc.sbuf_tensor("sEb2", [1, NE2 * MD], BF16))
        sWc2s = E(nc.sbuf_tensor("sWc2s", [64, F2], BF16))
        sWdec = E(nc.sbuf_tensor("sWdec", [128, 16 * C], BF16))
        sDecb = E(nc.sbuf_tensor("sDecb", [1, C], BF16))
        sEncb = E(nc.sbuf_tensor("sEncb", [128, 16], F32))
        sS2cb = E(nc.sbuf_tensor("sS2cb", [64, 1], F32))
        sRb1 = E(nc.sbuf_tensor("sRb1", [64, 1], F32))
        sEb1 = E(nc.sbuf_tensor("sEb1", [128, NE2 * 8], F32))
        sC2sb = E(nc.sbuf_tensor("sC2sb", [128, 16], F32))
        sLng = E(nc.sbuf_tensor("sLng", [128, C], F32))
        sLnb = E(nc.sbuf_tensor("sLnb", [128, C], F32))
        sIdb = E(nc.sbuf_tensor("sIdb", [128, 128], BF16))
        sOnes = E(nc.sbuf_tensor("sOnes", [1, 128], F32))
        sOnesB = E(nc.sbuf_tensor("sOnesB", [1, 128], BF16))
        sXnt = [E(nc.sbuf_tensor(f"sXnt{j}", [128, 8 * 512], BF16))
                for j in range(2)]
        sSPg = E(nc.sbuf_tensor("sSPg", [128, 16 * 512], BF16))
        sCT = E(nc.sbuf_tensor("sCT", [64, TLOC], BF16))
        sRh = E(nc.sbuf_tensor("sRh", [64, 512], BF16))
        sLgt = E(nc.sbuf_tensor("sLgt", [128, 8], F32))
        sM1 = E(nc.sbuf_tensor("sM1", [128, 1], F32))
        sM2 = E(nc.sbuf_tensor("sM2", [128, 1], F32))
        sW1c = E(nc.sbuf_tensor("sW1c", [128, 1], F32))
        sW2c = E(nc.sbuf_tensor("sW2c", [128, 1], F32))
        sOh1 = E(nc.sbuf_tensor("sOh1", [128, 8], F32))
        sOh2 = E(nc.sbuf_tensor("sOh2", [128, 8], F32))
        sCB = E(nc.sbuf_tensor("sCB", [128, NT * 8], F32))
        sEh = E(nc.sbuf_tensor("sEh", [128, 8 * 512], BF16))
        sMoe = E(nc.sbuf_tensor("sMoe", [128, NT * MD], F32))
        sMoeb = E(nc.sbuf_tensor("sMoeb", [128, MD], BF16))
        sMT = E(nc.sbuf_tensor("sMT", [64, TLOC], BF16))
        sRTg = E(nc.sbuf_tensor("sRTg", [128, 16 * 512], BF16))
        sDc = E(nc.sbuf_tensor("sDc", [128, C], F32))
        sSc = E(nc.sbuf_tensor("sSc", [128, C], F32))
        sSc2 = E(nc.sbuf_tensor("sSc2", [128, C], F32))
        sXs = [E(nc.sbuf_tensor(f"sXs{j}", [128, C], F32)) for j in range(2)]
        sSs = [E(nc.sbuf_tensor(f"sSs{j}", [128, C], F32)) for j in range(2)]
        stA = E(nc.sbuf_tensor("stA", [128, 1], F32))
        stB = E(nc.sbuf_tensor("stB", [128, 1], F32))
        stC = E(nc.sbuf_tensor("stC", [128, 1], F32))
        stD = E(nc.sbuf_tensor("stD", [128, 1], F32))
        pP = [E(nc.psum_tensor(f"pP{j}", [128, 512], F32)) for j in range(4)]
        pTb16 = E(nc.psum_tensor("pTb16", [128, 256], BF16))

        S.new("lds", "gw", "st", "tp", "tc", "tr", "te", "td",
              "vp", "vc", "vr", "ve", "vd", "aq", "ar", "ae", "ad",
              "vs", "as_")

        @block.gpsimd
        def _(gp):
            S.inc(gp.dma_start(out=sWenc[:].rearrange("p (k f) -> p k f", k=8),
                               in_=encw.rearrange("(k p) f -> p k f", p=128)),
                  "gw", 16)
            S.inc(gp.dma_start(out=sWs2c[:].rearrange("p (k m) -> p k m", k=16),
                               in_=s2cw.rearrange("(k p) m -> p k m", p=128)),
                  "gw", 16)
            S.inc(gp.dma_start(out=sWr1[:], in_=rw1[:, :]), "gw", 16)
            S.inc(gp.dma_start(out=sWr2[:], in_=rw2[:, :]), "gw", 16)
            S.inc(gp.dma_start(out=sRb2[:], in_=rb2[:, :]), "gw", 16)
            S.inc(gp.dma_start(out=sE1[:].rearrange("p (e f) -> p e f", e=NE2),
                               in_=ew1.rearrange("e p f -> p e f")), "gw", 16)
            for k8 in range(8):
                S.inc(gp.dma_start(
                    out=sE2[:, k8 * NE2 * MD:(k8 + 1) * NE2 * MD].rearrange(
                        "p (e m) -> p e m", e=NE2),
                    in_=ew2[:, k8 * 128:(k8 + 1) * 128, :].rearrange(
                        "e p m -> p e m")), "gw", 16)
            S.inc(gp.dma_start(out=sEb2[:], in_=eb2r[:, :]), "gw", 16)
            S.inc(gp.dma_start(out=sWc2s[:], in_=c2sw.rearrange("m f -> m f")),
                  "gw", 16)
            S.inc(gp.dma_start(out=sWdec[:].rearrange("p (k c) -> p k c", k=16),
                               in_=decw.rearrange("(k p) c -> p k c", p=128)),
                  "gw", 16)
            S.inc(gp.dma_start(out=sDecb[:], in_=decb[:, :]), "gw", 16)
        gw_all = S.n["gw"]

        @block.sync
        def _(sync):
            for (dst, src) in ((sEncb, encbR), (sS2cb, s2cb), (sRb1, rb1),
                               (sEb1, eb1R), (sC2sb, c2sbR), (sLng, lngR),
                               (sLnb, lnbR), (sIdb, identb), (sOnes, onesf),
                               (sOnesB, onesb)):
                S.inc(sync.dma_start(out=dst[:], in_=src[:, :]), "lds", 16)
            sync.wait_ge(S.h["lds"], S.n["lds"])
        lds_all = S.n["lds"]

        xn_t, sp_t, ct_t, rt_t = {}, {}, {}, {}
        for ig in range(4):
            @block.sync
            def _(sync, ig=ig):
                S.wait(sync, "lds", S.n["lds"])
                if ig >= 2:
                    S.wait(sync, "tp", xn_t[ig - 2])
                S.inc(sync.dma_start(
                    out=sXnt[ig % 2][:].rearrange("p (k n) -> p k n", k=8),
                    in_=xn1T[:, ig * 512:(ig + 1) * 512].rearrange(
                        "(k p) n -> p k n", p=128)), "lds", 16)
            lds_g = S.n["lds"]

            for ft in range(16):
                @block.tensor
                def _(t, ig=ig, ft=ft, lds_g=lds_g):
                    S.wait(t, "gw", gw_all)
                    S.wait(t, "lds", lds_g)
                    if ft == 0 and ig > 0:
                        S.wait(t, "vc", S.n["vc"])
                    if ft >= 2:
                        S.wait(t, "aq", S.n["aq"] - 1)
                    last = None
                    for kc in range(8):
                        last = t.matmul(
                            pP[ft % 2][:],
                            sWenc[:, kc * F2 + ft * 128:kc * F2 + (ft + 1) * 128],
                            sXnt[ig % 2][:, kc * 512:(kc + 1) * 512],
                            start=(kc == 0), stop=(kc == 7))
                    S.inc(last, "tp")
                    if ft == 15:
                        xn_t[ig] = S.n["tp"]

                @block.scalar
                def _(a, ig=ig, ft=ft):
                    S.wait(a, "tp", S.n["tp"])
                    S.inc(a.activation(sSPg[:, ft * 512:(ft + 1) * 512],
                                       pP[ft % 2][:], AF.Sigmoid,
                                       bias=sEncb[:, ft:ft + 1]), "aq")

            @block.tensor
            def _(t, ig=ig):
                S.wait(t, "aq", S.n["aq"])
                if ig > 0:
                    S.wait(t, "vc", S.n["vc"])
                last = None
                for fc in range(16):
                    last = t.matmul(pP[2][0:64, :],
                                    sWs2c[:, fc * MD:(fc + 1) * MD],
                                    sSPg[:, fc * 512:(fc + 1) * 512],
                                    start=(fc == 0), stop=(fc == 15))
                S.inc(last, "tc")

            @block.vector
            def _(v, ig=ig):
                S.wait(v, "tc", S.n["tc"])
                S.inc(v.tensor_single_scalar(
                    sCT[:, ig * 512:(ig + 1) * 512], pP[2][0:64, :],
                    sS2cb[:, 0:1], OP.add), "vc")
                ct_t[ig] = S.n["vc"]

            @block.tensor
            def _(t, ig=ig):
                S.wait(t, "vc", ct_t[ig])
                if ig > 0:
                    S.wait(t, "ar", S.n["ar"])
                S.inc(t.matmul(pP[3][0:64, :], sWr1[:, :],
                               sCT[:, ig * 512:(ig + 1) * 512],
                               start=True, stop=True), "tr")

            @block.scalar
            def _(a, ig=ig):
                S.wait(a, "tr", S.n["tr"])
                S.inc(a.activation(sRh[:], pP[3][0:64, :], AF.Relu,
                                   bias=sRb1[:, 0:1]), "ar")

            for tt in range(4):
                it = ig * 4 + tt

                @block.tensor
                def _(t, it=it, tt=tt, ig=ig):
                    S.wait(t, "ar", S.n["ar"])
                    if it > 0:
                        S.wait(t, "vr", S.n["vr"])
                    t.matmul(pP[2][0:128, 0:8],
                             sRh[:, tt * 128:(tt + 1) * 128],
                             sWr2[:, :], start=True, stop=False)
                    S.inc(t.matmul(pP[2][0:128, 0:8], sOnesB[:, 0:128],
                                   sRb2[:, :], start=False, stop=True), "tr")

                @block.vector
                def _(v, it=it):
                    S.wait(v, "tr", S.n["tr"])
                    S.wait(v, "as_", S.n["as_"])
                    S.chain(v, v.tensor_copy(sLgt[:], pP[2][0:128, 0:8]), "vs")
                    S.chain(v, v.reduce_max(sM1[:], sLgt[:], axis=AX.X), "vs")
                    S.chain(v, v.tensor_single_scalar(sOh1[:], sLgt[:],
                                                      sM1[:, 0:1], OP.is_ge), "vs")
                    S.chain(v, v.scalar_tensor_tensor(
                        sOh2[:], sOh1[:], -1e30, sLgt[:], OP.mult, OP.add), "vs")
                    S.chain(v, v.reduce_max(sM2[:], sOh2[:], axis=AX.X), "vs")
                    S.chain(v, v.tensor_single_scalar(sOh2[:], sOh2[:],
                                                      sM2[:, 0:1], OP.is_ge), "vs")
                    S.inc(v.tensor_sub(sM2[:], sM2[:], sM1[:]), "vr")

                @block.scalar
                def _(a, it=it):
                    S.wait(a, "vr", S.n["vr"])
                    S.chain(a, a.activation(sW2c[:], sM2[:], AF.Sigmoid), "as_")
                    S.inc(a.activation(sW1c[:], sM2[:], AF.Sigmoid, scale=-1.0),
                          "as_")

                @block.vector
                def _(v, it=it):
                    S.wait(v, "as_", S.n["as_"])
                    S.chain(v, v.tensor_single_scalar(sOh1[:], sOh1[:],
                                                      sW1c[:, 0:1], OP.mult), "vs")
                    S.inc(v.scalar_tensor_tensor(
                        sCB[:, it * 8:(it + 1) * 8], sOh2[:], sW2c[:, 0:1],
                        sOh1[:], OP.mult, OP.add), "vr")
                    rt_t[it] = S.n["vr"]

        f01_done_vr = S.n["vr"]
        f01_done_aq = S.n["aq"]

        @block.vector
        def _(v):
            S.wait(v, "vr", f01_done_vr)
            S.inc(v.memset(sMoe[:], 0.0), "ve")

        eh_t, eo_t = {}, {}
        for e in range(NE2):
            for ig in range(4):
                for ht in range(8):
                    @block.tensor
                    def _(t, e=e, ig=ig, ht=ht):
                        S.wait(t, "vc", ct_t[ig])
                        if (e, ig, ht) != (0, 0, 0):
                            S.wait(t, "ae", max(S.n["ae"] - 1, 0))
                        if e == 0 and ig == 0 and ht == 0:
                            S.wait(t, "vr", f01_done_vr)
                        S.inc(t.matmul(
                            pP[ht % 2][:],
                            sE1[:, e * EH + ht * 128:e * EH + (ht + 1) * 128],
                            sCT[:, ig * 512:(ig + 1) * 512],
                            start=True, stop=True), "te")

                    @block.scalar
                    def _(a, e=e, ht=ht):
                        S.wait(a, "te", S.n["te"])
                        S.inc(a.activation(sEh[:, ht * 512:(ht + 1) * 512],
                                           pP[ht % 2][:], AF.Relu,
                                           bias=sEb1[:, e * 8 + ht:e * 8 + ht + 1]),
                              "ae")

                for tt in range(4):
                    it = ig * 4 + tt

                    @block.tensor
                    def _(t, e=e, ig=ig, tt=tt, it=it):
                        S.wait(t, "ae", S.n["ae"])
                        if (e, it) != (0, 0):
                            S.wait(t, "ve", S.n["ve"])
                        last = None
                        for hc in range(8):
                            last = t.matmul(
                                pP[2 + tt % 2][:, 0:MD],
                                sEh[:, hc * 512 + tt * 128:
                                    hc * 512 + (tt + 1) * 128],
                                sE2[:, hc * NE2 * MD + e * MD:
                                    hc * NE2 * MD + (e + 1) * MD],
                                start=(hc == 0), stop=False)
                        S.inc(t.matmul(pP[2 + tt % 2][:, 0:MD], sOnesB[:, :],
                                       sEb2[:, e * MD:(e + 1) * MD],
                                       start=False, stop=True), "te")

                    @block.vector
                    def _(v, e=e, it=it, tt=tt):
                        S.wait(v, "te", S.n["te"])
                        S.wait(v, "ve", 1)
                        sl = sMoe[:, it * MD:(it + 1) * MD]
                        S.inc(v.scalar_tensor_tensor(
                            sl, pP[2 + tt % 2][:, 0:MD],
                            sCB[:, it * 8 + e:it * 8 + e + 1], sl,
                            OP.mult, OP.add), "ve")

        f2_done_ve = S.n["ve"]

        mt_t = {}
        for it in range(NT):
            @block.vector
            def _(v, it=it):
                if it == 0:
                    S.wait(v, "ve", f2_done_ve)
                else:
                    S.wait(v, "td", S.n["td"])
                S.inc(v.tensor_copy(sMoeb[:, 0:MD], sMoe[:, it * MD:(it + 1) * MD]),
                      "vd")

            @block.tensor
            def _(t, it=it):
                S.wait(t, "vd", S.n["vd"])
                if it >= 2:
                    S.wait(t, "ve", S.n["ve"])
                S.inc(t.transpose(pTb16[0:64, it % 2 * 128:(it % 2 + 1) * 128],
                                  sMoeb[:, 0:MD], sIdb[:]), "td")

            @block.vector
            def _(v, it=it):
                S.wait(v, "td", S.n["td"])
                S.inc(v.tensor_copy(sMT[:, it * 128:(it + 1) * 128],
                                    pTb16[0:64, it % 2 * 128:(it % 2 + 1) * 128]),
                      "ve")
                mt_t[it] = S.n["ve"]

        dc_t, out_t = {}, {}
        for ig in range(4):
            for ft in range(16):
                @block.tensor
                def _(t, ig=ig, ft=ft):
                    S.wait(t, "ve", mt_t[ig * 4 + 3])
                    if (ig, ft) != (0, 0):
                        S.wait(t, "ad", S.n["ad"])
                    S.inc(t.matmul(pP[ft % 2][:],
                                   sWc2s[:, ft * 128:(ft + 1) * 128],
                                   sMT[:, ig * 512:(ig + 1) * 512],
                                   start=True, stop=True), "td")

                @block.scalar
                def _(a, ig=ig, ft=ft):
                    S.wait(a, "td", S.n["td"])
                    S.inc(a.activation(sRTg[:, ft * 512:(ft + 1) * 512],
                                       pP[ft % 2][:], AF.Sigmoid,
                                       bias=sC2sb[:, ft:ft + 1]), "ad")

            for tt in range(4):
                it = ig * 4 + tt

                @block.sync
                def _(sync, it=it):
                    S.wait(sync, "lds", S.n["lds"])
                    if it >= 2:
                        S.wait(sync, "vd", out_t.get(it - 2, 0))
                        S.wait(sync, "st", 16 * (it - 1))
                    S.inc(sync.dma_start(out=sXs[it % 2][:],
                                         in_=x1[it * 128:(it + 1) * 128, :]),
                          "lds", 16)
                    S.inc(sync.dma_start(out=sSs[it % 2][:],
                                         in_=s1[it * 128:(it + 1) * 128, :]),
                          "lds", 16)
                lds_it = S.n["lds"]

                @block.tensor
                def _(t, ig=ig, tt=tt, it=it):
                    S.wait(t, "ad", S.n["ad"])
                    if it > 0:
                        S.wait(t, "vd", S.n["vd"])
                    last = None
                    for half in range(2):
                        for fc in range(16):
                            last = t.matmul(
                                pP[2 + half][:],
                                sRTg[:, fc * 512 + tt * 128:
                                    fc * 512 + (tt + 1) * 128],
                                sWdec[:, fc * C + half * 512:
                                      fc * C + (half + 1) * 512],
                                start=(fc == 0), stop=False)
                        last = t.matmul(pP[2 + half][:], sOnesB[:, :],
                                        sDecb[:, half * 512:(half + 1) * 512],
                                        start=False, stop=True)
                        S.inc(last, "td")

                @block.scalar
                def _(a, it=it):
                    S.wait(a, "td", S.n["td"] - 1)
                    a.activation(sDc[:, 0:512], pP[2][:], AF.Sigmoid)
                    S.wait(a, "td", S.n["td"])
                    S.inc(a.activation(sDc[:, 512:1024], pP[3][:], AF.Sigmoid),
                          "ad")

                @block.vector
                def _(v, it=it):
                    S.wait(v, "ad", S.n["ad"])
                    S.inc(v.engine_nop(), "vd")
                    S.chain(v, v.reduce_sum(stA[:], sDc[:], axis=AX.X), "vs")
                    S.chain(v, v.tensor_scalar_mul(stB[:], stA[:], 1.0 / C), "vs")
                    S.inc(v.tensor_single_scalar(sSc[:], sDc[:], stB[:, 0:1],
                                                 OP.subtract), "vr")

                @block.scalar
                def _(a, it=it):
                    S.wait(a, "vr", S.n["vr"])
                    S.inc(a.activation(sSc2[:], sSc[:], AF.Square,
                                       accum_out=stC[:]), "ad")

                @block.vector
                def _(v, it=it):
                    S.wait(v, "ad", S.n["ad"])
                    S.inc(v.tensor_scalar(stC[:], stC[:], 1.0 / C, EPS,
                                          OP.mult, OP.add), "vr")

                @block.scalar
                def _(a, it=it):
                    S.wait(a, "vr", S.n["vr"])
                    S.chain(a, a.activation(stD[:], stC[:], AF.Ln), "as_")
                    S.inc(a.activation(stD[:], stD[:], AF.Exp, scale=-0.5), "ad")

                @block.vector
                def _(v, it=it, lds_it=lds_it):
                    S.wait(v, "ad", S.n["ad"])
                    S.wait(v, "lds", lds_it)
                    S.chain(v, v.tensor_single_scalar(sSc[:], sSc[:],
                                                      stD[:, 0:1], OP.mult), "vs")
                    S.chain(v, v.tensor_mul(sSc2[:], sSc[:], sLng[:]), "vs")
                    S.chain(v, v.tensor_add(sSc2[:], sSc2[:], sLnb[:]), "vs")
                    S.inc(v.engine_nop(), "vr")

                @block.scalar
                def _(a, it=it):
                    S.wait(a, "vr", S.n["vr"])
                    if it >= 2:
                        S.wait(a, "st", 16 * (it - 1))
                    S.inc(a.activation(sDc[:], sSc2[:], AF.Silu), "ad")

                @block.vector
                def _(v, it=it):
                    S.wait(v, "ad", S.n["ad"])
                    S.chain(v, v.tensor_mul(sSs[it % 2][:], sSs[it % 2][:],
                                            sDc[:]), "vs")
                    S.inc(v.tensor_add(sXs[it % 2][:], sXs[it % 2][:],
                                       sSs[it % 2][:]), "vd")
                    out_t[it] = S.n["vd"]

                @block.gpsimd
                def _(gp, it=it):
                    S.wait(gp, "st", S.n["st"])
                    S.wait(gp, "vd", out_t[it])
                    S.inc(gp.dma_start(out=x2[it * 128:(it + 1) * 128, :],
                                       in_=sXs[it % 2][:]), "st", 16)

        @block.gpsimd
        def _(gp):
            gp.wait_ge(S.h["st"], S.n["st"])

        S.check_futures()




def build_ffn1():
    nc = bass.Bass("TRN2", target_bir_lowering=False)
    with contextlib.ExitStack() as stack:
        D = ffn1_drams(nc)
        D["identb"] = nc.dram_tensor("identb", [128, 128], BF16,
                                     kind="ExternalInput").ap()
        D["onesf"] = nc.dram_tensor("onesf", [1, 128], F32,
                                    kind="ExternalInput").ap()
        D["onesb"] = nc.dram_tensor("onesb", [1, 128], BF16,
                                    kind="ExternalInput").ap()
        S = Sems(nc, stack)
        block = stack.enter_context(nc.Block())
        emit_ffn1(nc, block, S, stack, D)
    return nc

def emit_ln1f(nc, block, S, stack, DR):
    xfull = DR["xfull"]
    ident = DR["identb"]
    xnT_d, xn0_d, xlnbT_d = DR["xnT_d"], DR["xn0_d"], DR["xlnbT_d"]
    NTF = N // 128
    E = stack.enter_context
    sx = [E(nc.sbuf_tensor(f"L1sx{j}", [128, C], F32)) for j in range(2)]
    sxc = E(nc.sbuf_tensor("L1sxc", [128, C], F32))
    sxc2 = E(nc.sbuf_tensor("L1sxc2", [128, C], F32))
    scrap = E(nc.sbuf_tensor("L1scrap", [128, C], F32))
    sxn0 = [E(nc.sbuf_tensor(f"L1sxn0{j}", [128, C], F32)) for j in range(2)]
    sxn0b = E(nc.sbuf_tensor("L1sxn0b", [128, C], BF16))
    sxlnb = E(nc.sbuf_tensor("L1sxlnb", [128, C], BF16))
    sxt = [E(nc.sbuf_tensor(f"L1sxt{j}", [128, C], BF16)) for j in range(2)]
    sxt2 = [E(nc.sbuf_tensor(f"L1sxt2{j}", [128, C], BF16)) for j in range(2)]
    sident = E(nc.sbuf_tensor("L1sident", [128, 128], BF16))
    st1 = E(nc.sbuf_tensor("L1st1", [128, 1], F32))
    smu = E(nc.sbuf_tensor("L1smu", [128, 1], F32))
    sss = E(nc.sbuf_tensor("L1sss", [128, 1], F32))
    svar = E(nc.sbuf_tensor("L1svar", [128, 1], F32))
    slog = E(nc.sbuf_tensor("L1slog", [128, 1], F32))
    srstd = E(nc.sbuf_tensor("L1srstd", [128, 1], F32))
    pt = [E(nc.psum_tensor(f"L1pt{j}", [128, 512], BF16)) for j in range(2)]

    S.new("ld", "ldi", "v", "a", "t", "st", "st2", "vs", "as_")
    v_t, a_t, t_t, e_t, st_t = {}, {}, {}, {}, {}

    @block.sync
    def _(sync):
        S.inc(sync.dma_start(out=sident[:], in_=ident[:, :]), "ldi", 16)
        sync.wait_ge(S.h["ldi"], 16)

    for i in range(NTF):
        local = i < NT

        @block.sync
        def _(sync, i=i):
            S.wait(sync, "ld", S.n["ld"])
            if i >= 2:
                S.wait(sync, "v", v_t[i - 2])
            S.inc(sync.dma_start(out=sx[i % 2][:],
                                 in_=xfull[i * 128:(i + 1) * 128, :]), "ld", 16)
        ld_i = S.n["ld"]

        @block.vector
        def _(v, i=i, ld_i=ld_i, local=local):
            S.wait(v, "ld", ld_i)
            S.chain(v, v.reduce_sum(st1[:], sx[i % 2][:], axis=AX.X), "vs")
            S.chain(v, v.tensor_scalar_mul(smu[:], st1[:], 1.0 / C), "vs")
            S.inc(v.tensor_single_scalar(sxc[:], sx[i % 2][:], smu[:, 0:1],
                                         OP.subtract), "v")
            v_t[i] = S.n["v"]

        @block.scalar
        def _(a, i=i):
            S.wait(a, "v", v_t[i])
            S.inc(a.activation(scrap[:], sxc[:], AF.Square, accum_out=sss[:]),
                  "a")

        @block.vector
        def _(v, i=i):
            S.wait(v, "a", S.n["a"])
            S.inc(v.tensor_scalar(svar[:], sss[:], 1.0 / C, EPS,
                                  OP.mult, OP.add), "v")

        @block.scalar
        def _(a, i=i):
            S.wait(a, "v", S.n["v"])
            S.chain(a, a.activation(slog[:], svar[:], AF.Ln), "as_")
            S.inc(a.activation(srstd[:], slog[:], AF.Exp, scale=-0.5), "a")
        a_t[i] = S.n["a"]

        @block.vector
        def _(v, i=i, local=local):
            S.wait(v, "a", a_t[i])
            if i >= 1:
                S.wait(v, "t", t_t[i - 1])
            if local and i >= 2:
                S.wait(v, "st2", st_t[i - 2])
            if local:
                S.chain(v, v.tensor_single_scalar(sxn0[i % 2][:], sxc[:],
                                                  srstd[:, 0:1], OP.mult), "vs")
                v.tensor_copy(sxn0b[:], sxn0[i % 2][:])
            else:
                v.tensor_single_scalar(sxn0b[:], sxc[:], srstd[:, 0:1], OP.mult)
            S.inc(v.engine_nop(), "v")
            v_t[(i, "b")] = S.n["v"]
            if local:
                S.chain(v, v.reduce_sum(st1[:], sxn0[i % 2][:], axis=AX.X), "vs")
                S.chain(v, v.tensor_scalar_mul(smu[:], st1[:], 1.0 / C), "vs")
                S.inc(v.tensor_single_scalar(sxc2[:], sxn0[i % 2][:],
                                             smu[:, 0:1], OP.subtract), "v")

        if local:
            @block.gpsimd
            def _(gp, i=i):
                S.wait(gp, "st2", S.n["st2"])
                S.wait(gp, "v", v_t[(i, "b")])
                S.inc(gp.dma_start(out=xn0_d[i * 128:(i + 1) * 128, :],
                                   in_=sxn0[i % 2][:]), "st2", 16)
            st_t[i] = S.n["st2"]

            @block.scalar
            def _(a, i=i):
                S.wait(a, "v", S.n["v"])
                S.inc(a.activation(scrap[:], sxc2[:], AF.Square,
                                   accum_out=sss[:]), "a")

            @block.vector
            def _(v, i=i):
                S.wait(v, "a", S.n["a"])
                S.inc(v.tensor_scalar(svar[:], sss[:], 1.0 / C, EPS,
                                      OP.mult, OP.add), "v")

            @block.scalar
            def _(a, i=i):
                S.wait(a, "v", S.n["v"])
                S.chain(a, a.activation(slog[:], svar[:], AF.Ln), "as_")
                S.inc(a.activation(srstd[:], slog[:], AF.Exp, scale=-0.5), "a")

            @block.vector
            def _(v, i=i):
                S.wait(v, "a", S.n["a"])
                if i >= 1:
                    S.wait(v, "t", t_t[i - 1])
                S.inc(v.tensor_single_scalar(sxlnb[:], sxc2[:], srstd[:, 0:1],
                                             OP.mult), "v")
                v_t[(i, "c")] = S.n["v"]

        @block.tensor
        def _(t, i=i, local=local):
            S.wait(t, "ldi", 16)
            S.wait(t, "v", v_t[(i, "c")] if local else v_t[(i, "b")])
            if i >= 1:
                S.wait(t, "v", e_t[i - 1])
            last = None
            for k in range(4):
                last = t.transpose(pt[0][:, k * 128:(k + 1) * 128],
                                   sxn0b[:, k * 128:(k + 1) * 128], sident[:])
                if local:
                    last = t.transpose(pt[1][:, k * 128:(k + 1) * 128],
                                       sxlnb[:, k * 128:(k + 1) * 128],
                                       sident[:])
            S.inc(last, "t")
            t_t[(i, 0)] = S.n["t"]

        @block.vector
        def _(v, i=i, local=local):
            S.wait(v, "t", t_t[(i, 0)])
            if i >= 2:
                S.wait(v, "st", st_t[(i - 2, "T")])
            last = None
            for k in range(4):
                last = v.tensor_copy(sxt[i % 2][:, k * 128:(k + 1) * 128],
                                     pt[0][:, k * 128:(k + 1) * 128])
                if local:
                    last = v.tensor_copy(sxt2[i % 2][:, k * 128:(k + 1) * 128],
                                         pt[1][:, k * 128:(k + 1) * 128])
            S.inc(last, "v")
            e_t[(i, 0)] = S.n["v"]

        @block.tensor
        def _(t, i=i, local=local):
            S.wait(t, "v", e_t[(i, 0)])
            last = None
            for k in range(4, 8):
                last = t.transpose(pt[0][:, (k - 4) * 128:(k - 3) * 128],
                                   sxn0b[:, k * 128:(k + 1) * 128], sident[:])
                if local:
                    last = t.transpose(pt[1][:, (k - 4) * 128:(k - 3) * 128],
                                       sxlnb[:, k * 128:(k + 1) * 128],
                                       sident[:])
            S.inc(last, "t")
            t_t[i] = S.n["t"]

        @block.vector
        def _(v, i=i, local=local):
            S.wait(v, "t", t_t[i])
            last = None
            for k in range(4, 8):
                last = v.tensor_copy(sxt[i % 2][:, k * 128:(k + 1) * 128],
                                     pt[0][:, (k - 4) * 128:(k - 3) * 128])
                if local:
                    last = v.tensor_copy(sxt2[i % 2][:, k * 128:(k + 1) * 128],
                                         pt[1][:, (k - 4) * 128:(k - 3) * 128])
            S.inc(last, "v")
            e_t[i] = S.n["v"]

        @block.gpsimd
        def _(gp, i=i, local=local):
            S.wait(gp, "st", S.n["st"])
            S.wait(gp, "v", e_t[i])
            last = gp.dma_start(
                out=xnT_d[:, i * 128:(i + 1) * 128].rearrange(
                    "(k p) n -> p k n", p=128),
                in_=sxt[i % 2][:].rearrange("p (k n) -> p k n", k=8))
            S.inc(last, "st", 16)
            if local:
                S.inc(gp.dma_start(
                    out=xlnbT_d[:, i * 128:(i + 1) * 128].rearrange(
                        "(k p) n -> p k n", p=128),
                    in_=sxt2[i % 2][:].rearrange("p (k n) -> p k n", k=8)),
                    "st", 16)
            st_t[(i, "T")] = S.n["st"]

    @block.gpsimd
    def _(gp):
        gp.wait_ge(S.h["st"], S.n["st"])
        gp.wait_ge(S.h["st2"], S.n["st2"])


def stage_barrier(nc, block, S_list, bar, idx):
    dma_sems = []
    for S in S_list:
        for nm, n in S.n.items():
            if n > 0:
                dma_sems.append((S.h[nm], n))

    for engname in ("sync", "gpsimd", "vector", "scalar", "tensor"):
        def body(eng, dma_sems=dma_sems, idx=idx):
            for h, n in dma_sems:
                eng.wait_ge(h, n)
            eng.sem_inc(bar, 1)
            eng.wait_ge(bar, 5 * (idx + 1))
        getattr(block, engname)(body)


def emit_tail_exchange(nc, block, S, stack, DR, no_cc=False):
    xn1T_d, gath_d, tail_d = DR["xn1T_d"], DR["gath_d"], DR["xn1tail_d"]
    tailidx, tmask = DR["tailidx"], DR["tmask"]
    E = stack.enter_context
    sidx = E(nc.sbuf_tensor("TXsidx", [128, 8], mybir.dt.int32))
    smask = E(nc.sbuf_tensor("TXsmask", [128, 1], F32))
    stl = E(nc.sbuf_tensor("TXstl", [128, 1024], BF16))
    stl2 = E(nc.sbuf_tensor("TXstl2", [128, 1024], BF16))
    S.new("cc", "ld", "gi", "vm", "stt")

    @block.sync
    def _(sync):
        S.inc(sync.dma_start(out=sidx[:], in_=tailidx[:, :]), "ld", 16)
        S.inc(sync.dma_start(out=smask[:], in_=tmask[:, :]), "ld", 16)

    tailsrc_d = DR["tailsrc_d"]

    @block.gpsimd
    def _(gp):
        S.inc(gp.dma_start(out=tailsrc_d[:, :], in_=xn1T_d[:, 0:1024]), "cc", 16)
        gp.wait_ge(S.h["cc"], 16)
        if no_cc:
            S.inc(gp.dma_start(out=gath_d[1024:2048, :],
                               in_=tailsrc_d[:, :]), "cc", 16)
        else:
            S.inc(gp.collective_compute(
                "AllGather", OP.bypass,
                [[0, 1, 2, 3], [4, 5, 6, 7]],
                [tailsrc_d[:, :].opt()], [gath_d[:, :].opt()]), "cc", 1)
        gp.wait_ge(S.h["cc"], S.n["cc"])
        gp.wait_ge(S.h["ld"], 32)
        for j in range(8):
            S.inc(gp.indirect_dma_start(
                stl[:], None, gath_d[:, :],
                bass.IndirectOffsetOnAxis(ap=sidx[:, j:j + 1], axis=0)),
                "gi", 16)
            gp.wait_ge(S.h["gi"], S.n["gi"])

            @block.vector
            def _(v, j=j):
                S.wait(v, "gi", S.n["gi"])
                if j >= 1:
                    S.wait(v, "stt", S.n["stt"])
                S.inc(v.tensor_single_scalar(stl2[:], stl[:], smask[:, 0:1],
                                             OP.mult), "vm")

            S.wait(gp, "vm", S.n["vm"])
            S.inc(gp.dma_start(out=tail_d[j * 128:(j + 1) * 128, :],
                               in_=stl2[:]), "stt", 16)
        gp.wait_ge(S.h["stt"], S.n["stt"])


class _Pfx:

    def __init__(self, nc, p):
        self._nc = nc
        self._p = p

    def sbuf_tensor(self, name, shape, dtype, **kw):
        return self._nc.sbuf_tensor(self._p + name, shape, dtype, **kw)

    def psum_tensor(self, name, shape, dtype, **kw):
        return self._nc.psum_tensor(self._p + name, shape, dtype, **kw)

    def __getattr__(self, a):
        return getattr(self._nc, a)


def build_fused(no_cc=False):
    nc = bass.Bass("TRN2", target_bir_lowering=False)
    with contextlib.ExitStack() as stack:
        identb = nc.dram_tensor("identb", [128, 128], BF16,
                                kind="ExternalInput").ap()
        identf = nc.dram_tensor("identf", [128, 128], F32,
                                kind="ExternalInput").ap()
        onesf = nc.dram_tensor("onesf", [1, 128], F32, kind="ExternalInput").ap()
        onesb = nc.dram_tensor("onesb", [1, 128], BF16, kind="ExternalInput").ap()
        xfull = nc.dram_tensor("xfull", [N, C], F32, kind="ExternalInput").ap()
        tailidx = nc.dram_tensor("tailidx", [128, 8], mybir.dt.int32,
                                 kind="ExternalInput").ap()
        tmask = nc.dram_tensor("tmask", [128, 1], F32, kind="ExternalInput").ap()
        xnT_d = nc.dram_tensor("xnT_d", [C, N], BF16, kind="Internal").ap()
        xn0_d = nc.dram_tensor("xn0_d", [TLOC, C], F32, kind="Internal").ap()
        xlnbT_d = nc.dram_tensor("xlnbT_d", [C, TLOC], BF16, kind="Internal").ap()
        s0_d = nc.dram_tensor("s0_d", [TLOC, C], F32, kind="Internal").ap()
        g0_d = nc.dram_tensor("g0_d", [TLOC, C], F32, kind="Internal").ap()
        x1_d = nc.dram_tensor("x1_d", [TLOC, C], F32, kind="Internal").ap()
        xn1T_d = nc.dram_tensor("xn1T_dd", [C, TLOC], BF16, kind="Internal").ap()
        gath_d = nc.dram_tensor("gath_d", [4096, 1024], BF16, kind="Internal").ap()
        xn1tail_d = nc.dram_tensor("xn1tail_d", [1024, 1024], BF16,
                                   kind="Internal").ap()
        tailsrc_d = nc.dram_tensor("tailsrc_d", [1024, 1024], BF16,
                                   kind="Internal").ap()
        s1_d = nc.dram_tensor("s1_d", [TLOC, C], F32, kind="Internal").ap()

        shared = {"identb": identb, "identf": identf, "onesf": onesf,
                  "onesb": onesb}
        bar = stack.enter_context(nc.semaphore("stagebar"))
        block = stack.enter_context(nc.Block())
        bidx = [0]

        def barrier(S_stage):
            stage_barrier(nc, block, [S_stage], bar, bidx[0])
            bidx[0] += 1

        with contextlib.ExitStack() as st1:
            S1 = Sems(nc, stack, prefix="a_")
            D1 = dict(shared, xfull=xfull, xnT_d=xnT_d, xn0_d=xn0_d,
                      xlnbT_d=xlnbT_d)
            emit_ln1f(_Pfx(nc, "a_"), block, S1, st1, D1)
            barrier(S1)

        with contextlib.ExitStack() as st2:
            S2 = Sems(nc, stack, prefix="b_")
            D2 = attn0_drams(nc, kind=None, okind=None)
            D2.update(shared)
            D2["xnT"] = xnT_d
            D2["s0"] = s0_d
            emit_attn0(_Pfx(nc, "b_"), block, S2, st2, D2)
            barrier(S2)

        with contextlib.ExitStack() as st3:
            S3 = Sems(nc, stack, prefix="c_")
            D3 = moe0_drams(nc, kind=None, okind=None)
            D3.update(shared)
            D3["xlnbT"] = xlnbT_d
            D3["xn0"] = xn0_d
            D3["g0"] = g0_d
            emit_moe0(_Pfx(nc, "c_"), block, S3, st3, D3)
            barrier(S3)

        with contextlib.ExitStack() as st4:
            S4 = Sems(nc, stack, prefix="d_")
            D4 = dict(shared)
            D4["x"] = xfull[0:TLOC, :]
            D4["s0c"] = s0_d
            D4["g0c"] = g0_d
            D4["x1"] = x1_d
            D4["xn1T"] = xn1T_d
            emit_comb1(_Pfx(nc, "d_"), block, S4, st4, D4)
            barrier(S4)

        with contextlib.ExitStack() as st4b:
            S4b = Sems(nc, stack, prefix="e_")
            D4b = {"xn1T_d": xn1T_d, "gath_d": gath_d, "xn1tail_d": xn1tail_d,
                   "tailsrc_d": tailsrc_d, "tailidx": tailidx, "tmask": tmask}
            emit_tail_exchange(_Pfx(nc, "e_"), block, S4b, st4b, D4b, no_cc=no_cc)
            barrier(S4b)

        with contextlib.ExitStack() as st5:
            S5 = Sems(nc, stack, prefix="f_")
            D5 = attn1_drams(nc, okind=None, fused=True)
            D5.update(shared)
            D5["xn1srcs"] = lambda g: ((xn1T_d, g * 512) if g < 4
                                       else (xn1tail_d, (g - 4) * 512))
            D5["s1"] = s1_d
            emit_attn1(_Pfx(nc, "f_"), block, S5, st5, D5)
            barrier(S5)

        with contextlib.ExitStack() as st6:
            S6 = Sems(nc, stack, prefix="g_")
            D6 = ffn1_drams(nc, kind=None, okind="ExternalOutput")
            D6.update(shared)
            D6["xn1Tf"] = xn1T_d
            D6["x1f"] = x1_d
            D6["s1f"] = s1_d
            emit_ffn1(_Pfx(nc, "g_"), block, S6, st6, D6)

    return nc


def kernel_fused(**inputs):
    inp = {k: np.ascontiguousarray(np.asarray(v)) for k, v in inputs.items()}
    nc = build_fused()
    dec, tabs = _attn1_consts()
    base = {
        "identb": _ident_np(),
        "identf": np.eye(128, dtype=np.float32),
        "onesf": np.ones((1, 128), dtype=np.float32),
        "onesb": np.ones((1, 128), dtype=bf16),
        "qkvwT": np.ascontiguousarray(inp["L0_attn_qkv_w"].T),
        "projwT": np.ascontiguousarray(inp["L0_attn_proj_w"].T),
        "qn_w": inp["L0_attn_q_norm_w"].reshape(1, -1),
        "qn_b": inp["L0_attn_q_norm_b"].reshape(1, -1),
        "kn_w": inp["L0_attn_k_norm_w"].reshape(1, -1),
        "kn_b": inp["L0_attn_k_norm_b"].reshape(1, -1),
        "proj_b": inp["L0_attn_proj_b"].reshape(1, -1),
        "cen": inp["L0_ffn_centroids"],
        "lngR": np.ascontiguousarray(
            inp["L0_ffn_ln_g"].reshape(8, 8, 128).transpose(2, 0, 1
                                                            ).reshape(128, 64)),
        "lnbR": np.ascontiguousarray(
            inp["L0_ffn_ln_b"].reshape(8, 8, 128).transpose(2, 0, 1
                                                            ).reshape(128, 64)),
        "w1": inp["L0_ffn_w1"],
        "b1R": np.ascontiguousarray(
            inp["L0_ffn_b1"].reshape(8, 32, 128).transpose(2, 0, 1
                                                           ).reshape(128, 256)),
        "w2": inp["L0_ffn_w2"],
        "b2": inp["L0_ffn_b2"],
        "WQ": inp["L1_attn_W_Q"], "WK": inp["L1_attn_W_K"],
        "WV": inp["L1_attn_W_V"],
        "dec11": dec,
        "gnwR": np.ascontiguousarray(np.tile(inp["L1_attn_gn_weight"], (128, 1))),
        "gnbR": np.ascontiguousarray(np.tile(inp["L1_attn_gn_bias"], (128, 1))),
    }
    base.update(_ffn1_base(inp))
    xf = inp["L0_attn_x"].reshape(B * N, C)
    in_maps = []
    for c in range(NCORES):
        b = c // 4
        off = (c % 4) * TLOC
        xr = np.ascontiguousarray(np.roll(xf[b * N:(b + 1) * N], -off, axis=0))
        rp = (c % 4) + 1
        valid = (c % 4) != 3
        idx = (np.arange(1024, dtype=np.int32) + (rp % 4) * 1024
               ).reshape(8, 128).T.copy()
        cosT, sinS = tabs[c]
        in_maps.append(dict(
            base, xfull=xr, tailidx=idx,
            tmask=np.full((128, 1), 1.0 if valid else 0.0, np.float32),
            cosT=cosT, sinS=sinS))
    res = run_bass_kernel_spmd(nc, in_maps, core_ids=list(range(NCORES)))
    x2 = np.concatenate([res.results[c]["x2"] for c in range(NCORES)], 0)
    return x2.reshape(B, N, C).astype(np.float32)


def _ident_np():
    return np.eye(128, dtype=bf16)


def run_comb1(x_full, s0_by_core, g0_by_core):
    nc = build_comb1()
    xf = x_full.reshape(-1, C)
    ident = _ident_np()
    in_maps = [{"x": xf[c * TLOC:(c + 1) * TLOC], "s0c": s0_by_core[c],
                "g0c": g0_by_core[c], "identb": ident} for c in range(NCORES)]
    return run_bass_kernel_spmd(nc, in_maps, core_ids=list(range(NCORES))).results


def _attn1_consts():
    GAMMA = 0.96875
    invf = (1.0 / 10000.0 ** (np.arange(0, 64, 2, dtype=np.float32) / 64)
            ).astype(np.float32)
    r = np.arange(128)
    dec = np.zeros((128, 11 * 128), np.float32)
    for m in range(11):
        ex = 128 * (m - 1) + r[:, None] - r[None, :]
        dec[:, m * 128:(m + 1) * 128] = np.where(
            ex >= 0, np.float64(GAMMA) ** ex, 0.0).astype(np.float32)
    sgn = np.where(np.arange(64) % 2 == 0, -1.0, 1.0).astype(np.float32)
    tabs = {}
    for c in range(NCORES):
        base = (c % 4) * TLOC
        t = np.minimum(np.arange(base, base + 3072), N - 1).astype(np.float32)
        fr = np.repeat(t[:, None] * invf[None, :], 2, axis=1)
        tabs[c] = (np.ascontiguousarray(np.cos(fr).T.astype(np.float32)),
                   np.ascontiguousarray(np.sin(fr).T.astype(np.float32)
                                        * sgn[:, None]))
    return dec, tabs


def run_attn1(inp, xn1T_by_core):
    nc = build_attn1()
    dec, tabs = _attn1_consts()
    gnw = np.ascontiguousarray(np.tile(inp["L1_attn_gn_weight"], (128, 1)))
    gnb = np.ascontiguousarray(np.tile(inp["L1_attn_gn_bias"], (128, 1)))
    in_maps = []
    for c in range(NCORES):
        if c % 4 == 3:
            tail = np.zeros((C, 1024), dtype=bf16)
        else:
            tail = xn1T_by_core[c + 1][:, 0:1024]
        xn1Te = np.ascontiguousarray(
            np.concatenate([xn1T_by_core[c], tail], axis=1))
        cosT, sinS = tabs[c]
        in_maps.append({
            "xn1Te": xn1Te, "WQ": inp["L1_attn_W_Q"], "WK": inp["L1_attn_W_K"],
            "WV": inp["L1_attn_W_V"], "cosT": cosT, "sinS": sinS, "dec11": dec,
            "gnwR": gnw, "gnbR": gnb,
        })
    return run_bass_kernel_spmd(nc, in_maps, core_ids=list(range(NCORES))).results


def _ffn1_base(inp):
    return {
        "encw": inp["L1_ffn_enc_w"],
        "encbR": np.ascontiguousarray(inp["L1_ffn_enc_b"].reshape(16, 128).T),
        "s2cw": inp["L1_ffn_s2c_w"],
        "s2cb": inp["L1_ffn_s2c_b"].reshape(64, 1),
        "rw1": inp["L1_ffn_r_w1"], "rb1": inp["L1_ffn_r_b1"].reshape(64, 1),
        "rw2": inp["L1_ffn_r_w2"], "rb2": inp["L1_ffn_r_b2"].reshape(1, 8),
        "ew1": inp["L1_ffn_e_w1"],
        "eb1R": np.ascontiguousarray(
            inp["L1_ffn_e_b1"].reshape(8, 8, 128).transpose(2, 0, 1
                                                            ).reshape(128, 64)),
        "ew2": inp["L1_ffn_e_w2"],
        "eb2r": inp["L1_ffn_e_b2"].reshape(1, 512),
        "c2sw": inp["L1_ffn_c2s_w"],
        "c2sbR": np.ascontiguousarray(inp["L1_ffn_c2s_b"].reshape(16, 128).T),
        "decw": inp["L1_ffn_dec_w"],
        "decb": inp["L1_ffn_dec_b"].reshape(1, 1024),
        "lngR1": np.ascontiguousarray(np.tile(inp["L1_ffn_ln_g"], (128, 1))),
        "lnbR1": np.ascontiguousarray(np.tile(inp["L1_ffn_ln_b"], (128, 1))),
        "identb": _ident_np(),
        "onesf": np.ones((1, 128), dtype=np.float32),
        "onesb": np.ones((1, 128), dtype=bf16),
    }


def run_ffn1(inp, xn1T_by_core, x1_by_core, s1_by_core):
    nc = build_ffn1()
    base = _ffn1_base(inp)
    in_maps = [dict(base, xn1Tf=xn1T_by_core[c], x1f=x1_by_core[c],
                    s1f=s1_by_core[c]) for c in range(NCORES)]
    return run_bass_kernel_spmd(nc, in_maps, core_ids=list(range(NCORES))).results


def kernel(**inputs):
    return kernel_fused(**inputs)


def kernel_unfused(**inputs):
    inp = {k: np.ascontiguousarray(np.asarray(v)) for k, v in inputs.items()}
    x = inp["L0_attn_x"]

    r1 = run_ln1(x)
    xn0 = np.concatenate([r["xn0"] for r in r1], 0)
    xn0T = [r1[c]["xn0T"] for c in range(NCORES)]
    xlnbT = [r1[c]["xlnbT"] for c in range(NCORES)]

    xnT_by_core = []
    for c in range(NCORES):
        b = c // 4
        slab = np.concatenate([xn0T[4 * b + j] for j in range(4)], axis=1)
        xnT_by_core.append(np.ascontiguousarray(
            np.roll(slab, -((c % 4) * TLOC), axis=1)))
    r2 = run_attn0(xnT_by_core, inp)
    s0 = [r2[c]["s0"] for c in range(NCORES)]

    r3 = run_moe0(inp, xn0, xlnbT)
    g0 = [r3[c]["g0"] for c in range(NCORES)]

    r4 = run_comb1(x, s0, g0)
    x1 = [r4[c]["x1"] for c in range(NCORES)]
    xn1T = [r4[c]["xn1T"] for c in range(NCORES)]

    r5 = run_attn1(inp, xn1T)
    s1 = [r5[c]["s1"] for c in range(NCORES)]

    r6 = run_ffn1(inp, xn1T, x1, s1)
    x2 = np.concatenate([r6[c]["x2"] for c in range(NCORES)], 0)
    return x2.reshape(B, N, C).astype(np.float32)


def run_ln1(x_full):
    nc = build_ln1()
    xf = np.ascontiguousarray(x_full.reshape(B * N, C))
    ident = _ident_np()
    in_maps = [{"x": xf[c * TLOC:(c + 1) * TLOC], "ident": ident}
               for c in range(NCORES)]
    res = run_bass_kernel_spmd(nc, in_maps, core_ids=list(range(NCORES)))
    return res.results





# version 13
# speedup vs baseline: 1.0692x; 1.0692x over previous
import contextlib
import numpy as np
import ml_dtypes
import concourse.bass as bass
import concourse.mybir as mybir
from concourse.bass_utils import run_bass_kernel_spmd

F32 = mybir.dt.float32
BF16 = mybir.dt.bfloat16
AF = mybir.ActivationFunctionType
AX = mybir.AxisListType
OP = mybir.AluOpType
bf16 = ml_dtypes.bfloat16

NCORES = 8
B, N, C = 2, 8192, 1024
TLOC = 2048
NT = TLOC // 128
EPS = 1e-5


class Sems:

    def __init__(self, nc, stack, prefix=""):
        self.nc, self.stack = nc, stack
        self.prefix = prefix
        self.h = {}
        self.n = {}

    def new(self, *names):
        for nm in names:
            self.h[nm] = self.stack.enter_context(
                self.nc.semaphore(self.prefix + nm))
            self.n[nm] = 0

    def inc(self, inst, nm, k=1):
        inst.then_inc(self.h[nm], k)
        self.n[nm] += k
        return self.n[nm]

    def wait(self, eng, nm, val):
        assert 0 <= val <= self.n[nm], f"wait on future {nm}: {val} > {self.n[nm]}"
        if val > 0:
            eng.wait_ge(self.h[nm], val)

    def waitf(self, eng, nm, val):
        if val > 0:
            eng.wait_ge(self.h[nm], val)
            m = getattr(self, "maxw", None)
            if m is None:
                m = self.maxw = {}
            m[nm] = max(m.get(nm, 0), val)

    def check_futures(self):
        for nm, val in getattr(self, "maxw", {}).items():
            assert val <= self.n[nm], f"unmet future wait {nm}: {val} > {self.n[nm]}"

    def chain(self, eng, inst, nm):
        inst.then_inc(self.h[nm], 1)
        self.n[nm] += 1
        eng.wait_ge(self.h[nm], self.n[nm])


def build_ln1():
    nc = bass.Bass("TRN2", target_bir_lowering=False)
    x = nc.dram_tensor("x", [TLOC, C], F32, kind="ExternalInput").ap()
    ident = nc.dram_tensor("ident", [128, 128], BF16, kind="ExternalInput").ap()
    xn0 = nc.dram_tensor("xn0", [TLOC, C], F32, kind="ExternalOutput").ap()
    xn0T = nc.dram_tensor("xn0T", [C, TLOC], BF16, kind="ExternalOutput").ap()
    xlnbT = nc.dram_tensor("xlnbT", [C, TLOC], BF16, kind="ExternalOutput").ap()

    with contextlib.ExitStack() as stack:
        E = stack.enter_context
        sx = [E(nc.sbuf_tensor(f"sx{j}", [128, C], F32)) for j in range(2)]
        sxc = E(nc.sbuf_tensor("sxc", [128, C], F32))
        sxc2 = E(nc.sbuf_tensor("sxc2", [128, C], F32))
        scrap = E(nc.sbuf_tensor("scrap", [128, C], F32))
        sxn0 = [E(nc.sbuf_tensor(f"sxn0_{j}", [128, C], F32)) for j in range(2)]
        sxn0b = E(nc.sbuf_tensor("sxn0b", [128, C], BF16))
        sxlnb = E(nc.sbuf_tensor("sxlnb", [128, C], BF16))
        sident = E(nc.sbuf_tensor("sident", [128, 128], BF16))
        st1 = E(nc.sbuf_tensor("st1", [128, 1], F32))
        smu = E(nc.sbuf_tensor("smu", [128, 1], F32))
        sss = E(nc.sbuf_tensor("sss", [128, 1], F32))
        svar = E(nc.sbuf_tensor("svar", [128, 1], F32))
        slog = E(nc.sbuf_tensor("slog", [128, 1], F32))
        srstd = E(nc.sbuf_tensor("srstd", [128, 1], F32))
        smu2 = E(nc.sbuf_tensor("smu2", [128, 1], F32))
        sss2 = E(nc.sbuf_tensor("sss2", [128, 1], F32))
        svar2 = E(nc.sbuf_tensor("svar2", [128, 1], F32))
        srstd2 = E(nc.sbuf_tensor("srstd2", [128, 1], F32))
        sT0 = [E(nc.sbuf_tensor(f"sT0_{k}", [128, TLOC], BF16)) for k in range(8)]
        sT1 = [E(nc.sbuf_tensor(f"sT1_{k}", [128, TLOC], BF16)) for k in range(8)]
        pt = [E(nc.psum_tensor(f"pt{j}", [128, 512], BF16)) for j in range(4)]

        S = Sems(nc, stack)
        S.new("ld", "v", "a", "t", "st", "vs", "as_")
        block = E(nc.Block())

        @block.sync
        def _(sync):
            sync.dma_start(out=sident[:], in_=ident[:, :]).then_inc(S.h["ld"], 16)
            for i in range(NT):
                if i >= 2:
                    S.waitf(sync, "v", 6 * (i - 2) + 1)
                sync.dma_start(out=sx[i % 2][:], in_=x[i * 128:(i + 1) * 128, :]
                               ).then_inc(S.h["ld"], 16)
        @block.gpsimd
        def _(g):
            for i in range(NT):
                S.waitf(g, "v", 6 * i + 3)
                g.dma_start(out=xn0[i * 128:(i + 1) * 128, :], in_=sxn0[i % 2][:]
                            ).then_inc(S.h["st"], 16)
            S.waitf(g, "v", 6 * NT)
            for k in range(8):
                g.dma_start(out=xn0T[k * 128:(k + 1) * 128, :], in_=sT0[k][:]
                            ).then_inc(S.h["st"], 16)
                g.dma_start(out=xlnbT[k * 128:(k + 1) * 128, :], in_=sT1[k][:]
                            ).then_inc(S.h["st"], 16)
            g.wait_ge(S.h["st"], 16 * (NT + 16))

        @block.vector
        def _(v):
            for i in range(NT):
                xt = sx[i % 2]
                S.waitf(v, "ld", 16 * (i + 2))
                if i >= 2:
                    S.waitf(v, "st", 16 * (i - 1))
                S.chain(v, v.reduce_sum(st1[:], xt[:], axis=AX.X), "vs")
                S.chain(v, v.tensor_scalar_mul(smu[:], st1[:], 1.0 / C), "vs")
                S.inc(v.tensor_single_scalar(sxc[:], xt[:], smu[:, 0:1], OP.subtract),
                      "v")
                S.waitf(v, "a", 4 * i + 1)
                S.inc(v.tensor_scalar(svar[:], sss[:], 1.0 / C, EPS, OP.mult, OP.add),
                      "v")
                S.waitf(v, "a", 4 * i + 2)
                v.tensor_single_scalar(sxn0[i % 2][:], sxc[:], srstd[:, 0:1], OP.mult)
                v.tensor_copy(sxn0b[:], sxn0[i % 2][:])
                S.chain(v, v.reduce_sum(st1[:], sxn0[i % 2][:], axis=AX.X), "vs")
                S.chain(v, v.tensor_scalar_mul(smu2[:], st1[:], 1.0 / C), "vs")
                S.inc(v.tensor_single_scalar(sxc2[:], sxn0[i % 2][:], smu2[:, 0:1],
                                             OP.subtract), "v")
                S.waitf(v, "a", 4 * i + 3)
                S.inc(v.tensor_scalar(svar2[:], sss2[:], 1.0 / C, EPS, OP.mult, OP.add),
                      "v")
                S.waitf(v, "a", 4 * i + 4)
                S.inc(v.tensor_single_scalar(sxlnb[:], sxc2[:], srstd2[:, 0:1], OP.mult),
                      "v")
                for g in range(4):
                    S.waitf(v, "t", 4 * i + g + 1)
                    tgt = sT0 if g < 2 else sT1
                    for j in range(4):
                        k = (g % 2) * 4 + j
                        v.tensor_copy(tgt[k][:, i * 128:(i + 1) * 128],
                                      pt[g][:, j * 128:(j + 1) * 128])
                S.inc(v.engine_nop(), "v")

        @block.scalar
        def _(a):
            for i in range(NT):
                S.waitf(a, "v", 6 * i + 1)
                S.inc(a.activation(scrap[:], sxc[:], AF.Square, accum_out=sss[:]),
                      "a")
                S.waitf(a, "v", 6 * i + 2)
                S.chain(a, a.activation(slog[:], svar[:], AF.Ln), "as_")
                S.inc(a.activation(srstd[:], slog[:], AF.Exp, scale=-0.5), "a")
                S.waitf(a, "v", 6 * i + 3)
                S.inc(a.activation(scrap[:], sxc2[:], AF.Square, accum_out=sss2[:]),
                      "a")
                S.waitf(a, "v", 6 * i + 4)
                S.chain(a, a.activation(slog[:], svar2[:], AF.Ln), "as_")
                S.inc(a.activation(srstd2[:], slog[:], AF.Exp, scale=-0.5), "a")

        @block.tensor
        def _(t):
            for i in range(NT):
                S.waitf(t, "ld", 16)
                if i >= 1:
                    S.waitf(t, "v", 6 * (i - 1) + 6)
                S.waitf(t, "v", 6 * i + 5)
                for g in range(4):
                    src = sxn0b if g < 2 else sxlnb
                    ins = None
                    for j in range(4):
                        k = (g % 2) * 4 + j
                        ins = t.transpose(pt[g][:, j * 128:(j + 1) * 128],
                                          src[:, k * 128:(k + 1) * 128], sident[:])
                    S.inc(ins, "t")

    return nc


ATTN0_INPUTS = ["xnT", "qkvwT", "qn_w", "qn_b", "kn_w", "kn_b", "projwT",
                "proj_b", "identb", "identf", "onesf"]


def attn0_drams(nc, dbg=None, kind="ExternalInput", okind="ExternalOutput"):
    H, D_ = 16, 64
    NB = N
    d = {}
    if kind:
        d["xnT"] = nc.dram_tensor("xnT", [C, NB], BF16, kind=kind).ap()
    d["qkvwT"] = nc.dram_tensor("qkvwT", [C, 3 * C], F32, kind="ExternalInput").ap()
    d["qn_w"] = nc.dram_tensor("qn_w", [1, D_], F32, kind="ExternalInput").ap()
    d["qn_b"] = nc.dram_tensor("qn_b", [1, D_], F32, kind="ExternalInput").ap()
    d["kn_w"] = nc.dram_tensor("kn_w", [1, D_], F32, kind="ExternalInput").ap()
    d["kn_b"] = nc.dram_tensor("kn_b", [1, D_], F32, kind="ExternalInput").ap()
    d["projwT"] = nc.dram_tensor("projwT", [C, C], F32, kind="ExternalInput").ap()
    d["proj_b"] = nc.dram_tensor("proj_b", [1, C], F32, kind="ExternalInput").ap()
    if okind:
        d["s0"] = nc.dram_tensor("s0", [TLOC, C], F32, kind=okind).ap()
    if dbg:
        d["dbg_k"] = nc.dram_tensor("dbg_k", [4 * 64, NB], BF16, kind="ExternalOutput").ap()
        d["dbg_q"] = nc.dram_tensor("dbg_q", [4 * 64, TLOC], BF16, kind="ExternalOutput").ap()
        d["dbg_v"] = nc.dram_tensor("dbg_v", [128, NB // 128 * 4 * 65], BF16,
                                    kind="ExternalOutput").ap()
        d["dbg_m"] = nc.dram_tensor("dbg_m", [1, 4 * TLOC], BF16, kind="ExternalOutput").ap()
        d["dbg_ot"] = nc.dram_tensor("dbg_ot", [128, 2 * TLOC], BF16,
                                     kind="ExternalOutput").ap()
    return d


def emit_attn0(nc, block, S, stack, DR, dbg=None):
    H, D = 16, 64
    SCALE = D ** -0.5
    NB = N
    NJ = NB // 128
    NI = TLOC // 128
    NG, HPG = 4, 4
    xnT, qkvwT = DR["xnT"], DR["qkvwT"]
    qn_w, qn_b, kn_w, kn_b = DR["qn_w"], DR["qn_b"], DR["kn_w"], DR["kn_b"]
    projwT, proj_b = DR["projwT"], DR["proj_b"]
    identb, identf, onesf = DR["identb"], DR["identf"], DR["onesf"]
    s0 = DR["s0"]
    if dbg:
        dbg_k, dbg_q, dbg_v = DR["dbg_k"], DR["dbg_q"], DR["dbg_v"]
        dbg_m, dbg_ot = DR["dbg_m"], DR["dbg_ot"]
    if True:
        E = stack.enter_context

        sxn = [E(nc.sbuf_tensor(f"sxn{j}", [128, 1024], BF16)) for j in range(2)]
        sqkvw = E(nc.sbuf_tensor("sqkvw", [128, 8192], BF16))
        skh = [E(nc.sbuf_tensor(f"skh{i}", [128, 256], BF16)) for i in range(2)]
        sqh = [E(nc.sbuf_tensor(f"sqh{i}", [128, 256], BF16)) for i in range(2)]
        sxcL = E(nc.sbuf_tensor("sxcL", [128, 256], F32))
        scrapL = E(nc.sbuf_tensor("scrapL", [128, 256], F32))
        swfL = E(nc.sbuf_tensor("swfL", [128, 256], F32))
        ssum4 = E(nc.sbuf_tensor("ssum4", [128, 4], F32))
        smu4 = E(nc.sbuf_tensor("smu4", [128, 4], F32))
        sss4 = E(nc.sbuf_tensor("sss4", [128, 4], F32))
        svar4 = E(nc.sbuf_tensor("svar4", [128, 4], F32))
        slog4 = E(nc.sbuf_tensor("slog4", [128, 4], F32))
        srstd4 = E(nc.sbuf_tensor("srstd4", [128, 4], F32))
        scm = E(nc.sbuf_tensor("scm", [64, 1], F32))
        scs = E(nc.sbuf_tensor("scs", [64, 1], F32))
        swq = E(nc.sbuf_tensor("swq", [128, D], F32))
        sbq = E(nc.sbuf_tensor("sbq", [128, D], F32))
        swk = E(nc.sbuf_tensor("swk", [128, D], F32))
        sbk = E(nc.sbuf_tensor("sbk", [128, D], F32))
        sKB = [E(nc.sbuf_tensor(f"sKB{h}", [128, NB], BF16)) for h in range(HPG)]
        sQB = [E(nc.sbuf_tensor(f"sQB{h}", [128, TLOC], BF16)) for h in range(HPG)]
        sV = E(nc.sbuf_tensor("sV", [128, NJ * HPG * 65], BF16))
        srm = E(nc.sbuf_tensor("srm", [128, 16], F32))
        sm = E(nc.sbuf_tensor("sm", [128, 1], F32))
        sm0 = E(nc.sbuf_tensor("sm0", [1, HPG * TLOC], BF16))
        sE = [E(nc.sbuf_tensor(f"sE{j}", [128, 512], BF16)) for j in range(3)]
        sOF = E(nc.sbuf_tensor("sOF", [65, 512], F32))
        sden = E(nc.sbuf_tensor("sden", [65, 512], F32))
        sOnes64 = E(nc.sbuf_tensor("sOnes64", [65, 64], F32))
        sOT = E(nc.sbuf_tensor("sOT", [128, 8 * TLOC], BF16))
        sbrep = E(nc.sbuf_tensor("sbrep", [128, C], F32))
        ss0 = [E(nc.sbuf_tensor(f"ss0_{j}", [128, C], F32)) for j in range(2)]
        sidb = E(nc.sbuf_tensor("sidb", [128, 128], BF16))
        sidf = E(nc.sbuf_tensor("sidf", [128, 128], F32))
        sonesf = E(nc.sbuf_tensor("sonesf", [1, 128], F32))
        sraw = E(nc.sbuf_tensor("sraw", [1, 128], F32))

        pbank = [E(nc.psum_tensor(f"pbank{j}", [128, 512], F32)) for j in range(3)]
        pS2 = E(nc.psum_tensor("pS2", [128, 512], F32))
        pO = E(nc.psum_tensor("pO", [65, 512], F32))
        pTb = [E(nc.psum_tensor(f"pTb{j}", [128, 512], BF16)) for j in range(2)]
        pRep = E(nc.psum_tensor("pRep", [128, 512], F32))
        pBr = pRep

        S.new("ld", "ld0", "ld1", "lds", "gld", "st", "tmm", "ttr", "tS", "tPV", "tRep",
              "vst", "ast", "vfree", "vkq", "vtb", "vmA", "vOF", "vdma", "vrep",
              "vs", "as_", "aexp")

        @block.sync
        def _(sync):
            S.inc(sync.dma_start(out=sidb[:], in_=identb[:, :]), "lds", 16)
            S.inc(sync.dma_start(out=sidf[:], in_=identf[:, :]), "lds", 16)
            S.inc(sync.dma_start(out=sonesf[:], in_=onesf[:, :]), "lds", 16)

        @block.vector
        def _(v):
            v.memset(sOnes64[64:65, :], 1.0)
            v.memset(sm0[:], 0.0)
            v.memset(sV[:].rearrange("p (j c) -> p j c", c=65)[:, :, 64:65], 1.0)
            for hh in range(HPG):
                S.chain(v, v.memset(sKB[hh][64:128, :], 0.0), "vs")
                v.memset(sQB[hh][64:128, :], 0.0)
                S.inc(v.memset(sKB[hh][64:65, :], 1.0), "vrep")
        reps = [(qn_w, swq, SCALE), (qn_b, sbq, SCALE),
                (kn_w, swk, 1.0), (kn_b, sbk, 1.0)]
        for idx, (src, dst, scale) in enumerate(reps):
            @block.sync
            def _(sync, src=src, idx=idx):
                if idx > 0:
                    S.wait(sync, "tRep", idx)
                S.inc(sync.dma_start(out=sraw[0:1, 0:D], in_=src[0:1, :]
                               ), "lds", 16)

            @block.tensor
            def _(t, idx=idx):
                S.wait(t, "lds", S.n["lds"])
                if idx > 0:
                    S.wait(t, "vrep", S.n["vrep"])
                S.inc(t.matmul(pBr[:, 0:D], sonesf[:, :], sraw[0:1, 0:D],
                         start=True, stop=True), "tRep", 1)

            @block.vector
            def _(v, dst=dst, scale=scale):
                S.wait(v, "tRep", S.n["tRep"])
                S.inc(v.tensor_scalar_mul(dst[:], pBr[:, 0:D], scale), "vrep")

        for kk in range(8):
            @block.sync
            def _(sync, kk=kk):
                S.wait(sync, "tRep", S.n["tRep"])
                S.inc(sync.dma_start(out=sraw[0:1, :],
                               in_=proj_b[0:1, kk * 128:(kk + 1) * 128]
                               ), "lds", 16)

            @block.tensor
            def _(t, kk=kk):
                S.wait(t, "lds", S.n["lds"])
                S.wait(t, "vrep", S.n["vrep"])
                S.inc(t.matmul(pBr[:, 0:128], sonesf[:, :], sraw[0:1, :],
                         start=True, stop=True), "tRep", 1)

            @block.vector
            def _(v, kk=kk):
                S.wait(v, "tRep", S.n["tRep"])
                S.inc(v.tensor_copy(sbrep[:, kk * 128:(kk + 1) * 128],
                                    pBr[:, 0:128]), "vrep")

        def segln_pre(v, psum_ap):
            a3 = psum_ap.rearrange("p (s d) -> p s d", s=4)
            S.chain(v, v.reduce_sum(ssum4[:], a3, axis=AX.X), "vs")
            S.chain(v, v.tensor_scalar_mul(smu4[:], ssum4[:], 1.0 / D), "vs")
            mu_bc = bass.AP(smu4.ap().tensor, 0, [[4, 128], [1, 4], [0, D]])
            S.chain(v, v.tensor_tensor(sxcL[:].rearrange("p (s d) -> p s d", s=4),
                                       a3, mu_bc, OP.subtract), "vs")
            psum_free = ("vs", S.n["vs"])
            S.chain(v, v.tensor_mul(scrapL[:], sxcL[:], sxcL[:]), "vs")
            S.chain(v, v.reduce_sum(
                sss4[:], scrapL[:].rearrange("p (s d) -> p s d", s=4),
                axis=AX.X), "vs")
            S.inc(v.tensor_scalar(svar4[:], sss4[:], 1.0 / D, EPS,
                                  OP.mult, OP.add), "vst")
            return psum_free

        def segln_act(a):
            S.wait(a, "vst", S.n["vst"])
            S.chain(a, a.activation(slog4[:], svar4[:], AF.Ln), "as_")
            S.inc(a.activation(srstd4[:], slog4[:], AF.Exp, scale=-0.5), "ast")

        def segln_post(v, wrep, brep, out_bf):
            S.wait(v, "ast", S.n["ast"])
            rstd_bc = bass.AP(srstd4.ap().tensor, 0, [[4, 128], [1, 4], [0, D]])
            w_bc = bass.AP(wrep.ap().tensor, 0, [[D, 128], [0, 4], [1, D]])
            S.chain(v, v.tensor_tensor(swfL[:].rearrange("p (s d) -> p s d", s=4),
                                       rstd_bc, w_bc, OP.mult), "vs")
            S.chain(v, v.tensor_mul(scrapL[:], sxcL[:], swfL[:]), "vs")
            b_bc = bass.AP(brep.ap().tensor, 0, [[D, 128], [0, 4], [1, D]])
            return v.tensor_tensor(out_bf.rearrange("p (s d) -> p s d", s=4),
                                   scrapL[:].rearrange("p (s d) -> p s d", s=4),
                                   b_bc, OP.add)

        NG_run = 1 if dbg else NG
        for g in range(NG_run):
            tmm_base = S.n["tmm"]
            tpv_base_g = S.n["tPV"]

            @block.gpsimd
            def _(gp, g=g):
                S.wait(gp, "tmm", tmm_base)
                S.wait(gp, "gld", S.n["gld"])
                for part in range(3):
                    cols = qkvwT[:, part * C + g * 256: part * C + g * 256 + 256]
                    S.inc(gp.dma_start(
                        out=sqkvw[:, 0:6144].rearrange("p (k r) -> p k r", k=8
                                                       )[:, :, part * 256:(part + 1) * 256],
                        in_=cols.rearrange("(k p) r -> p k r", p=128)
                    ), "gld", 16)
            ld_w_done = S.n["gld"]

            tmm_all, tmm_kv = {}, {}
            vfree_kv, vfree_q = {}, {}
            vkq_t, vtb_t, ttr_t = {}, {}, {}
            ld_t = {}

            for j in range(NJ + 1):
                local = j < NI
                if j < NJ:
                    @block.sync
                    def _(sync, j=j, g=g):
                        if j >= 2:
                            S.wait(sync, "tmm", tmm_all[j - 2])
                        elif g > 0:
                            S.wait(sync, "tmm", tmm_base)
                        S.inc(sync.dma_start(
                            out=sxn[j % 2][:].rearrange("p (k n) -> p k n", k=8),
                            in_=xnT[:, j * 128:(j + 1) * 128].rearrange(
                                "(k p) n -> p k n", p=128)
                        ), "ld" + str(j % 2), 16)
                        ld_t[j] = S.n["ld" + str(j % 2)]

                    @block.tensor
                    def _(t, j=j, g=g, local=local):
                        S.wait(t, "ld" + str(j % 2), ld_t[j])
                        S.wait(t, "gld", ld_w_done)
                        if j >= 2:
                            S.wait(t, "vfree", vfree_kv[j - 2])
                        elif g > 0:
                            S.wait(t, "aexp", aexp_base_next)
                        if local and j >= 1:
                            S.wait(t, vfree_q[j - 1][0], vfree_q[j - 1][1])
                        last = None
                        for k in range(8):
                            lhs = sxn[j % 2][:, k * 128:(k + 1) * 128]
                            last = t.matmul(pbank[j % 2][:], lhs,
                                            sqkvw[:, k * 768 + 256:k * 768 + 768],
                                            start=(k == 0), stop=(k == 7))
                            if local:
                                last = t.matmul(pbank[2][:, 0:256], lhs,
                                                sqkvw[:, k * 768:k * 768 + 256],
                                                start=(k == 0), stop=(k == 7))
                        S.inc(last, "tmm")
                        tmm_kv[j] = tmm_all[j] = S.n["tmm"]

                if j >= 1:
                    @block.tensor
                    def _(t, j=j):
                        jj = j - 1
                        S.wait(t, "vkq", vkq_t[jj])
                        if jj >= 2:
                            S.wait(t, "vtb", vtb_t[jj - 2])
                        pT = pTb[jj % 2]
                        last = None
                        for hh in range(HPG):
                            last = t.transpose(pT[0:64, hh * 128:(hh + 1) * 128],
                                               skh[jj % 2][:, hh * 64:(hh + 1) * 64],
                                               sidb[:])
                        if jj < NI:
                            for hh in range(HPG):
                                last = t.transpose(
                                    pT[64:128, hh * 128:(hh + 1) * 128],
                                    sqh[jj % 2][:, hh * 64:(hh + 1) * 64],
                                    sidb[:], tile_position=(0, 64))
                        S.inc(last, "ttr")
                        ttr_t[jj] = S.n["ttr"]

                if j < NJ:
                    @block.vector
                    def _(v, j=j, g=g, local=local):
                        S.wait(v, "tmm", tmm_kv[j])
                        if j == 0 and g > 0:
                            S.wait(v, "tPV", tpv_base_g)
                        if j >= 2:
                            S.wait(v, "ttr", ttr_t[j - 2])
                        segln_pre(v, pbank[j % 2][:, 0:256])
                        last = None
                        for hh in range(HPG):
                            last = v.tensor_copy(
                                sV[:, (j * HPG + hh) * 65:(j * HPG + hh) * 65 + 64],
                                pbank[j % 2][:, 256 + hh * 64:256 + (hh + 1) * 64])
                        S.inc(last, "vfree")
                        vfree_kv[j] = S.n["vfree"]

                    @block.scalar
                    def _(a, j=j):
                        segln_act(a)

                    @block.vector
                    def _(v, j=j):
                        S.inc(segln_post(v, swk, sbk, skh[j % 2][:]), "vkq")

                    if local:
                        @block.vector
                        def _(v, j=j):
                            vfree_q[j] = segln_pre(v, pbank[2][:, 0:256])

                        @block.scalar
                        def _(a, j=j):
                            segln_act(a)

                        @block.vector
                        def _(v, j=j):
                            S.inc(segln_post(v, swq, sbq, sqh[j % 2][:]), "vkq")
                    vkq_t[j] = S.n["vkq"]

                if j >= 1:
                    @block.vector
                    def _(v, j=j):
                        jj = j - 1
                        S.wait(v, "ttr", ttr_t[jj])
                        pT = pTb[jj % 2]
                        for hh in range(HPG):
                            last = v.tensor_copy(sKB[hh][0:64, jj * 128:(jj + 1) * 128],
                                                 pT[0:64, hh * 128:(hh + 1) * 128])
                        if jj < NI:
                            for hh in range(HPG):
                                last = v.tensor_copy(
                                    sQB[hh][0:64, jj * 128:(jj + 1) * 128],
                                    pT[64:128, hh * 128:(hh + 1) * 128])
                        S.inc(last, "vtb")
                        vtb_t[jj] = S.n["vtb"]

            @block.vector
            def _(v, g=g):
                S.wait(v, "vtb", S.n["vtb"])
                for hh in range(HPG):
                    S.chain(v, v.reduce_sum(scm[:], sKB[hh][0:64, :], axis=AX.X),
                            "vs")
                    S.chain(v, v.tensor_scalar_mul(scs[:], scm[:], -1.0 / NB), "vs")
                    S.inc(v.tensor_single_scalar(sKB[hh][0:64, :], sKB[hh][0:64, :],
                                                 scs[:, 0:1], OP.add), "vkq")
            vcenter_done = S.n["vkq"]

            if dbg == "p0":
                @block.gpsimd
                def _(gp):
                    S.wait(gp, "vkq", S.n["vkq"])
                    S.wait(gp, "vtb", S.n["vtb"])
                    for hh in range(HPG):
                        S.inc(gp.dma_start(out=dbg_k[hh * 64:(hh + 1) * 64, :],
                                           in_=sKB[hh][0:64, :]), "st", 16)
                        S.inc(gp.dma_start(out=dbg_q[hh * 64:(hh + 1) * 64, :],
                                           in_=sQB[hh][0:64, :]), "st", 16)
                    S.inc(gp.dma_start(out=dbg_v[:, :], in_=sV[:]), "st", 16)
                    gp.wait_ge(S.h["st"], S.n["st"])
                S.check_futures()
                return

            vfA_base = S.n["vfree"]
            vdma_base_g = S.n["vdma"]
            tSa = {}
            tSb = {}
            tRepA = {}
            pSB = [pbank[2], pS2]

            def emit_quarter(hh, ph, g):
                haveA = False
                haveB = ph >= 1 or hh > 0
                if haveB:
                    bhh, big = (hh, ph - 1) if ph >= 1 else (hh - 1, 3)
                    combo = bhh * 4 + big
                    pv_base = S.n["tPV"]
                    vOF_base = S.n["vOF"]
                    ax_base = S.n["aexp"]
                    vdma_q = S.n["vdma"]
                vmA_q = S.n["vmA"]

                @block.tensor
                def _(t, hh=hh, ph=ph, g=g):
                    if haveB:
                        S.wait(t, "vdma", vdma_q)
                        if combo > 0 or g > 0:
                            S.wait(t, "vOF", vOF_base)
                        if combo == 0:
                            S.wait(t, "vfree", vfA_base)
                            S.wait(t, "vkq", vcenter_done)
                    if haveA and hh == 0 and ph == 0:
                        S.wait(t, "vkq", vcenter_done)
                    for k in range(64):
                        if haveA:
                            it = ph * 4 + k // 16
                            jg = k % 16
                            idx = (hh * NI + it) * 16 + jg
                            if idx >= 2:
                                S.waitf(t, "vfree", vfA_base + idx - 1)
                            else:
                                S.wait(t, "vfree", vfA_base)
                            S.inc(t.matmul(
                                pbank[jg % 2][:],
                                sQB[hh][0:128, it * 128:(it + 1) * 128],
                                sKB[hh][0:128, jg * 512:(jg + 1) * 512],
                                start=True, stop=True), "tS")
                            tSa[idx] = S.n["tS"]
                            if k % 16 == 15:
                                S.waitf(t, "vmA", vmA_q + (k // 16) * 2 + 1)
                                S.wait(t, "vOF", S.n["vOF"])
                                S.inc(t.transpose(pRep[0:1, 0:128], sm[:],
                                                  sidf[:]), "tRep")
                                tRepA[(hh, it)] = S.n["tRep"]
                        if haveB:
                            j = k
                            S.waitf(t, "aexp", ax_base + max(j - 1, 0))
                            S.inc(t.matmul(
                                pSB[j % 2][:],
                                sKB[bhh][0:128, j * 128:(j + 1) * 128],
                                sQB[bhh][0:128, big * 512:(big + 1) * 512],
                                start=True, stop=True), "tS")
                            tSb[(combo, j)] = S.n["tS"]
                            if j >= 2:
                                S.inc(t.matmul(
                                    pO[:],
                                    sV[:, ((j - 2) * HPG + bhh) * 65:
                                       ((j - 2) * HPG + bhh) * 65 + 65],
                                    sE[(j - 2) % 3][:],
                                    start=(j == 2), stop=False), "tPV", 1)
                    if haveB:
                        for jt in (NJ - 2, NJ - 1):
                            S.waitf(t, "aexp", ax_base + jt + 1)
                            S.inc(t.matmul(
                                pO[:],
                                sV[:, (jt * HPG + bhh) * 65:
                                   (jt * HPG + bhh) * 65 + 65],
                                sE[jt % 3][:],
                                start=False, stop=(jt == NJ - 1)), "tPV", 1)

                if haveA:
                    @block.vector
                    def _(v, hh=hh, ph=ph):
                        if hh == 0 and ph == 0:
                            S.wait(v, "vdma", vdma_base_g)
                        for tt in range(4):
                            it = ph * 4 + tt
                            for jg in range(16):
                                idx = (hh * NI + it) * 16 + jg
                                S.wait(v, "tS", tSa[idx])
                                S.inc(v.reduce_max(srm[:, jg:jg + 1],
                                                   pbank[jg % 2][:],
                                                   axis=AX.X), "vfree")
                            S.wait(v, "vfree", S.n["vfree"])
                            S.chain(v, v.reduce_max(sm[:], srm[:], axis=AX.X),
                                    "vs")
                            S.inc(v.engine_nop(), "vmA")
                            S.wait(v, "tRep", tRepA[(hh, it)])
                            sm0sl = sm0[0:1, hh * TLOC + it * 128:
                                        hh * TLOC + (it + 1) * 128]
                            S.inc(v.tensor_tensor(
                                sm0sl, sm0sl, pRep[0:1, 0:128],
                                OP.subtract), "vmA")

                    @block.gpsimd
                    def _(gp, hh=hh, ph=ph):
                        S.wait(gp, "vmA", S.n["vmA"])
                        S.wait(gp, "vdma", S.n["vdma"])
                        S.inc(gp.dma_start(
                            out=sQB[hh][64:65, ph * 512:(ph + 1) * 512],
                            in_=sm0[0:1, hh * TLOC + ph * 512:
                                    hh * TLOC + (ph + 1) * 512]), "vdma", 16)

                if haveB:
                    @block.scalar
                    def _(a, pv_base=pv_base, combo=combo):
                        for j in range(NJ):
                            S.wait(a, "tS", tSb[(combo, j)])
                            if j >= 3:
                                S.wait(a, "tPV", pv_base + j - 2)
                            else:
                                S.wait(a, "tPV", pv_base)
                            S.inc(a.activation(sE[j % 3][:], pSB[j % 2][:],
                                               AF.Exp), "aexp")

                    @block.vector
                    def _(v):
                        S.wait(v, "tPV", S.n["tPV"])
                        S.wait(v, "tRep", S.n["tRep"])
                        S.inc(v.tensor_copy(sOF[:], pO[:]), "vOF")

                    @block.scalar
                    def _(a):
                        S.wait(a, "vOF", S.n["vOF"])
                        S.chain(a, a.activation(sden[64:65, :], sOF[64:65, :],
                                                AF.Ln), "as_")
                        S.inc(a.activation(sOF[64:65, :], sden[64:65, :],
                                           AF.Exp, scale=-1.0), "ast")

                    @block.tensor
                    def _(t):
                        S.wait(t, "ast", S.n["ast"])
                        S.wait(t, "vmA", S.n["vmA"])
                        S.inc(t.matmul(pRep[0:64, :], sOnes64[64:65, :],
                                       sOF[64:65, :], start=True, stop=True),
                              "tRep")

                    @block.vector
                    def _(v, combo=combo, g=g):
                        S.wait(v, "tRep", S.n["tRep"])
                        h_abs = g * HPG + combo // 4
                        ig2 = combo % 4
                        p, r0 = h_abs // 2, (h_abs % 2) * 64
                        S.inc(v.tensor_tensor(
                            sOT[r0:r0 + 64,
                                p * TLOC + ig2 * 512:p * TLOC + (ig2 + 1) * 512],
                            sOF[0:64, :], pRep[0:64, :], OP.mult), "vOF")

            for hh in range(HPG):
                for ph in range(5):
                    if ph == 4 and hh < HPG - 1:
                        break
                    if hh == 0 and ph == 0:
                        continue
                    emit_quarter(hh, ph, g)
            aexp_base_next = S.n["aexp"]
            if dbg == "passB":
                @block.gpsimd
                def _(gp):
                    S.wait(gp, "vOF", S.n["vOF"])
                    S.inc(gp.dma_start(out=dbg_ot[:, :], in_=sOT[:, 0:2 * TLOC]),
                          "st", 16)
                    gp.wait_ge(S.h["st"], S.n["st"])
                S.check_futures()
                return

        @block.gpsimd
        def _(gp):
            S.wait(gp, "tS", S.n["tS"])
            S.wait(gp, "gld", S.n["gld"])
            S.inc(gp.dma_start(
                out=sqkvw[:].rearrange("p (k r) -> p k r", k=8),
                in_=projwT[:, :].rearrange("(k p) r -> p k r", p=128)), "gld", 16)
        ld_pw = S.n["gld"]

        vproj = {}
        for it in range(NI):
            @block.tensor
            def _(t, it=it):
                S.wait(t, "gld", ld_pw)
                S.wait(t, "vOF", S.n["vOF"])
                S.wait(t, "vrep", S.n["vrep"])
                if it >= 1:
                    S.wait(t, "vfree", vproj[it - 1])
                for half in range(2):
                    last = None
                    for p in range(8):
                        last = t.matmul(
                            pbank[half][:],
                            sOT[:, p * TLOC + it * 128:p * TLOC + (it + 1) * 128],
                            sqkvw[:, p * 1024 + half * 512:
                                  p * 1024 + half * 512 + 512],
                            start=(p == 0), stop=(p == 7))
                    S.inc(last, "tS")

            tbase = S.n["tS"]

            @block.vector
            def _(v, it=it, tbase=tbase):
                if it >= 2:
                    S.wait(v, "st", 16 * (it - 1))
                for half in range(2):
                    S.wait(v, "tS", tbase - 1 + half)
                    v.tensor_tensor(ss0[it % 2][:, half * 512:(half + 1) * 512],
                                    pbank[half][:],
                                    sbrep[:, half * 512:(half + 1) * 512], OP.add)
                S.inc(v.engine_nop(), "vfree")
                vproj[it] = S.n["vfree"]

            @block.gpsimd
            def _(gp, it=it):
                S.wait(gp, "vfree", vproj[it])
                S.wait(gp, "st", S.n["st"])
                S.inc(gp.dma_start(out=s0[it * 128:(it + 1) * 128, :], in_=ss0[it % 2][:]
                             ), "st", 16)

        @block.gpsimd
        def _(gp):
            gp.wait_ge(S.h["st"], 16 * NI)

        S.check_futures()




def build_attn0(dbg=None):
    nc = bass.Bass("TRN2", target_bir_lowering=False)
    with contextlib.ExitStack() as stack:
        D = attn0_drams(nc, dbg)
        D["identb"] = nc.dram_tensor("identb", [128, 128], BF16,
                                     kind="ExternalInput").ap()
        D["identf"] = nc.dram_tensor("identf", [128, 128], F32,
                                     kind="ExternalInput").ap()
        D["onesf"] = nc.dram_tensor("onesf", [1, 128], F32,
                                    kind="ExternalInput").ap()
        S = Sems(nc, stack)
        block = stack.enter_context(nc.Block())
        emit_attn0(nc, block, S, stack, D, dbg)
    return nc

def run_attn0(xnT_by_core, inp, dbg=None):
    nc = build_attn0(dbg)
    qkvwT = np.ascontiguousarray(inp["L0_attn_qkv_w"].T)
    projwT = np.ascontiguousarray(inp["L0_attn_proj_w"].T)
    base = {
        "qkvwT": qkvwT, "projwT": projwT,
        "qn_w": inp["L0_attn_q_norm_w"].reshape(1, -1),
        "qn_b": inp["L0_attn_q_norm_b"].reshape(1, -1),
        "kn_w": inp["L0_attn_k_norm_w"].reshape(1, -1),
        "kn_b": inp["L0_attn_k_norm_b"].reshape(1, -1),
        "proj_b": inp["L0_attn_proj_b"].reshape(1, -1),
        "identb": np.eye(128, dtype=bf16),
        "identf": np.eye(128, dtype=np.float32),
        "onesf": np.ones((1, 128), dtype=np.float32),
    }
    in_maps = [dict(base, xnT=xnT_by_core[c]) for c in range(NCORES)]
    res = run_bass_kernel_spmd(nc, in_maps, core_ids=list(range(NCORES)))
    return res.results


def moe0_drams(nc, kind="ExternalInput", okind="ExternalOutput"):
    F, NE = 4096, 8
    d = {}
    if kind:
        d["xlnbT"] = nc.dram_tensor("xlnbT", [C, TLOC], BF16, kind=kind).ap()
        d["xn0"] = nc.dram_tensor("xn0", [TLOC, C], F32, kind=kind).ap()
    d["cen"] = nc.dram_tensor("cen", [NE, C], F32, kind="ExternalInput").ap()
    d["lngR"] = nc.dram_tensor("lngR", [128, NE * 8], F32, kind="ExternalInput").ap()
    d["lnbR"] = nc.dram_tensor("lnbR", [128, NE * 8], F32, kind="ExternalInput").ap()
    d["w1"] = nc.dram_tensor("w1", [NE, C, F], F32, kind="ExternalInput").ap()
    d["b1R"] = nc.dram_tensor("b1R", [128, NE * 32], F32, kind="ExternalInput").ap()
    d["w2"] = nc.dram_tensor("w2", [NE, F, C], F32, kind="ExternalInput").ap()
    d["b2"] = nc.dram_tensor("b2", [NE, C], F32, kind="ExternalInput").ap()
    if okind:
        d["g0"] = nc.dram_tensor("g0", [TLOC, C], F32, kind=okind).ap()
    return d


def emit_moe0(nc, block, S, stack, DR):
    F = 4096
    NE = 8
    xlnbT, xn0 = DR["xlnbT"], DR["xn0"]
    cen, lngR, lnbR = DR["cen"], DR["lngR"], DR["lnbR"]
    w1, b1R, w2, b2 = DR["w1"], DR["b1R"], DR["w2"], DR["b2"]
    identf, onesf, g0 = DR["identf"], DR["onesf"], DR["g0"]
    NI_ = TLOC // 128
    if True:
        E = stack.enter_context

        sW1 = E(nc.sbuf_tensor("sW1", [128, 8 * F], BF16))
        sW2b = [E(nc.sbuf_tensor(f"sW2b{j}", [128, 8 * C], BF16)) for j in range(2)]
        sHT = E(nc.sbuf_tensor("sHT", [128, 32 * 512], BF16))
        sFA = E(nc.sbuf_tensor("sFA", [128, NI_ * C], BF16))
        sXE = E(nc.sbuf_tensor("sXE", [128, 8 * 512], BF16))
        sXBf = [E(nc.sbuf_tensor("sXBf0", [128, 8 * 512], BF16))] * 2
        scen1 = E(nc.sbuf_tensor("scen1", [1, C], F32))
        sonesf = E(nc.sbuf_tensor("sonesf", [1, 128], F32))
        sLNG = E(nc.sbuf_tensor("sLNG", [128, 64], F32))
        sLNB = E(nc.sbuf_tensor("sLNB", [128, 64], F32))
        sB1 = E(nc.sbuf_tensor("sB1", [128, 256], F32))
        sB2 = E(nc.sbuf_tensor("sB2", [8, C], BF16))
        sOH = E(nc.sbuf_tensor("sOH", [128, NI_ * 8], F32))
        sOHT = E(nc.sbuf_tensor("sOHT", [8, TLOC], BF16))
        sAL = E(nc.sbuf_tensor("sAL", [128, NI_], F32))
        sAmax = E(nc.sbuf_tensor("sAmax", [128, 1], F32))
        sIdf = E(nc.sbuf_tensor("sIdf", [128, 128], F32))
        sX0 = [E(nc.sbuf_tensor(f"sX0{j}", [128, C], F32)) for j in range(2)]
        sG = [E(nc.sbuf_tensor(f"sG{j}", [128, C], F32)) for j in range(2)]
        sCR = sG[0]
        sscr = sG[1]
        pB = [E(nc.psum_tensor(f"pB{j}", [128, 512], F32)) for j in range(8)]

        S.new("lds", "gld", "gld2", "gw1", "st", "ta", "th", "ty", "tb",
              "va", "vfa", "vy", "vg", "aal", "ah", "ax", "vs", "as_")
        ah_t, xb_g, vh_t, w2l_t, ty_t = {}, {}, {}, {}, {}
        bfa_t, vg_t, ag_t = {}, {}, {}

        @block.sync
        def _(sync):
            S.inc(sync.dma_start(out=sIdf[:], in_=identf[:, :]), "lds", 16)
            S.inc(sync.dma_start(out=sLNG[:], in_=lngR[:, :]), "lds", 16)
            S.inc(sync.dma_start(out=sLNB[:], in_=lnbR[:, :]), "lds", 16)
            S.inc(sync.dma_start(out=sB1[:], in_=b1R[:, :]), "lds", 16)

        @block.gpsimd
        def _(gp):
            S.inc(gp.dma_start(out=sB2[:], in_=b2[:, :]), "gld", 16)

        @block.sync
        def _(sync):
            S.inc(sync.dma_start(out=sonesf[:], in_=onesf[:, :]), "lds", 16)
        gld_m0 = S.n["lds"]

        scenT = E(nc.sbuf_tensor("scenT", [128, 64], F32))
        sXT = [E(nc.sbuf_tensor(f"sXT{j}", [128, 128], F32)) for j in range(2)]

        @block.sync
        def _(sync):
            S.wait(sync, "lds", S.n["lds"])
            for e_ in range(NE):
                S.inc(sync.dma_start(
                    out=bass.AP(scenT.ap().tensor, e_, [[64, 128], [8, 8], [1, 1]]),
                    in_=bass.AP(cen.tensor, e_ * 1024, [[1, 128], [128, 8], [1, 1]]),
                    allow_slow_non_contiguous=True), "lds", 16)
        lds_cen = S.n["lds"]

        tr_t = {}
        ev_t = {}
        mm_t = {}
        for it in range(NI_):
            @block.sync
            def _(sync, it=it):
                S.wait(sync, "lds", S.n["lds"])
                if it >= 2:
                    S.wait(sync, "ta", tr_t[(it - 2) * 8 + 7])
                S.inc(sync.dma_start(out=sX0[it % 2][:],
                                     in_=xn0[it * 128:(it + 1) * 128, :]),
                      "lds", 16)
            lds_x = S.n["lds"]

            for kc in range(8):
                idx = it * 8 + kc

                @block.tensor
                def _(t, it=it, kc=kc, idx=idx, lds_x=lds_x):
                    if kc == 0:
                        S.wait(t, "lds", lds_x)
                    if idx >= 2:
                        S.wait(t, "va", ev_t[idx - 2])
                    S.inc(t.transpose(pB[idx % 2][:, 0:128],
                                      sX0[it % 2][:, kc * 128:(kc + 1) * 128],
                                      sIdf[:]), "ta")
                    tr_t[idx] = S.n["ta"]
                    if idx >= 1:
                        pidx = idx - 1
                        pit, pkc = divmod(pidx, 8)
                        S.wait(t, "va", ev_t[pidx])
                        S.inc(t.matmul(pB[2][:, pit * 8:(pit + 1) * 8],
                                       sXT[pidx % 2][:],
                                       scenT[:, pkc * 8:(pkc + 1) * 8],
                                       start=(pkc == 0), stop=(pkc == 7)),
                              "tb")
                        mm_t[pidx] = S.n["tb"]

                @block.vector
                def _(v, idx=idx):
                    S.wait(v, "ta", tr_t[idx])
                    if idx >= 2:
                        S.wait(v, "tb", mm_t[idx - 2])
                    S.inc(v.tensor_copy(sXT[idx % 2][:], pB[idx % 2][:, 0:128]),
                          "va")
                    ev_t[idx] = S.n["va"]

        last_idx = NI_ * 8 - 1

        @block.tensor
        def _(t):
            pit, pkc = divmod(last_idx, 8)
            S.wait(t, "va", ev_t[last_idx])
            S.inc(t.matmul(pB[2][:, pit * 8:(pit + 1) * 8],
                           sXT[last_idx % 2][:],
                           scenT[:, pkc * 8:(pkc + 1) * 8],
                           start=(pkc == 0), stop=(pkc == 7)), "tb")
            mm_t[last_idx] = S.n["tb"]

        @block.vector
        def _(v):
            S.wait(v, "tb", mm_t[last_idx])
            S.inc(v.tensor_copy(sOH[:, 0:NI_ * 8], pB[2][:, 0:NI_ * 8]), "va")

        for it in range(NI_):
            @block.vector
            def _(v, it=it):
                S.wait(v, "va", S.n["va"])
                S.wait(v, "aal", S.n["aal"])
                S.chain(v, v.reduce_max(sAmax[:], sOH[:, it * 8:(it + 1) * 8],
                                        axis=AX.X), "vs")
                S.inc(v.tensor_single_scalar(sOH[:, it * 8:(it + 1) * 8],
                                             sOH[:, it * 8:(it + 1) * 8],
                                             sAmax[:, 0:1], OP.is_ge), "va")

            @block.scalar
            def _(a, it=it):
                S.wait(a, "va", S.n["va"])
                S.inc(a.activation(sAL[:, it:it + 1], sAmax[:, 0:1], AF.Sigmoid),
                      "aal")

            @block.tensor
            def _(t, it=it):
                S.wait(t, "va", S.n["va"])
                S.inc(t.transpose(pB[3][0:8, 0:128], sOH[:, it * 8:(it + 1) * 8],
                                  sIdf[:]), "tb")

            @block.vector
            def _(v, it=it):
                S.wait(v, "tb", S.n["tb"])
                S.inc(v.tensor_copy(sOHT[0:8, it * 128:(it + 1) * 128],
                                    pB[3][0:8, 0:128]), "va")

        m0_done_va = S.n["va"]
        m0_done_ta = S.n["ta"]

        for it in range(NI_):
            @block.tensor
            def _(t, it=it):
                S.wait(t, "gld", 16)
                S.wait(t, "va", m0_done_va)
                if it >= 2:
                    S.wait(t, "vfa", bfa_t[it - 2])
                last = None
                for half in range(2):
                    last = t.matmul(pB[4 + (it * 2 + half) % 4][:],
                                    sOHT[0:8, it * 128:(it + 1) * 128],
                                    sB2[0:8, half * 512:(half + 1) * 512],
                                    start=True, stop=True)
                S.inc(last, "tb")

            @block.vector
            def _(v, it=it):
                S.wait(v, "tb", S.n["tb"])
                last = None
                for half in range(2):
                    last = v.tensor_copy(
                        sFA[:, it * C + half * 512:it * C + (half + 1) * 512],
                        pB[4 + (it * 2 + half) % 4][:])
                S.inc(last, "vfa")
                bfa_t[it] = S.n["vfa"]

        for e in range(NE):
            @block.gpsimd
            def _(gp, e=e):
                if e > 0:
                    S.wait(gp, "th", S.n["th"])
                for fh in range(2):
                    S.inc(gp.dma_start(
                        out=sW1[:].rearrange("p (k f) -> p k f", k=8
                                             )[:, :, fh * 2048:(fh + 1) * 2048],
                        in_=w1[e][:, fh * 2048:(fh + 1) * 2048].rearrange(
                            "(k p) f -> p k f", p=128)), "gw1", 16)
            gld_w1 = S.n["gw1"]
            gld_w1h = {0: gld_w1 - 16, 1: gld_w1}

            for ig in range(4):
                gidx = e * 4 + ig

                @block.sync
                def _(sync, e=e, ig=ig, gidx=gidx):
                    S.wait(sync, "lds", S.n["lds"])
                    if gidx > 0:
                        S.wait(sync, "ah", ah_t[(gidx - 1, 7)])
                    S.inc(sync.dma_start(
                        out=sXBf[gidx % 2][:].rearrange("p (k n) -> p k n", k=8),
                        in_=xlnbT[:, ig * 512:(ig + 1) * 512].rearrange(
                            "(k p) n -> p k n", p=128)), "lds", 16)
                lds_xb = S.n["lds"]

                for kc in range(8):
                    @block.vector
                    def _(v, e=e, ig=ig, kc=kc, gidx=gidx, lds_xb=lds_xb):
                        S.wait(v, "lds", lds_xb)
                        if (e, ig) != (0, 0) and kc == 0:
                            S.wait(v, "th", S.n["th"])
                        b_bc = bass.AP(sLNB.ap().tensor, e * 8 + kc,
                                       [[64, 128], [0, 512]])
                        S.inc(v.scalar_tensor_tensor(
                            sXE[:, kc * 512:(kc + 1) * 512],
                            sXBf[gidx % 2][:, kc * 512:(kc + 1) * 512],
                            sLNG[:, e * 8 + kc:e * 8 + kc + 1], b_bc,
                            OP.mult, OP.add), "ah")
                        ah_t[(gidx, kc)] = S.n["ah"]

                for ft in range(32):
                    @block.tensor
                    def _(t, e=e, ig=ig, ft=ft):
                        if ft == 0:
                            S.wait(t, "gw1", gld_w1h[0])
                            S.wait(t, "ah", ah_t[(e * 4 + ig, 7)])
                            if (e, ig) == (0, 0):
                                S.wait(t, "va", m0_done_va)
                            else:
                                S.wait(t, "vy", S.n["vy"])
                        if ft == 16:
                            S.wait(t, "gw1", gld_w1h[1])
                        if ft >= 2:
                            S.wait(t, "ax", vh_t[(e, ig, ft - 2)])
                        last = None
                        for k in range(8):
                            last = t.matmul(
                                pB[ft % 2][:],
                                sW1[:, k * F + ft * 128:k * F + (ft + 1) * 128],
                                sXE[:, k * 512:(k + 1) * 512],
                                start=(k == 0), stop=(k == 7))
                        S.inc(last, "th")

                    @block.scalar
                    def _(a, e=e, ig=ig, ft=ft):
                        S.wait(a, "th", S.n["th"])
                        if (e, ig) == (0, 0) and ft == 0:
                            S.wait(a, "ta", m0_done_ta)
                        S.inc(a.activation(
                            sHT[:, ft * 512:(ft + 1) * 512], pB[ft % 2][:],
                            AF.Relu,
                            bias=sB1[:, e * 32 + ft:e * 32 + ft + 1]), "ax")
                        vh_t[(e, ig, ft)] = S.n["ax"]

                ty_base = S.n["ty"]

                @block.gpsimd
                def _(gp, e=e, ig=ig, ty_base=ty_base):
                    for wb in range(4):
                        S.wait(gp, "gld2", S.n["gld2"])
                        if wb == 0:
                            S.waitf(gp, "ty", max(ty_base - 8, 0))
                        elif wb == 1:
                            S.wait(gp, "ty", ty_base)
                        else:
                            S.waitf(gp, "ty", ty_base + 8 * (wb - 1))
                        S.inc(gp.dma_start(
                            out=sW2b[wb % 2][:].rearrange("p (k c) -> p k c", k=8),
                            in_=w2[e, wb * 1024:(wb + 1) * 1024, :].rearrange(
                                "(k p) c -> p k c", p=128)), "gld2", 16)
                        w2l_t[(e, ig, wb)] = S.n["gld2"]

                @block.tensor
                def _(t, e=e, ig=ig):
                    for ft in range(32):
                        wb = ft // 8
                        if ft == 0:
                            S.wait(t, "ax", vh_t[(e, ig, 31)])
                            if (e, ig) == (0, 0):
                                S.wait(t, "vfa", S.n["vfa"])
                        if ft % 8 == 0:
                            S.wait(t, "gld2", w2l_t[(e, ig, wb)])
                        last = None
                        for sub in range(8):
                            it, half = divmod(sub, 2)
                            last = t.matmul(
                                pB[sub][:],
                                sHT[:, ft * 512 + it * 128:
                                    ft * 512 + (it + 1) * 128],
                                sW2b[wb % 2][:, (ft % 8) * C + half * 512:
                                             (ft % 8) * C + (half + 1) * 512],
                                start=(ft == 0), stop=(ft == 31))
                        S.inc(last, "ty")
                        ty_t[(e, ig, ft)] = S.n["ty"]

                @block.vector
                def _(v, e=e, ig=ig):
                    S.wait(v, "ty", ty_t[(e, ig, 31)])
                    last = None
                    for sub in range(8):
                        it, half = divmod(sub, 2)
                        it_abs = ig * 4 + it
                        sl = sFA[:, it_abs * C + half * 512:
                                 it_abs * C + (half + 1) * 512]
                        last = v.scalar_tensor_tensor(
                            sl, pB[sub][:],
                            sOH[:, it_abs * 8 + e:it_abs * 8 + e + 1],
                            sl, OP.mult, OP.add)
                    S.inc(last, "vy")

        my_done = S.n["vy"]

        for it in range(NI_):
            @block.sync
            def _(sync, it=it):
                S.wait(sync, "lds", S.n["lds"])
                if it >= 2:
                    S.wait(sync, "vg", vg_t[it - 2])
                S.inc(sync.dma_start(out=sX0[it % 2][:],
                                     in_=xn0[it * 128:(it + 1) * 128, :]), "lds", 16)

            @block.vector
            def _(v, it=it):
                S.wait(v, "lds", S.n["lds"])
                S.wait(v, "vy", my_done)
                S.wait(v, "aal", S.n["aal"])
                if it >= 1:
                    S.wait(v, "ax", ag_t[it - 1])
                sTv = sXE.ap().bitcast(F32)[:, 0:C]
                S.inc(v.scalar_tensor_tensor(
                    sTv, sFA[:, it * C:(it + 1) * C], sAL[:, it:it + 1],
                    sX0[it % 2][:], OP.mult, OP.add), "vg")
                vg_t[it] = S.n["vg"]

            @block.scalar
            def _(a, it=it):
                S.wait(a, "vg", vg_t[it])
                if it >= 2:
                    S.wait(a, "st", 16 * (it - 1))
                sTv = sXE.ap().bitcast(F32)[:, 0:C]
                S.inc(a.activation(sG[it % 2][:], sTv, AF.Silu), "ax")
                ag_t[it] = S.n["ax"]

            @block.gpsimd
            def _(gp, it=it):
                S.wait(gp, "ax", ag_t[it])
                S.wait(gp, "st", S.n["st"])
                S.inc(gp.dma_start(out=g0[it * 128:(it + 1) * 128, :],
                                   in_=sG[it % 2][:]), "st", 16)

        @block.gpsimd
        def _(gp):
            gp.wait_ge(S.h["st"], S.n["st"])

        S.check_futures()




def build_moe0():
    nc = bass.Bass("TRN2", target_bir_lowering=False)
    with contextlib.ExitStack() as stack:
        D = moe0_drams(nc)
        D["identf"] = nc.dram_tensor("identf", [128, 128], F32,
                                     kind="ExternalInput").ap()
        D["onesf"] = nc.dram_tensor("onesf", [1, 128], F32,
                                    kind="ExternalInput").ap()
        S = Sems(nc, stack)
        block = stack.enter_context(nc.Block())
        emit_moe0(nc, block, S, stack, D)
    return nc

def run_moe0(inp, xn0, xlnbT_by_core):
    nc = build_moe0()
    base = {
        "cen": inp["L0_ffn_centroids"],
        "onesf": np.ones((1, 128), dtype=np.float32),
        "lngR": np.ascontiguousarray(
            inp["L0_ffn_ln_g"].reshape(8, 8, 128).transpose(2, 0, 1).reshape(128, 64)),
        "lnbR": np.ascontiguousarray(
            inp["L0_ffn_ln_b"].reshape(8, 8, 128).transpose(2, 0, 1).reshape(128, 64)),
        "w1": inp["L0_ffn_w1"],
        "b1R": np.ascontiguousarray(
            inp["L0_ffn_b1"].reshape(8, 32, 128).transpose(2, 0, 1).reshape(128, 256)),
        "w2": inp["L0_ffn_w2"],
        "b2": inp["L0_ffn_b2"],
        "identf": np.eye(128, dtype=np.float32),
    }
    xf = xn0.reshape(-1, C)
    in_maps = [dict(base, xn0=xf[c * TLOC:(c + 1) * TLOC],
                    xlnbT=xlnbT_by_core[c])
               for c in range(NCORES)]
    res = run_bass_kernel_spmd(nc, in_maps, core_ids=list(range(NCORES)))
    return res.results


def comb1_drams(nc, kind="ExternalInput", okind="ExternalOutput"):
    d = {}
    d["x"] = nc.dram_tensor("x", [TLOC, C], F32, kind=kind).ap()
    d["s0c"] = nc.dram_tensor("s0c", [TLOC, C], F32, kind=kind).ap()
    d["g0c"] = nc.dram_tensor("g0c", [TLOC, C], F32, kind=kind).ap()
    d["x1"] = nc.dram_tensor("x1", [TLOC, C], F32, kind=okind).ap()
    d["xn1T"] = nc.dram_tensor("xn1T", [C, TLOC], BF16, kind=okind).ap()
    return d


def emit_comb1(nc, block, S, stack, DR):
    x, s0, g0 = DR["x"], DR["s0c"], DR["g0c"]
    ident = DR["identb"]
    x1, xn1T = DR["x1"], DR["xn1T"]
    if True:
        E = stack.enter_context

        sx = [E(nc.sbuf_tensor(f"sx{j}", [128, C], F32)) for j in range(2)]
        ss = [E(nc.sbuf_tensor(f"ss{j}", [128, C], F32)) for j in range(2)]
        sg = [E(nc.sbuf_tensor(f"sg{j}", [128, C], F32)) for j in range(2)]
        sx1 = [E(nc.sbuf_tensor(f"sx1{j}", [128, C], F32)) for j in range(2)]
        sxc = E(nc.sbuf_tensor("sxc", [128, C], F32))
        scrap = E(nc.sbuf_tensor("scrap", [128, C], F32))
        sxnb = E(nc.sbuf_tensor("sxnb", [128, C], BF16))
        sident = E(nc.sbuf_tensor("sident", [128, 128], BF16))
        st1 = E(nc.sbuf_tensor("st1", [128, 1], F32))
        smu = E(nc.sbuf_tensor("smu", [128, 1], F32))
        sss = E(nc.sbuf_tensor("sss", [128, 1], F32))
        svar = E(nc.sbuf_tensor("svar", [128, 1], F32))
        slog = E(nc.sbuf_tensor("slog", [128, 1], F32))
        srstd = E(nc.sbuf_tensor("srstd", [128, 1], F32))
        sT0 = [E(nc.sbuf_tensor(f"sT0_{k}", [128, TLOC], BF16)) for k in range(8)]
        pt = [E(nc.psum_tensor(f"pt{j}", [128, 512], BF16)) for j in range(2)]

        S.new("ld", "ldi", "v", "a", "t", "st", "vs", "as_")
        v_t, a_t, t_t, e_t = {}, {}, {}, {}

        @block.sync
        def _(sync):
            S.inc(sync.dma_start(out=sident[:], in_=ident[:, :]), "ldi", 16)
            sync.wait_ge(S.h["ldi"], 16)

        for i in range(NT):
            @block.sync
            def _(sync, i=i):
                S.wait(sync, "ld", S.n["ld"])
                if i >= 2:
                    S.wait(sync, "v", v_t[i - 2])
                S.inc(sync.dma_start(out=sx[i % 2][:],
                                     in_=x[i * 128:(i + 1) * 128, :]), "ld", 16)
                S.inc(sync.dma_start(out=ss[i % 2][:],
                                     in_=s0[i * 128:(i + 1) * 128, :]), "ld", 16)
                S.inc(sync.dma_start(out=sg[i % 2][:],
                                     in_=g0[i * 128:(i + 1) * 128, :]), "ld", 16)
            ld_i = S.n["ld"]

            @block.vector
            def _(v, i=i, ld_i=ld_i):
                S.wait(v, "ld", ld_i)
                if i >= 2:
                    S.wait(v, "st", 16 * (i - 1))
                S.chain(v, v.tensor_mul(scrap[:], ss[i % 2][:], sg[i % 2][:]), "vs")
                S.inc(v.tensor_add(sx1[i % 2][:], sx[i % 2][:], scrap[:]), "v")
                S.wait(v, "v", S.n["v"])
                S.chain(v, v.reduce_sum(st1[:], sx1[i % 2][:], axis=AX.X), "vs")
                S.chain(v, v.tensor_scalar_mul(smu[:], st1[:], 1.0 / C), "vs")
                S.inc(v.tensor_single_scalar(sxc[:], sx1[i % 2][:], smu[:, 0:1],
                                             OP.subtract), "v")
                v_t[i] = S.n["v"]

            @block.gpsimd
            def _(gp, i=i):
                S.wait(gp, "st", S.n["st"])
                S.wait(gp, "v", v_t[i])
                S.inc(gp.dma_start(out=x1[i * 128:(i + 1) * 128, :],
                                   in_=sx1[i % 2][:]), "st", 16)

            @block.scalar
            def _(a, i=i):
                S.wait(a, "v", v_t[i])
                S.inc(a.activation(scrap[:], sxc[:], AF.Square, accum_out=sss[:]),
                      "a")

            @block.vector
            def _(v, i=i):
                S.wait(v, "a", S.n["a"])
                S.inc(v.tensor_scalar(svar[:], sss[:], 1.0 / C, EPS,
                                      OP.mult, OP.add), "v")

            @block.scalar
            def _(a, i=i):
                S.wait(a, "v", S.n["v"])
                S.chain(a, a.activation(slog[:], svar[:], AF.Ln), "as_")
                S.inc(a.activation(srstd[:], slog[:], AF.Exp, scale=-0.5), "a")
            a_t[i] = S.n["a"]

            @block.vector
            def _(v, i=i):
                S.wait(v, "a", a_t[i])
                if i >= 1:
                    S.wait(v, "t", t_t[i - 1])
                S.inc(v.tensor_single_scalar(sxnb[:], sxc[:], srstd[:, 0:1],
                                             OP.mult), "v")
                vb_i = S.n["v"]
                v_t[(i, "b")] = vb_i

            @block.tensor
            def _(t, i=i):
                S.wait(t, "ldi", 16)
                S.wait(t, "v", v_t[(i, "b")])
                if i >= 2:
                    S.wait(t, "v", e_t[i - 2])
                last = None
                for k in range(4):
                    last = t.transpose(pt[i % 2][:, k * 128:(k + 1) * 128],
                                       sxnb[:, k * 128:(k + 1) * 128], sident[:])
                S.inc(last, "t")
                t_t[(i, 0)] = S.n["t"]

            @block.vector
            def _(v, i=i):
                S.wait(v, "t", t_t[(i, 0)])
                last = None
                for k in range(4):
                    last = v.tensor_copy(sT0[k][:, i * 128:(i + 1) * 128],
                                         pt[i % 2][:, k * 128:(k + 1) * 128])
                S.inc(last, "v")
                e_t[(i, 0)] = S.n["v"]

            @block.tensor
            def _(t, i=i):
                S.wait(t, "v", e_t[(i, 0)])
                last = None
                for k in range(4, 8):
                    last = t.transpose(pt[i % 2][:, (k - 4) * 128:(k - 3) * 128],
                                       sxnb[:, k * 128:(k + 1) * 128], sident[:])
                S.inc(last, "t")
                t_t[i] = S.n["t"]

            @block.vector
            def _(v, i=i):
                S.wait(v, "t", t_t[i])
                last = None
                for k in range(4, 8):
                    last = v.tensor_copy(sT0[k][:, i * 128:(i + 1) * 128],
                                         pt[i % 2][:, (k - 4) * 128:(k - 3) * 128])
                S.inc(last, "v")
                e_t[i] = S.n["v"]

        @block.gpsimd
        def _(gp):
            S.wait(gp, "v", e_t[NT - 1])
            S.wait(gp, "st", S.n["st"])
            for k in range(8):
                S.inc(gp.dma_start(out=xn1T[k * 128:(k + 1) * 128, :],
                                   in_=sT0[k][:]), "st", 16)
            gp.wait_ge(S.h["st"], S.n["st"])

        S.check_futures()




def build_comb1():
    nc = bass.Bass("TRN2", target_bir_lowering=False)
    with contextlib.ExitStack() as stack:
        D = comb1_drams(nc)
        D["identb"] = nc.dram_tensor("identb", [128, 128], BF16,
                                     kind="ExternalInput").ap()
        S = Sems(nc, stack)
        block = stack.enter_context(nc.Block())
        emit_comb1(nc, block, S, stack, D)
    return nc

def attn1_drams(nc, kind="ExternalInput", okind="ExternalOutput", fused=False):
    H1, QEXT = 8, 3072
    d = {}
    if not fused:
        d["xn1Te"] = nc.dram_tensor("xn1Te", [C, QEXT], BF16, kind=kind).ap()
    d["WQ"] = nc.dram_tensor("WQ", [C, H1 * C], F32, kind="ExternalInput").ap()
    d["WK"] = nc.dram_tensor("WK", [C, H1 * C], F32, kind="ExternalInput").ap()
    d["WV"] = nc.dram_tensor("WV", [C, H1 * C], F32, kind="ExternalInput").ap()
    d["cosT"] = nc.dram_tensor("cosT", [64, QEXT], F32, kind="ExternalInput").ap()
    d["sinS"] = nc.dram_tensor("sinS", [64, QEXT], F32, kind="ExternalInput").ap()
    d["dec11"] = nc.dram_tensor("dec11", [128, 11 * 128], F32,
                                kind="ExternalInput").ap()
    d["gnwR"] = nc.dram_tensor("gnwR", [128, C], F32, kind="ExternalInput").ap()
    d["gnbR"] = nc.dram_tensor("gnbR", [128, C], F32, kind="ExternalInput").ap()
    if okind:
        d["s1"] = nc.dram_tensor("s1", [TLOC, C], F32, kind=okind).ap()
    d["s1pre"] = nc.dram_tensor("s1pre", [TLOC, C], F32, kind="Internal").ap()
    return d


def emit_attn1(nc, block, S, stack, DR):
    H1 = 8
    QEXT = 3072
    NSL = 12
    xn1srcs = DR["xn1srcs"]
    WQ, WK, WV = DR["WQ"], DR["WK"], DR["WV"]
    cosT, sinS, dec11 = DR["cosT"], DR["sinS"], DR["dec11"]
    gnwR, gnbR = DR["gnwR"], DR["gnbR"]
    s1, s1pre = DR["s1"], DR["s1pre"]
    if True:
        E = stack.enter_context

        sWq = E(nc.sbuf_tensor("sWq", [128, 8 * C], BF16))
        sWk = E(nc.sbuf_tensor("sWk", [128, 8 * C], BF16))
        sWv = E(nc.sbuf_tensor("sWv", [128, 8 * C], BF16))
        sKT = E(nc.sbuf_tensor("sKT", [128, 8 * TLOC], BF16))
        sQT = E(nc.sbuf_tensor("sQT", [128, NSL * 1024], BF16))
        sV = E(nc.sbuf_tensor("sV", [128, NSL * 1024], BF16))
        sCos = E(nc.sbuf_tensor("sCos", [64, QEXT], F32))
        sSin = E(nc.sbuf_tensor("sSin", [64, QEXT], F32))
        sDec = E(nc.sbuf_tensor("sDec", [128, 11 * 128], F32))
        sXn = [E(nc.sbuf_tensor(f"sXn{j}", [128, 8 * 512], BF16)) for j in range(2)]
        sAm = [E(nc.sbuf_tensor(f"sAm{j}", [128, 256], BF16)) for j in range(2)]
        sStg = [E(nc.sbuf_tensor(f"sStg{j}", [128, C], F32)) for j in range(2)]
        srp = E(nc.sbuf_tensor("srp", [64, 512], F32))
        st1_ = E(nc.sbuf_tensor("st1_", [64, 512], F32))
        st2_ = E(nc.sbuf_tensor("st2_", [64, 512], F32))
        sMu = E(nc.sbuf_tensor("sMu", [128, 16], F32))
        sSm = E(nc.sbuf_tensor("sSm", [128, 16], F32))
        sVr = E(nc.sbuf_tensor("sVr", [128, 16], F32))
        sLg = E(nc.sbuf_tensor("sLg", [128, 16], F32))
        sRs = E(nc.sbuf_tensor("sRs", [128, 16], F32))
        sGw = E(nc.sbuf_tensor("sGw", [128, C], F32))
        sGb = E(nc.sbuf_tensor("sGb", [128, C], F32))
        pPr = [E(nc.psum_tensor(f"pPr{j}", [128, 512], F32)) for j in range(2)]
        pA = [E(nc.psum_tensor(f"pA{j}", [128, 256], F32)) for j in range(2)]
        pOutb = [E(nc.psum_tensor(f"pOut{j}", [128, 512], F32)) for j in range(4)]

        S.new("lds", "gw", "st", "sta", "tp", "ta", "tav",
              "vp", "vm", "vev", "aq", "vs", "as_", "ag", "vg")

        @block.sync
        def _(sync):
            S.inc(sync.dma_start(out=sCos[:], in_=cosT[:, :]), "lds", 16)
            S.inc(sync.dma_start(out=sSin[:], in_=sinS[:, :]), "lds", 16)
            S.inc(sync.dma_start(out=sDec[:], in_=dec11[:, :]), "lds", 16)
            S.inc(sync.dma_start(out=sGw[:], in_=gnwR[:, :]), "lds", 16)
            S.inc(sync.dma_start(out=sGb[:], in_=gnbR[:, :]), "lds", 16)
            sync.wait_ge(S.h["lds"], S.n["lds"])

        gi_ctr = [0]
        xn_t = {}
        qrdy, vrdy = {}, {}
        am_t, av_t = {}, {}
        aq_hist = []

        def emit_proj_group(h, kind, g, head_first):
            gi = gi_ctr[0]
            gi_ctr[0] += 1
            buf = gi % 2

            @block.sync
            def _(sync, kind=kind, g=g, buf=buf, gi=gi):
                S.wait(sync, "lds", S.n["lds"])
                if gi >= 2:
                    S.wait(sync, "tp", xn_t[gi - 2])
                srcap, colb = xn1srcs(g)
                S.inc(sync.dma_start(
                    out=sXn[buf][:].rearrange("p (k n) -> p k n", k=8),
                    in_=srcap[:, colb:colb + 512].rearrange(
                        "(k p) n -> p k n", p=128)), "lds", 16)
            lds_g = S.n["lds"]

            if kind == "v":
                for tt in range(4):
                    qt = g * 4 + tt
                    sl = qt % NSL
                    for half in range(2):
                        u = (2 * tt + half) % 2

                        @block.tensor
                        def _(t, h=h, tt=tt, half=half, u=u, buf=buf,
                              lds_g=lds_g):
                            S.wait(t, "lds", lds_g)
                            S.wait(t, "gw", wld_t[h])
                            if len(aq_hist) >= 2:
                                S.wait(t, "aq", aq_hist[-2])
                            last = None
                            for kc in range(8):
                                last = t.matmul(
                                    pPr[u][:],
                                    sXn[buf][:, kc * 512 + tt * 128:
                                             kc * 512 + (tt + 1) * 128],
                                    sWv[:, kc * 1024 + half * 512:
                                        kc * 1024 + (half + 1) * 512],
                                    start=(kc == 0), stop=(kc == 7))
                            S.inc(last, "tp")

                        @block.scalar
                        def _(a, sl=sl, half=half, u=u, head_first=head_first):
                            S.wait(a, "tp", S.n["tp"])
                            if head_first[0]:
                                S.wait(a, "ta", S.n["ta"])
                                S.wait(a, "tav", S.n["tav"])
                                head_first[0] = False
                            S.inc(a.activation(
                                sV[:, sl * 1024 + half * 512:
                                    sl * 1024 + (half + 1) * 512],
                                pPr[u][:], AF.Silu), "aq")
                            aq_hist.append(S.n["aq"])
                    vrdy[qt] = S.n["aq"]
                xn_t[gi] = S.n["tp"]
                return

            wsel = sWk if kind == "k" else sWq
            sl0 = (g * 4) % NSL
            for ct in range(8):
                @block.tensor
                def _(t, h=h, ct=ct, buf=buf, lds_g=lds_g):
                    S.wait(t, "lds", lds_g)
                    S.wait(t, "gw", wld_t[h])
                    if len(aq_hist) >= 2:
                        S.wait(t, "aq", aq_hist[-2])
                    last = None
                    for kc in range(8):
                        last = t.matmul(
                            pPr[ct % 2][:],
                            wsel[:, kc * 1024 + ct * 128:kc * 1024 + (ct + 1) * 128],
                            sXn[buf][:, kc * 512:(kc + 1) * 512],
                            start=(kc == 0), stop=(kc == 7))
                    S.inc(last, "tp")

                if ct == 0:
                    @block.vector
                    def _(v, g=g, ct=ct):
                        S.wait(v, "tp", S.n["tp"])
                        pos = g * 512
                        S.chain(v, v.stream_shuffle(
                            srp[:], pPr[ct % 2][0:64, :],
                            [i ^ 1 for i in range(32)]), "vs")
                        S.chain(v, v.tensor_mul(st1_[:], pPr[ct % 2][0:64, :],
                                                sCos[:, pos:pos + 512]), "vs")
                        S.chain(v, v.tensor_mul(st2_[:], srp[:],
                                                sSin[:, pos:pos + 512]), "vs")
                        S.inc(v.tensor_add(st1_[:], st1_[:], st2_[:]), "vp")

                    @block.scalar
                    def _(a, kind=kind, g=g, ct=ct, sl0=sl0,
                          head_first=head_first):
                        S.wait(a, "vp", S.n["vp"])
                        if head_first[0]:
                            S.wait(a, "ta", S.n["ta"])
                            S.wait(a, "tav", S.n["tav"])
                            head_first[0] = False
                        if kind == "k":
                            a.activation(sKT[0:64, g * 512:(g + 1) * 512],
                                         st1_[:], AF.Silu)
                            S.inc(a.activation(
                                sKT[64:128, ct * TLOC + g * 512:
                                    ct * TLOC + (g + 1) * 512],
                                pPr[ct % 2][64:128, :], AF.Silu), "aq")
                        else:
                            a.activation(
                                bass.AP(sQT.ap().tensor, sl0 * 1024,
                                        [[NSL * 1024, 64], [1024, 4], [1, 128]]),
                                st1_[:].rearrange("p (s n) -> p s n", s=4),
                                AF.Silu)
                            S.inc(a.activation(
                                bass.AP(sQT.ap().tensor,
                                        64 * NSL * 1024 + sl0 * 1024,
                                        [[NSL * 1024, 64], [1024, 4], [1, 128]]),
                                pPr[ct % 2][64:128, :].rearrange(
                                    "p (s n) -> p s n", s=4), AF.Silu), "aq")
                        aq_hist.append(S.n["aq"])
                else:
                    @block.scalar
                    def _(a, kind=kind, g=g, ct=ct, sl0=sl0):
                        S.wait(a, "tp", S.n["tp"])
                        if kind == "k":
                            S.inc(a.activation(
                                sKT[:, ct * TLOC + g * 512:
                                    ct * TLOC + (g + 1) * 512],
                                pPr[ct % 2][:], AF.Silu), "aq")
                        else:
                            S.inc(a.activation(
                                bass.AP(sQT.ap().tensor, sl0 * 1024 + ct * 128,
                                        [[NSL * 1024, 128], [1024, 4], [1, 128]]),
                                pPr[ct % 2][:].rearrange("p (s n) -> p s n", s=4),
                                AF.Silu), "aq")
                        aq_hist.append(S.n["aq"])
            if kind == "q":
                for tt in range(4):
                    qrdy[g * 4 + tt] = S.n["aq"]
            else:
                krdy[g] = S.n["aq"]
            xn_t[gi] = S.n["tp"]

        wld_t, krdy = {}, {}
        sta_after = {}

        for h in range(H1):
            head_first = [h > 0]

            @block.gpsimd
            def _(gp, h=h):
                S.wait(gp, "gw", S.n["gw"])
                if h > 0:
                    S.wait(gp, "tp", S.n["tp"])
                for (wd, ws) in ((WQ, sWq), (WK, sWk), (WV, sWv)):
                    S.inc(gp.dma_start(
                        out=ws[:].rearrange("p (k c) -> p k c", k=8),
                        in_=wd[:, h * C:(h + 1) * C].rearrange(
                            "(k p) c -> p k c", p=128)), "gw", 16)
            wld_t[h] = S.n["gw"]

            for g in range(4):
                emit_proj_group(h, "k", g, head_first)
            for g in range(3):
                emit_proj_group(h, "q", g, head_first)
                emit_proj_group(h, "v", g, head_first)
            for kg in range(8):
                if kg in (2, 4, 6):
                    g = kg // 2 + 2
                    emit_proj_group(h, "q", g, head_first)
                    emit_proj_group(h, "v", g, head_first)
                qts = list(range(2 * kg, min(2 * kg + 10, 24)))
                for qi, qt in enumerate(qts):
                    @block.tensor
                    def _(t, h=h, kg=kg, qt=qt, qi=qi):
                        S.wait(t, "aq", qrdy[qt])
                        S.wait(t, "aq", krdy[3])
                        if (h, kg, qi - 2) in am_t:
                            S.wait(t, "vm", am_t[(h, kg, qi - 2)])
                        elif qi < 2:
                            S.wait(t, "vm", S.n["vm"])
                        last = None
                        for kc in range(8):
                            last = t.matmul(
                                pA[qi % 2][:],
                                sQT[:, (qt % NSL) * 1024 + kc * 128:
                                    (qt % NSL) * 1024 + (kc + 1) * 128],
                                sKT[:, kc * TLOC + kg * 256:
                                    kc * TLOC + (kg + 1) * 256],
                                start=(kc == 0), stop=(kc == 7))
                        S.inc(last, "ta")

                    @block.vector
                    def _(v, h=h, kg=kg, qt=qt, qi=qi):
                        S.wait(v, "ta", S.n["ta"])
                        if (h, kg, qi - 2) in av_t:
                            S.wait(v, "tav", av_t[(h, kg, qi - 2)])
                        elif qi < 2:
                            S.wait(v, "tav", S.n["tav"])
                        o0 = qt - 2 * kg
                        mask = bass.AP(sDec.ap().tensor, (o0 + 1) * 128,
                                       [[11 * 128, 128], [-128, 2], [1, 128]])
                        S.inc(v.tensor_tensor(
                            sAm[qi % 2][:].rearrange("p (j n) -> p j n", j=2),
                            pA[qi % 2][:].rearrange("p (j n) -> p j n", j=2),
                            mask, OP.mult), "vm")
                        am_t[(h, kg, qi)] = S.n["vm"]

                    @block.tensor
                    def _(t, h=h, kg=kg, qt=qt, qi=qi, nq=len(qts)):
                        S.wait(t, "vm", am_t[(h, kg, qi)])
                        S.wait(t, "aq", vrdy[qt])
                        if qi == 0:
                            S.wait(t, "vev", S.n["vev"])
                        last = None
                        for j in range(2):
                            for half in range(2):
                                last = t.matmul(
                                    pOutb[j * 2 + half][:],
                                    sAm[qi % 2][:, j * 128:(j + 1) * 128],
                                    sV[:, (qt % NSL) * 1024 + half * 512:
                                        (qt % NSL) * 1024 + (half + 1) * 512],
                                    start=(qi == 0), stop=(qi == nq - 1))
                        S.inc(last, "tav")
                        av_t[(h, kg, qi)] = S.n["tav"]

                @block.vector
                def _(v, h=h, kg=kg):
                    S.wait(v, "tav", S.n["tav"])
                    prev = (h, kg - 1) if kg > 0 else (h - 1, 7)
                    if prev in sta_after:
                        S.wait(v, "sta", sta_after[prev])
                    last = None
                    for j in range(2):
                        for half in range(2):
                            last = v.tensor_copy(
                                sStg[j][:, half * 512:(half + 1) * 512],
                                pOutb[j * 2 + half][:])
                    S.inc(last, "vev")

                @block.gpsimd
                def _(gp, h=h, kg=kg):
                    S.wait(gp, "sta", S.n["sta"])
                    S.wait(gp, "vev", S.n["vev"])
                    for j in range(2):
                        kt = 2 * kg + j
                        S.inc(gp.dma_start(
                            out=s1pre[kt * 128:(kt + 1) * 128, :],
                            in_=sStg[j][:],
                            accum_op=(OP.add if h > 0 else OP.bypass)), "sta", 16)
                sta_after[(h, kg)] = S.n["sta"]

        sta_all = S.n["sta"]

        gn_v = {}
        for it in range(NT):
            @block.sync
            def _(sync, it=it):
                S.wait(sync, "lds", S.n["lds"])
                S.wait(sync, "sta", sta_all)
                if it >= 2:
                    S.wait(sync, "vg", gn_v[it - 2])
                    S.wait(sync, "st", 16 * (it - 1))
                S.inc(sync.dma_start(out=sStg[it % 2][:],
                                     in_=s1pre[it * 128:(it + 1) * 128, :]),
                      "lds", 16)
            lds_it = S.n["lds"]

            @block.vector
            def _(v, it=it, lds_it=lds_it):
                S.wait(v, "lds", lds_it)
                if it == 0:
                    S.wait(v, "tp", S.n["tp"])
                x3 = sStg[it % 2][:].rearrange("p (s d) -> p s d", s=16)
                S.chain(v, v.reduce_sum(sSm[:], x3, axis=AX.X), "vs")
                S.chain(v, v.tensor_scalar_mul(sMu[:], sSm[:], 1.0 / 64), "vs")
                mu_bc = bass.AP(sMu.ap().tensor, 0, [[16, 128], [1, 16], [0, 64]])
                sc = sXn[0].ap().bitcast(F32)[:, 0:C]
                S.chain(v, v.tensor_tensor(sc.rearrange("p (s d) -> p s d", s=16),
                                           x3, mu_bc, OP.subtract), "vs")
                sc2 = sXn[1].ap().bitcast(F32)[:, 0:C]
                S.chain(v, v.tensor_mul(sc2, sc, sc), "vs")
                S.chain(v, v.reduce_sum(sVr[:],
                                        sc2.rearrange("p (s d) -> p s d", s=16),
                                        axis=AX.X), "vs")
                S.inc(v.tensor_scalar(sVr[:], sVr[:], 1.0 / 64, EPS,
                                      OP.mult, OP.add), "vg")

            @block.scalar
            def _(a, it=it):
                S.wait(a, "vg", S.n["vg"])
                S.chain(a, a.activation(sLg[:], sVr[:], AF.Ln), "as_")
                S.inc(a.activation(sRs[:], sLg[:], AF.Exp, scale=-0.5), "ag")

            @block.vector
            def _(v, it=it):
                S.wait(v, "ag", S.n["ag"])
                sc = sXn[0].ap().bitcast(F32)[:, 0:C]
                rs_bc = bass.AP(sRs.ap().tensor, 0, [[16, 128], [1, 16], [0, 64]])
                sc2 = sXn[1].ap().bitcast(F32)[:, 0:C]
                S.chain(v, v.tensor_tensor(sc2.rearrange("p (s d) -> p s d", s=16),
                                           sc.rearrange("p (s d) -> p s d", s=16),
                                           rs_bc, OP.mult), "vs")
                S.chain(v, v.tensor_mul(sc, sc2, sGw[:]), "vs")
                S.inc(v.tensor_add(sStg[it % 2][:], sc, sGb[:]), "vg")
                gn_v[it] = S.n["vg"]

            @block.gpsimd
            def _(gp, it=it):
                S.wait(gp, "st", S.n["st"])
                S.wait(gp, "vg", gn_v[it])
                S.inc(gp.dma_start(out=s1[it * 128:(it + 1) * 128, :],
                                   in_=sStg[it % 2][:]), "st", 16)

        @block.gpsimd
        def _(gp):
            gp.wait_ge(S.h["st"], S.n["st"])

        S.check_futures()




def build_attn1():
    nc = bass.Bass("TRN2", target_bir_lowering=False)
    with contextlib.ExitStack() as stack:
        D = attn1_drams(nc)
        D["xn1srcs"] = lambda g: (D["xn1Te"], g * 512)
        S = Sems(nc, stack)
        block = stack.enter_context(nc.Block())
        emit_attn1(nc, block, S, stack, D)
    return nc

FFN1_WNAMES = ["encw", "encbR", "s2cw", "s2cb", "rw1", "rb1", "rw2", "rb2",
               "ew1", "eb1R", "ew2", "eb2r", "c2sw", "c2sbR", "decw", "decb",
               "lngR1", "lnbR1"]


def ffn1_drams(nc, kind="ExternalInput", okind="ExternalOutput"):
    F2, MD, NE2, EH = 2048, 64, 8, 1024
    d = {}
    if kind:
        d["xn1Tf"] = nc.dram_tensor("xn1Tf", [C, TLOC], BF16, kind=kind).ap()
        d["x1f"] = nc.dram_tensor("x1f", [TLOC, C], F32, kind=kind).ap()
        d["s1f"] = nc.dram_tensor("s1f", [TLOC, C], F32, kind=kind).ap()
    d["encw"] = nc.dram_tensor("encw", [C, F2], F32, kind="ExternalInput").ap()
    d["encbR"] = nc.dram_tensor("encbR", [128, 16], F32, kind="ExternalInput").ap()
    d["s2cw"] = nc.dram_tensor("s2cw", [F2, MD], F32, kind="ExternalInput").ap()
    d["s2cb"] = nc.dram_tensor("s2cb", [MD, 1], F32, kind="ExternalInput").ap()
    d["rw1"] = nc.dram_tensor("rw1", [MD, 64], F32, kind="ExternalInput").ap()
    d["rb1"] = nc.dram_tensor("rb1", [64, 1], F32, kind="ExternalInput").ap()
    d["rw2"] = nc.dram_tensor("rw2", [64, NE2], F32, kind="ExternalInput").ap()
    d["rb2"] = nc.dram_tensor("rb2", [1, NE2], F32, kind="ExternalInput").ap()
    d["ew1"] = nc.dram_tensor("ew1", [NE2, MD, EH], F32, kind="ExternalInput").ap()
    d["eb1R"] = nc.dram_tensor("eb1R", [128, NE2 * 8], F32, kind="ExternalInput").ap()
    d["ew2"] = nc.dram_tensor("ew2", [NE2, EH, MD], F32, kind="ExternalInput").ap()
    d["eb2r"] = nc.dram_tensor("eb2r", [1, NE2 * MD], F32, kind="ExternalInput").ap()
    d["c2sw"] = nc.dram_tensor("c2sw", [MD, F2], F32, kind="ExternalInput").ap()
    d["c2sbR"] = nc.dram_tensor("c2sbR", [128, 16], F32, kind="ExternalInput").ap()
    d["decw"] = nc.dram_tensor("decw", [F2, C], F32, kind="ExternalInput").ap()
    d["decb"] = nc.dram_tensor("decb", [1, C], F32, kind="ExternalInput").ap()
    d["lngR1"] = nc.dram_tensor("lngR1", [128, C], F32, kind="ExternalInput").ap()
    d["lnbR1"] = nc.dram_tensor("lnbR1", [128, C], F32, kind="ExternalInput").ap()
    d["x2"] = nc.dram_tensor("x2", [TLOC, C], F32, kind=okind).ap()
    return d


def emit_ffn1(nc, block, S, stack, DR):
    F2, MD, NE2, EH = 2048, 64, 8, 1024
    xn1T, x1, s1 = DR["xn1Tf"], DR["x1f"], DR["s1f"]
    encw, encbR, s2cw, s2cb = DR["encw"], DR["encbR"], DR["s2cw"], DR["s2cb"]
    rw1, rb1, rw2, rb2 = DR["rw1"], DR["rb1"], DR["rw2"], DR["rb2"]
    ew1, eb1R, ew2, eb2r = DR["ew1"], DR["eb1R"], DR["ew2"], DR["eb2r"]
    c2sw, c2sbR, decw, decb = DR["c2sw"], DR["c2sbR"], DR["decw"], DR["decb"]
    lngR, lnbR = DR["lngR1"], DR["lnbR1"]
    identb, onesf, onesb = DR["identb"], DR["onesf"], DR["onesb"]
    x2 = DR["x2"]
    if True:
        E = stack.enter_context

        sWenc = E(nc.sbuf_tensor("sWenc", [128, 8 * F2], BF16))
        sWs2c = E(nc.sbuf_tensor("sWs2c", [128, 16 * MD], BF16))
        sWr1 = E(nc.sbuf_tensor("sWr1", [64, 64], BF16))
        sWr2 = E(nc.sbuf_tensor("sWr2", [64, NE2], BF16))
        sRb2 = E(nc.sbuf_tensor("sRb2", [1, NE2], BF16))
        sE1 = E(nc.sbuf_tensor("sE1", [64, NE2 * EH], BF16))
        sE2 = E(nc.sbuf_tensor("sE2", [128, 8 * NE2 * MD], BF16))
        sEb2 = E(nc.sbuf_tensor("sEb2", [1, NE2 * MD], BF16))
        sWc2s = E(nc.sbuf_tensor("sWc2s", [64, F2], BF16))
        sWdec = E(nc.sbuf_tensor("sWdec", [128, 16 * C], BF16))
        sDecb = E(nc.sbuf_tensor("sDecb", [1, C], BF16))
        sEncb = E(nc.sbuf_tensor("sEncb", [128, 16], F32))
        sS2cb = E(nc.sbuf_tensor("sS2cb", [64, 1], F32))
        sRb1 = E(nc.sbuf_tensor("sRb1", [64, 1], F32))
        sEb1 = E(nc.sbuf_tensor("sEb1", [128, NE2 * 8], F32))
        sC2sb = E(nc.sbuf_tensor("sC2sb", [128, 16], F32))
        sLng = E(nc.sbuf_tensor("sLng", [128, C], F32))
        sLnb = E(nc.sbuf_tensor("sLnb", [128, C], F32))
        sIdb = E(nc.sbuf_tensor("sIdb", [128, 128], BF16))
        sOnes = E(nc.sbuf_tensor("sOnes", [1, 128], F32))
        sOnesB = E(nc.sbuf_tensor("sOnesB", [1, 128], BF16))
        sXnt = [E(nc.sbuf_tensor(f"sXnt{j}", [128, 8 * 512], BF16))
                for j in range(2)]
        sSPg = E(nc.sbuf_tensor("sSPg", [128, 16 * 512], BF16))
        sCT = E(nc.sbuf_tensor("sCT", [64, TLOC], BF16))
        sRh = E(nc.sbuf_tensor("sRh", [64, 512], BF16))
        sLgt = E(nc.sbuf_tensor("sLgt", [128, 8], F32))
        sM1 = E(nc.sbuf_tensor("sM1", [128, 1], F32))
        sM2 = E(nc.sbuf_tensor("sM2", [128, 1], F32))
        sW1c = E(nc.sbuf_tensor("sW1c", [128, 1], F32))
        sW2c = E(nc.sbuf_tensor("sW2c", [128, 1], F32))
        sOh1 = E(nc.sbuf_tensor("sOh1", [128, 8], F32))
        sOh2 = E(nc.sbuf_tensor("sOh2", [128, 8], F32))
        sCB = E(nc.sbuf_tensor("sCB", [128, NT * 8], F32))
        sEh = E(nc.sbuf_tensor("sEh", [128, 8 * 512], BF16))
        sMoe = E(nc.sbuf_tensor("sMoe", [128, NT * MD], F32))
        sMoeb = E(nc.sbuf_tensor("sMoeb", [128, MD], BF16))
        sMT = E(nc.sbuf_tensor("sMT", [64, TLOC], BF16))
        sRTg = E(nc.sbuf_tensor("sRTg", [128, 16 * 512], BF16))
        sDc = E(nc.sbuf_tensor("sDc", [128, C], F32))
        sSc = E(nc.sbuf_tensor("sSc", [128, C], F32))
        sSc2 = E(nc.sbuf_tensor("sSc2", [128, C], F32))
        sXs = [E(nc.sbuf_tensor(f"sXs{j}", [128, C], F32)) for j in range(2)]
        sSs = [E(nc.sbuf_tensor(f"sSs{j}", [128, C], F32)) for j in range(2)]
        stA = E(nc.sbuf_tensor("stA", [128, 1], F32))
        stB = E(nc.sbuf_tensor("stB", [128, 1], F32))
        stC = E(nc.sbuf_tensor("stC", [128, 1], F32))
        stD = E(nc.sbuf_tensor("stD", [128, 1], F32))
        pP = [E(nc.psum_tensor(f"pP{j}", [128, 512], F32)) for j in range(4)]
        pTb16 = E(nc.psum_tensor("pTb16", [128, 256], BF16))

        S.new("lds", "gw", "st", "tp", "tc", "tr", "te", "td",
              "vp", "vc", "vr", "ve", "vd", "aq", "ar", "ae", "ad",
              "vs", "as_")

        @block.gpsimd
        def _(gp):
            S.inc(gp.dma_start(out=sWenc[:].rearrange("p (k f) -> p k f", k=8),
                               in_=encw.rearrange("(k p) f -> p k f", p=128)),
                  "gw", 16)
            S.inc(gp.dma_start(out=sWs2c[:].rearrange("p (k m) -> p k m", k=16),
                               in_=s2cw.rearrange("(k p) m -> p k m", p=128)),
                  "gw", 16)
            S.inc(gp.dma_start(out=sWr1[:], in_=rw1[:, :]), "gw", 16)
            S.inc(gp.dma_start(out=sWr2[:], in_=rw2[:, :]), "gw", 16)
            S.inc(gp.dma_start(out=sRb2[:], in_=rb2[:, :]), "gw", 16)
            S.inc(gp.dma_start(out=sE1[:].rearrange("p (e f) -> p e f", e=NE2),
                               in_=ew1.rearrange("e p f -> p e f")), "gw", 16)
            for k8 in range(8):
                S.inc(gp.dma_start(
                    out=sE2[:, k8 * NE2 * MD:(k8 + 1) * NE2 * MD].rearrange(
                        "p (e m) -> p e m", e=NE2),
                    in_=ew2[:, k8 * 128:(k8 + 1) * 128, :].rearrange(
                        "e p m -> p e m")), "gw", 16)
            S.inc(gp.dma_start(out=sEb2[:], in_=eb2r[:, :]), "gw", 16)
            S.inc(gp.dma_start(out=sWc2s[:], in_=c2sw.rearrange("m f -> m f")),
                  "gw", 16)
            S.inc(gp.dma_start(out=sWdec[:].rearrange("p (k c) -> p k c", k=16),
                               in_=decw.rearrange("(k p) c -> p k c", p=128)),
                  "gw", 16)
            S.inc(gp.dma_start(out=sDecb[:], in_=decb[:, :]), "gw", 16)
        gw_all = S.n["gw"]

        @block.sync
        def _(sync):
            for (dst, src) in ((sEncb, encbR), (sS2cb, s2cb), (sRb1, rb1),
                               (sEb1, eb1R), (sC2sb, c2sbR), (sLng, lngR),
                               (sLnb, lnbR), (sIdb, identb), (sOnes, onesf),
                               (sOnesB, onesb)):
                S.inc(sync.dma_start(out=dst[:], in_=src[:, :]), "lds", 16)
            sync.wait_ge(S.h["lds"], S.n["lds"])
        lds_all = S.n["lds"]

        xn_t, sp_t, ct_t, rt_t = {}, {}, {}, {}
        for ig in range(4):
            @block.sync
            def _(sync, ig=ig):
                S.wait(sync, "lds", S.n["lds"])
                if ig >= 2:
                    S.wait(sync, "tp", xn_t[ig - 2])
                S.inc(sync.dma_start(
                    out=sXnt[ig % 2][:].rearrange("p (k n) -> p k n", k=8),
                    in_=xn1T[:, ig * 512:(ig + 1) * 512].rearrange(
                        "(k p) n -> p k n", p=128)), "lds", 16)
            lds_g = S.n["lds"]

            for ft in range(16):
                @block.tensor
                def _(t, ig=ig, ft=ft, lds_g=lds_g):
                    S.wait(t, "gw", gw_all)
                    S.wait(t, "lds", lds_g)
                    if ft == 0 and ig > 0:
                        S.wait(t, "vc", S.n["vc"])
                    if ft >= 2:
                        S.wait(t, "aq", S.n["aq"] - 1)
                    last = None
                    for kc in range(8):
                        last = t.matmul(
                            pP[ft % 2][:],
                            sWenc[:, kc * F2 + ft * 128:kc * F2 + (ft + 1) * 128],
                            sXnt[ig % 2][:, kc * 512:(kc + 1) * 512],
                            start=(kc == 0), stop=(kc == 7))
                    S.inc(last, "tp")
                    if ft == 15:
                        xn_t[ig] = S.n["tp"]

                @block.scalar
                def _(a, ig=ig, ft=ft):
                    S.wait(a, "tp", S.n["tp"])
                    S.inc(a.activation(sSPg[:, ft * 512:(ft + 1) * 512],
                                       pP[ft % 2][:], AF.Sigmoid,
                                       bias=sEncb[:, ft:ft + 1]), "aq")

            @block.tensor
            def _(t, ig=ig):
                S.wait(t, "aq", S.n["aq"])
                if ig > 0:
                    S.wait(t, "vc", S.n["vc"])
                last = None
                for fc in range(16):
                    last = t.matmul(pP[2][0:64, :],
                                    sWs2c[:, fc * MD:(fc + 1) * MD],
                                    sSPg[:, fc * 512:(fc + 1) * 512],
                                    start=(fc == 0), stop=(fc == 15))
                S.inc(last, "tc")

            @block.vector
            def _(v, ig=ig):
                S.wait(v, "tc", S.n["tc"])
                S.inc(v.tensor_single_scalar(
                    sCT[:, ig * 512:(ig + 1) * 512], pP[2][0:64, :],
                    sS2cb[:, 0:1], OP.add), "vc")
                ct_t[ig] = S.n["vc"]

            @block.tensor
            def _(t, ig=ig):
                S.wait(t, "vc", ct_t[ig])
                if ig > 0:
                    S.wait(t, "ar", S.n["ar"])
                S.inc(t.matmul(pP[3][0:64, :], sWr1[:, :],
                               sCT[:, ig * 512:(ig + 1) * 512],
                               start=True, stop=True), "tr")

            @block.scalar
            def _(a, ig=ig):
                S.wait(a, "tr", S.n["tr"])
                S.inc(a.activation(sRh[:], pP[3][0:64, :], AF.Relu,
                                   bias=sRb1[:, 0:1]), "ar")

            for tt in range(4):
                it = ig * 4 + tt

                @block.tensor
                def _(t, it=it, tt=tt, ig=ig):
                    S.wait(t, "ar", S.n["ar"])
                    if it > 0:
                        S.wait(t, "vr", S.n["vr"])
                    t.matmul(pP[2][0:128, 0:8],
                             sRh[:, tt * 128:(tt + 1) * 128],
                             sWr2[:, :], start=True, stop=False)
                    S.inc(t.matmul(pP[2][0:128, 0:8], sOnesB[:, 0:128],
                                   sRb2[:, :], start=False, stop=True), "tr")

                @block.vector
                def _(v, it=it):
                    S.wait(v, "tr", S.n["tr"])
                    S.wait(v, "as_", S.n["as_"])
                    S.chain(v, v.tensor_copy(sLgt[:], pP[2][0:128, 0:8]), "vs")
                    S.chain(v, v.reduce_max(sM1[:], sLgt[:], axis=AX.X), "vs")
                    S.chain(v, v.tensor_single_scalar(sOh1[:], sLgt[:],
                                                      sM1[:, 0:1], OP.is_ge), "vs")
                    S.chain(v, v.scalar_tensor_tensor(
                        sOh2[:], sOh1[:], -1e30, sLgt[:], OP.mult, OP.add), "vs")
                    S.chain(v, v.reduce_max(sM2[:], sOh2[:], axis=AX.X), "vs")
                    S.chain(v, v.tensor_single_scalar(sOh2[:], sOh2[:],
                                                      sM2[:, 0:1], OP.is_ge), "vs")
                    S.inc(v.tensor_sub(sM2[:], sM2[:], sM1[:]), "vr")

                @block.scalar
                def _(a, it=it):
                    S.wait(a, "vr", S.n["vr"])
                    S.chain(a, a.activation(sW2c[:], sM2[:], AF.Sigmoid), "as_")
                    S.inc(a.activation(sW1c[:], sM2[:], AF.Sigmoid, scale=-1.0),
                          "as_")

                @block.vector
                def _(v, it=it):
                    S.wait(v, "as_", S.n["as_"])
                    S.chain(v, v.tensor_single_scalar(sOh1[:], sOh1[:],
                                                      sW1c[:, 0:1], OP.mult), "vs")
                    S.inc(v.scalar_tensor_tensor(
                        sCB[:, it * 8:(it + 1) * 8], sOh2[:], sW2c[:, 0:1],
                        sOh1[:], OP.mult, OP.add), "vr")
                    rt_t[it] = S.n["vr"]

        f01_done_vr = S.n["vr"]
        f01_done_aq = S.n["aq"]

        @block.vector
        def _(v):
            S.wait(v, "vr", f01_done_vr)
            S.inc(v.memset(sMoe[:], 0.0), "ve")

        eh_t, eo_t = {}, {}
        for e in range(NE2):
            for ig in range(4):
                for ht in range(8):
                    @block.tensor
                    def _(t, e=e, ig=ig, ht=ht):
                        S.wait(t, "vc", ct_t[ig])
                        if (e, ig, ht) != (0, 0, 0):
                            S.wait(t, "ae", max(S.n["ae"] - 1, 0))
                        if e == 0 and ig == 0 and ht == 0:
                            S.wait(t, "vr", f01_done_vr)
                        S.inc(t.matmul(
                            pP[ht % 2][:],
                            sE1[:, e * EH + ht * 128:e * EH + (ht + 1) * 128],
                            sCT[:, ig * 512:(ig + 1) * 512],
                            start=True, stop=True), "te")

                    @block.scalar
                    def _(a, e=e, ht=ht):
                        S.wait(a, "te", S.n["te"])
                        S.inc(a.activation(sEh[:, ht * 512:(ht + 1) * 512],
                                           pP[ht % 2][:], AF.Relu,
                                           bias=sEb1[:, e * 8 + ht:e * 8 + ht + 1]),
                              "ae")

                for tt in range(4):
                    it = ig * 4 + tt

                    @block.tensor
                    def _(t, e=e, ig=ig, tt=tt, it=it):
                        S.wait(t, "ae", S.n["ae"])
                        if (e, it) != (0, 0):
                            S.wait(t, "ve", S.n["ve"])
                        last = None
                        for hc in range(8):
                            last = t.matmul(
                                pP[2 + tt % 2][:, 0:MD],
                                sEh[:, hc * 512 + tt * 128:
                                    hc * 512 + (tt + 1) * 128],
                                sE2[:, hc * NE2 * MD + e * MD:
                                    hc * NE2 * MD + (e + 1) * MD],
                                start=(hc == 0), stop=False)
                        S.inc(t.matmul(pP[2 + tt % 2][:, 0:MD], sOnesB[:, :],
                                       sEb2[:, e * MD:(e + 1) * MD],
                                       start=False, stop=True), "te")

                    @block.vector
                    def _(v, e=e, it=it, tt=tt):
                        S.wait(v, "te", S.n["te"])
                        S.wait(v, "ve", 1)
                        sl = sMoe[:, it * MD:(it + 1) * MD]
                        S.inc(v.scalar_tensor_tensor(
                            sl, pP[2 + tt % 2][:, 0:MD],
                            sCB[:, it * 8 + e:it * 8 + e + 1], sl,
                            OP.mult, OP.add), "ve")

        f2_done_ve = S.n["ve"]

        mt_t = {}
        for it in range(NT):
            @block.vector
            def _(v, it=it):
                if it == 0:
                    S.wait(v, "ve", f2_done_ve)
                else:
                    S.wait(v, "td", S.n["td"])
                S.inc(v.tensor_copy(sMoeb[:, 0:MD], sMoe[:, it * MD:(it + 1) * MD]),
                      "vd")

            @block.tensor
            def _(t, it=it):
                S.wait(t, "vd", S.n["vd"])
                if it >= 2:
                    S.wait(t, "ve", S.n["ve"])
                S.inc(t.transpose(pTb16[0:64, it % 2 * 128:(it % 2 + 1) * 128],
                                  sMoeb[:, 0:MD], sIdb[:]), "td")

            @block.vector
            def _(v, it=it):
                S.wait(v, "td", S.n["td"])
                S.inc(v.tensor_copy(sMT[:, it * 128:(it + 1) * 128],
                                    pTb16[0:64, it % 2 * 128:(it % 2 + 1) * 128]),
                      "ve")
                mt_t[it] = S.n["ve"]

        dc_t, out_t = {}, {}
        for ig in range(4):
            for ft in range(16):
                @block.tensor
                def _(t, ig=ig, ft=ft):
                    S.wait(t, "ve", mt_t[ig * 4 + 3])
                    if (ig, ft) != (0, 0):
                        S.wait(t, "ad", S.n["ad"])
                    S.inc(t.matmul(pP[ft % 2][:],
                                   sWc2s[:, ft * 128:(ft + 1) * 128],
                                   sMT[:, ig * 512:(ig + 1) * 512],
                                   start=True, stop=True), "td")

                @block.scalar
                def _(a, ig=ig, ft=ft):
                    S.wait(a, "td", S.n["td"])
                    S.inc(a.activation(sRTg[:, ft * 512:(ft + 1) * 512],
                                       pP[ft % 2][:], AF.Sigmoid,
                                       bias=sC2sb[:, ft:ft + 1]), "ad")

            for tt in range(4):
                it = ig * 4 + tt

                @block.sync
                def _(sync, it=it):
                    S.wait(sync, "lds", S.n["lds"])
                    if it >= 2:
                        S.wait(sync, "vd", out_t.get(it - 2, 0))
                        S.wait(sync, "st", 16 * (it - 1))
                    S.inc(sync.dma_start(out=sXs[it % 2][:],
                                         in_=x1[it * 128:(it + 1) * 128, :]),
                          "lds", 16)
                    S.inc(sync.dma_start(out=sSs[it % 2][:],
                                         in_=s1[it * 128:(it + 1) * 128, :]),
                          "lds", 16)
                lds_it = S.n["lds"]

                @block.tensor
                def _(t, ig=ig, tt=tt, it=it):
                    S.wait(t, "ad", S.n["ad"])
                    if it > 0:
                        S.wait(t, "vd", S.n["vd"])
                    last = None
                    for half in range(2):
                        for fc in range(16):
                            last = t.matmul(
                                pP[2 + half][:],
                                sRTg[:, fc * 512 + tt * 128:
                                    fc * 512 + (tt + 1) * 128],
                                sWdec[:, fc * C + half * 512:
                                      fc * C + (half + 1) * 512],
                                start=(fc == 0), stop=False)
                        last = t.matmul(pP[2 + half][:], sOnesB[:, :],
                                        sDecb[:, half * 512:(half + 1) * 512],
                                        start=False, stop=True)
                        S.inc(last, "td")

                @block.scalar
                def _(a, it=it):
                    S.wait(a, "td", S.n["td"] - 1)
                    a.activation(sDc[:, 0:512], pP[2][:], AF.Sigmoid)
                    S.wait(a, "td", S.n["td"])
                    S.inc(a.activation(sDc[:, 512:1024], pP[3][:], AF.Sigmoid),
                          "ad")

                @block.vector
                def _(v, it=it):
                    S.wait(v, "ad", S.n["ad"])
                    S.inc(v.engine_nop(), "vd")
                    S.chain(v, v.reduce_sum(stA[:], sDc[:], axis=AX.X), "vs")
                    S.chain(v, v.tensor_scalar_mul(stB[:], stA[:], 1.0 / C), "vs")
                    S.inc(v.tensor_single_scalar(sSc[:], sDc[:], stB[:, 0:1],
                                                 OP.subtract), "vr")

                @block.scalar
                def _(a, it=it):
                    S.wait(a, "vr", S.n["vr"])
                    S.inc(a.activation(sSc2[:], sSc[:], AF.Square,
                                       accum_out=stC[:]), "ad")

                @block.vector
                def _(v, it=it):
                    S.wait(v, "ad", S.n["ad"])
                    S.inc(v.tensor_scalar(stC[:], stC[:], 1.0 / C, EPS,
                                          OP.mult, OP.add), "vr")

                @block.scalar
                def _(a, it=it):
                    S.wait(a, "vr", S.n["vr"])
                    S.chain(a, a.activation(stD[:], stC[:], AF.Ln), "as_")
                    S.inc(a.activation(stD[:], stD[:], AF.Exp, scale=-0.5), "ad")

                @block.vector
                def _(v, it=it, lds_it=lds_it):
                    S.wait(v, "ad", S.n["ad"])
                    S.wait(v, "lds", lds_it)
                    S.chain(v, v.tensor_single_scalar(sSc[:], sSc[:],
                                                      stD[:, 0:1], OP.mult), "vs")
                    S.chain(v, v.tensor_mul(sSc2[:], sSc[:], sLng[:]), "vs")
                    S.chain(v, v.tensor_add(sSc2[:], sSc2[:], sLnb[:]), "vs")
                    S.inc(v.engine_nop(), "vr")

                @block.scalar
                def _(a, it=it):
                    S.wait(a, "vr", S.n["vr"])
                    if it >= 2:
                        S.wait(a, "st", 16 * (it - 1))
                    S.inc(a.activation(sDc[:], sSc2[:], AF.Silu), "ad")

                @block.vector
                def _(v, it=it):
                    S.wait(v, "ad", S.n["ad"])
                    S.chain(v, v.tensor_mul(sSs[it % 2][:], sSs[it % 2][:],
                                            sDc[:]), "vs")
                    S.inc(v.tensor_add(sXs[it % 2][:], sXs[it % 2][:],
                                       sSs[it % 2][:]), "vd")
                    out_t[it] = S.n["vd"]

                @block.gpsimd
                def _(gp, it=it):
                    S.wait(gp, "st", S.n["st"])
                    S.wait(gp, "vd", out_t[it])
                    S.inc(gp.dma_start(out=x2[it * 128:(it + 1) * 128, :],
                                       in_=sXs[it % 2][:]), "st", 16)

        @block.gpsimd
        def _(gp):
            gp.wait_ge(S.h["st"], S.n["st"])

        S.check_futures()




def build_ffn1():
    nc = bass.Bass("TRN2", target_bir_lowering=False)
    with contextlib.ExitStack() as stack:
        D = ffn1_drams(nc)
        D["identb"] = nc.dram_tensor("identb", [128, 128], BF16,
                                     kind="ExternalInput").ap()
        D["onesf"] = nc.dram_tensor("onesf", [1, 128], F32,
                                    kind="ExternalInput").ap()
        D["onesb"] = nc.dram_tensor("onesb", [1, 128], BF16,
                                    kind="ExternalInput").ap()
        S = Sems(nc, stack)
        block = stack.enter_context(nc.Block())
        emit_ffn1(nc, block, S, stack, D)
    return nc

def emit_ln1f(nc, block, S, stack, DR):
    xfull = DR["xfull"]
    ident = DR["identb"]
    xnT_d, xn0_d, xlnbT_d = DR["xnT_d"], DR["xn0_d"], DR["xlnbT_d"]
    NTF = N // 128
    E = stack.enter_context
    sx = [E(nc.sbuf_tensor(f"L1sx{j}", [128, C], F32)) for j in range(2)]
    sxc = E(nc.sbuf_tensor("L1sxc", [128, C], F32))
    sxc2 = E(nc.sbuf_tensor("L1sxc2", [128, C], F32))
    scrap = E(nc.sbuf_tensor("L1scrap", [128, C], F32))
    sxn0 = [E(nc.sbuf_tensor(f"L1sxn0{j}", [128, C], F32)) for j in range(2)]
    sxn0b = E(nc.sbuf_tensor("L1sxn0b", [128, C], BF16))
    sxlnb = E(nc.sbuf_tensor("L1sxlnb", [128, C], BF16))
    sxt = [E(nc.sbuf_tensor(f"L1sxt{j}", [128, C], BF16)) for j in range(2)]
    sxt2 = [E(nc.sbuf_tensor(f"L1sxt2{j}", [128, C], BF16)) for j in range(2)]
    sident = E(nc.sbuf_tensor("L1sident", [128, 128], BF16))
    st1 = E(nc.sbuf_tensor("L1st1", [128, 1], F32))
    smu = E(nc.sbuf_tensor("L1smu", [128, 1], F32))
    sss = E(nc.sbuf_tensor("L1sss", [128, 1], F32))
    svar = E(nc.sbuf_tensor("L1svar", [128, 1], F32))
    slog = E(nc.sbuf_tensor("L1slog", [128, 1], F32))
    srstd = E(nc.sbuf_tensor("L1srstd", [128, 1], F32))
    pt = [E(nc.psum_tensor(f"L1pt{j}", [128, 512], BF16)) for j in range(2)]

    S.new("ld", "ldi", "v", "a", "t", "st", "st2", "vs", "as_")
    v_t, a_t, t_t, e_t, st_t = {}, {}, {}, {}, {}

    @block.sync
    def _(sync):
        S.inc(sync.dma_start(out=sident[:], in_=ident[:, :]), "ldi", 16)
        sync.wait_ge(S.h["ldi"], 16)

    for i in range(NTF):
        local = i < NT

        @block.sync
        def _(sync, i=i):
            S.wait(sync, "ld", S.n["ld"])
            if i >= 2:
                S.wait(sync, "v", v_t[i - 2])
            S.inc(sync.dma_start(out=sx[i % 2][:],
                                 in_=xfull[i * 128:(i + 1) * 128, :]), "ld", 16)
        ld_i = S.n["ld"]

        @block.vector
        def _(v, i=i, ld_i=ld_i, local=local):
            S.wait(v, "ld", ld_i)
            S.chain(v, v.reduce_sum(st1[:], sx[i % 2][:], axis=AX.X), "vs")
            S.chain(v, v.tensor_scalar_mul(smu[:], st1[:], 1.0 / C), "vs")
            S.inc(v.tensor_single_scalar(sxc[:], sx[i % 2][:], smu[:, 0:1],
                                         OP.subtract), "v")
            v_t[i] = S.n["v"]

        @block.scalar
        def _(a, i=i):
            S.wait(a, "v", v_t[i])
            S.inc(a.activation(scrap[:], sxc[:], AF.Square, accum_out=sss[:]),
                  "a")

        @block.vector
        def _(v, i=i):
            S.wait(v, "a", S.n["a"])
            S.inc(v.tensor_scalar(svar[:], sss[:], 1.0 / C, EPS,
                                  OP.mult, OP.add), "v")

        @block.scalar
        def _(a, i=i):
            S.wait(a, "v", S.n["v"])
            S.chain(a, a.activation(slog[:], svar[:], AF.Ln), "as_")
            S.inc(a.activation(srstd[:], slog[:], AF.Exp, scale=-0.5), "a")
        a_t[i] = S.n["a"]

        @block.vector
        def _(v, i=i, local=local):
            S.wait(v, "a", a_t[i])
            if i >= 1:
                S.wait(v, "t", t_t[i - 1])
            if local and i >= 2:
                S.wait(v, "st2", st_t[i - 2])
            if local:
                S.chain(v, v.tensor_single_scalar(sxn0[i % 2][:], sxc[:],
                                                  srstd[:, 0:1], OP.mult), "vs")
                v.tensor_copy(sxn0b[:], sxn0[i % 2][:])
            else:
                v.tensor_single_scalar(sxn0b[:], sxc[:], srstd[:, 0:1], OP.mult)
            S.inc(v.engine_nop(), "v")
            v_t[(i, "b")] = S.n["v"]
            if local:
                S.chain(v, v.reduce_sum(st1[:], sxn0[i % 2][:], axis=AX.X), "vs")
                S.chain(v, v.tensor_scalar_mul(smu[:], st1[:], 1.0 / C), "vs")
                S.inc(v.tensor_single_scalar(sxc2[:], sxn0[i % 2][:],
                                             smu[:, 0:1], OP.subtract), "v")

        if local:
            @block.gpsimd
            def _(gp, i=i):
                S.wait(gp, "st2", S.n["st2"])
                S.wait(gp, "v", v_t[(i, "b")])
                S.inc(gp.dma_start(out=xn0_d[i * 128:(i + 1) * 128, :],
                                   in_=sxn0[i % 2][:]), "st2", 16)
            st_t[i] = S.n["st2"]

            @block.scalar
            def _(a, i=i):
                S.wait(a, "v", S.n["v"])
                S.inc(a.activation(scrap[:], sxc2[:], AF.Square,
                                   accum_out=sss[:]), "a")

            @block.vector
            def _(v, i=i):
                S.wait(v, "a", S.n["a"])
                S.inc(v.tensor_scalar(svar[:], sss[:], 1.0 / C, EPS,
                                      OP.mult, OP.add), "v")

            @block.scalar
            def _(a, i=i):
                S.wait(a, "v", S.n["v"])
                S.chain(a, a.activation(slog[:], svar[:], AF.Ln), "as_")
                S.inc(a.activation(srstd[:], slog[:], AF.Exp, scale=-0.5), "a")

            @block.vector
            def _(v, i=i):
                S.wait(v, "a", S.n["a"])
                if i >= 1:
                    S.wait(v, "t", t_t[i - 1])
                S.inc(v.tensor_single_scalar(sxlnb[:], sxc2[:], srstd[:, 0:1],
                                             OP.mult), "v")
                v_t[(i, "c")] = S.n["v"]

        @block.tensor
        def _(t, i=i, local=local):
            S.wait(t, "ldi", 16)
            S.wait(t, "v", v_t[(i, "c")] if local else v_t[(i, "b")])
            if i >= 1:
                S.wait(t, "v", e_t[i - 1])
            last = None
            for k in range(4):
                last = t.transpose(pt[0][:, k * 128:(k + 1) * 128],
                                   sxn0b[:, k * 128:(k + 1) * 128], sident[:])
                if local:
                    last = t.transpose(pt[1][:, k * 128:(k + 1) * 128],
                                       sxlnb[:, k * 128:(k + 1) * 128],
                                       sident[:])
            S.inc(last, "t")
            t_t[(i, 0)] = S.n["t"]

        @block.vector
        def _(v, i=i, local=local):
            S.wait(v, "t", t_t[(i, 0)])
            if i >= 2:
                S.wait(v, "st", st_t[(i - 2, "T")])
            last = None
            for k in range(4):
                last = v.tensor_copy(sxt[i % 2][:, k * 128:(k + 1) * 128],
                                     pt[0][:, k * 128:(k + 1) * 128])
                if local:
                    last = v.tensor_copy(sxt2[i % 2][:, k * 128:(k + 1) * 128],
                                         pt[1][:, k * 128:(k + 1) * 128])
            S.inc(last, "v")
            e_t[(i, 0)] = S.n["v"]

        @block.tensor
        def _(t, i=i, local=local):
            S.wait(t, "v", e_t[(i, 0)])
            last = None
            for k in range(4, 8):
                last = t.transpose(pt[0][:, (k - 4) * 128:(k - 3) * 128],
                                   sxn0b[:, k * 128:(k + 1) * 128], sident[:])
                if local:
                    last = t.transpose(pt[1][:, (k - 4) * 128:(k - 3) * 128],
                                       sxlnb[:, k * 128:(k + 1) * 128],
                                       sident[:])
            S.inc(last, "t")
            t_t[i] = S.n["t"]

        @block.vector
        def _(v, i=i, local=local):
            S.wait(v, "t", t_t[i])
            last = None
            for k in range(4, 8):
                last = v.tensor_copy(sxt[i % 2][:, k * 128:(k + 1) * 128],
                                     pt[0][:, (k - 4) * 128:(k - 3) * 128])
                if local:
                    last = v.tensor_copy(sxt2[i % 2][:, k * 128:(k + 1) * 128],
                                         pt[1][:, (k - 4) * 128:(k - 3) * 128])
            S.inc(last, "v")
            e_t[i] = S.n["v"]

        @block.gpsimd
        def _(gp, i=i, local=local):
            S.wait(gp, "st", S.n["st"])
            S.wait(gp, "v", e_t[i])
            last = gp.dma_start(
                out=xnT_d[:, i * 128:(i + 1) * 128].rearrange(
                    "(k p) n -> p k n", p=128),
                in_=sxt[i % 2][:].rearrange("p (k n) -> p k n", k=8))
            S.inc(last, "st", 16)
            if local:
                S.inc(gp.dma_start(
                    out=xlnbT_d[:, i * 128:(i + 1) * 128].rearrange(
                        "(k p) n -> p k n", p=128),
                    in_=sxt2[i % 2][:].rearrange("p (k n) -> p k n", k=8)),
                    "st", 16)
            st_t[(i, "T")] = S.n["st"]

    @block.gpsimd
    def _(gp):
        gp.wait_ge(S.h["st"], S.n["st"])
        gp.wait_ge(S.h["st2"], S.n["st2"])


def stage_barrier(nc, block, S_list, bar, idx):
    dma_sems = []
    for S in S_list:
        for nm, n in S.n.items():
            if n > 0:
                dma_sems.append((S.h[nm], n))

    for engname in ("sync", "gpsimd", "vector", "scalar", "tensor"):
        def body(eng, dma_sems=dma_sems, idx=idx):
            for h, n in dma_sems:
                eng.wait_ge(h, n)
            eng.sem_inc(bar, 1)
            eng.wait_ge(bar, 5 * (idx + 1))
        getattr(block, engname)(body)


def emit_tail_exchange(nc, block, S, stack, DR, no_cc=False):
    xn1T_d, gath_d, tail_d = DR["xn1T_d"], DR["gath_d"], DR["xn1tail_d"]
    tailidx, tmask = DR["tailidx"], DR["tmask"]
    E = stack.enter_context
    sidx = E(nc.sbuf_tensor("TXsidx", [128, 8], mybir.dt.int32))
    smask = E(nc.sbuf_tensor("TXsmask", [128, 1], F32))
    stl = E(nc.sbuf_tensor("TXstl", [128, 1024], BF16))
    stl2 = E(nc.sbuf_tensor("TXstl2", [128, 1024], BF16))
    S.new("cc", "ld", "gi", "vm", "stt")

    @block.sync
    def _(sync):
        S.inc(sync.dma_start(out=sidx[:], in_=tailidx[:, :]), "ld", 16)
        S.inc(sync.dma_start(out=smask[:], in_=tmask[:, :]), "ld", 16)

    tailsrc_d = DR["tailsrc_d"]

    @block.gpsimd
    def _(gp):
        S.inc(gp.dma_start(out=tailsrc_d[:, :], in_=xn1T_d[:, 0:1024]), "cc", 16)
        gp.wait_ge(S.h["cc"], 16)
        if no_cc:
            S.inc(gp.dma_start(out=gath_d[1024:2048, :],
                               in_=tailsrc_d[:, :]), "cc", 16)
        else:
            S.inc(gp.collective_compute(
                "AllGather", OP.bypass,
                [[0, 1, 2, 3], [4, 5, 6, 7]],
                [tailsrc_d[:, :].opt()], [gath_d[:, :].opt()]), "cc", 1)
        gp.wait_ge(S.h["cc"], S.n["cc"])
        gp.wait_ge(S.h["ld"], 32)
        for j in range(8):
            S.inc(gp.indirect_dma_start(
                stl[:], None, gath_d[:, :],
                bass.IndirectOffsetOnAxis(ap=sidx[:, j:j + 1], axis=0)),
                "gi", 16)
            gp.wait_ge(S.h["gi"], S.n["gi"])

            @block.vector
            def _(v, j=j):
                S.wait(v, "gi", S.n["gi"])
                if j >= 1:
                    S.wait(v, "stt", S.n["stt"])
                S.inc(v.tensor_single_scalar(stl2[:], stl[:], smask[:, 0:1],
                                             OP.mult), "vm")

            S.wait(gp, "vm", S.n["vm"])
            S.inc(gp.dma_start(out=tail_d[j * 128:(j + 1) * 128, :],
                               in_=stl2[:]), "stt", 16)
        gp.wait_ge(S.h["stt"], S.n["stt"])


class _Pfx:

    def __init__(self, nc, p):
        self._nc = nc
        self._p = p

    def sbuf_tensor(self, name, shape, dtype, **kw):
        return self._nc.sbuf_tensor(self._p + name, shape, dtype, **kw)

    def psum_tensor(self, name, shape, dtype, **kw):
        return self._nc.psum_tensor(self._p + name, shape, dtype, **kw)

    def __getattr__(self, a):
        return getattr(self._nc, a)


def build_fused(no_cc=False):
    nc = bass.Bass("TRN2", target_bir_lowering=False)
    with contextlib.ExitStack() as stack:
        identb = nc.dram_tensor("identb", [128, 128], BF16,
                                kind="ExternalInput").ap()
        identf = nc.dram_tensor("identf", [128, 128], F32,
                                kind="ExternalInput").ap()
        onesf = nc.dram_tensor("onesf", [1, 128], F32, kind="ExternalInput").ap()
        onesb = nc.dram_tensor("onesb", [1, 128], BF16, kind="ExternalInput").ap()
        xfull = nc.dram_tensor("xfull", [N, C], F32, kind="ExternalInput").ap()
        tailidx = nc.dram_tensor("tailidx", [128, 8], mybir.dt.int32,
                                 kind="ExternalInput").ap()
        tmask = nc.dram_tensor("tmask", [128, 1], F32, kind="ExternalInput").ap()
        xnT_d = nc.dram_tensor("xnT_d", [C, N], BF16, kind="Internal").ap()
        xn0_d = nc.dram_tensor("xn0_d", [TLOC, C], F32, kind="Internal").ap()
        xlnbT_d = nc.dram_tensor("xlnbT_d", [C, TLOC], BF16, kind="Internal").ap()
        s0_d = nc.dram_tensor("s0_d", [TLOC, C], F32, kind="Internal").ap()
        g0_d = nc.dram_tensor("g0_d", [TLOC, C], F32, kind="Internal").ap()
        x1_d = nc.dram_tensor("x1_d", [TLOC, C], F32, kind="Internal").ap()
        xn1T_d = nc.dram_tensor("xn1T_dd", [C, TLOC], BF16, kind="Internal").ap()
        gath_d = nc.dram_tensor("gath_d", [4096, 1024], BF16, kind="Internal").ap()
        xn1tail_d = nc.dram_tensor("xn1tail_d", [1024, 1024], BF16,
                                   kind="Internal").ap()
        tailsrc_d = nc.dram_tensor("tailsrc_d", [1024, 1024], BF16,
                                   kind="Internal").ap()
        s1_d = nc.dram_tensor("s1_d", [TLOC, C], F32, kind="Internal").ap()

        shared = {"identb": identb, "identf": identf, "onesf": onesf,
                  "onesb": onesb}
        bar = stack.enter_context(nc.semaphore("stagebar"))
        block = stack.enter_context(nc.Block())
        bidx = [0]

        def barrier(S_stage):
            stage_barrier(nc, block, [S_stage], bar, bidx[0])
            bidx[0] += 1

        with contextlib.ExitStack() as st1:
            S1 = Sems(nc, stack, prefix="a_")
            D1 = dict(shared, xfull=xfull, xnT_d=xnT_d, xn0_d=xn0_d,
                      xlnbT_d=xlnbT_d)
            emit_ln1f(_Pfx(nc, "a_"), block, S1, st1, D1)
            barrier(S1)

        with contextlib.ExitStack() as st2:
            S2 = Sems(nc, stack, prefix="b_")
            D2 = attn0_drams(nc, kind=None, okind=None)
            D2.update(shared)
            D2["xnT"] = xnT_d
            D2["s0"] = s0_d
            emit_attn0(_Pfx(nc, "b_"), block, S2, st2, D2)
            barrier(S2)

        with contextlib.ExitStack() as st3:
            S3 = Sems(nc, stack, prefix="c_")
            D3 = moe0_drams(nc, kind=None, okind=None)
            D3.update(shared)
            D3["xlnbT"] = xlnbT_d
            D3["xn0"] = xn0_d
            D3["g0"] = g0_d
            emit_moe0(_Pfx(nc, "c_"), block, S3, st3, D3)
            barrier(S3)

        with contextlib.ExitStack() as st4:
            S4 = Sems(nc, stack, prefix="d_")
            D4 = dict(shared)
            D4["x"] = xfull[0:TLOC, :]
            D4["s0c"] = s0_d
            D4["g0c"] = g0_d
            D4["x1"] = x1_d
            D4["xn1T"] = xn1T_d
            emit_comb1(_Pfx(nc, "d_"), block, S4, st4, D4)
            barrier(S4)

        with contextlib.ExitStack() as st4b:
            S4b = Sems(nc, stack, prefix="e_")
            D4b = {"xn1T_d": xn1T_d, "gath_d": gath_d, "xn1tail_d": xn1tail_d,
                   "tailsrc_d": tailsrc_d, "tailidx": tailidx, "tmask": tmask}
            emit_tail_exchange(_Pfx(nc, "e_"), block, S4b, st4b, D4b, no_cc=no_cc)
            barrier(S4b)

        with contextlib.ExitStack() as st5:
            S5 = Sems(nc, stack, prefix="f_")
            D5 = attn1_drams(nc, okind=None, fused=True)
            D5.update(shared)
            D5["xn1srcs"] = lambda g: ((xn1T_d, g * 512) if g < 4
                                       else (xn1tail_d, (g - 4) * 512))
            D5["s1"] = s1_d
            emit_attn1(_Pfx(nc, "f_"), block, S5, st5, D5)
            barrier(S5)

        with contextlib.ExitStack() as st6:
            S6 = Sems(nc, stack, prefix="g_")
            D6 = ffn1_drams(nc, kind=None, okind="ExternalOutput")
            D6.update(shared)
            D6["xn1Tf"] = xn1T_d
            D6["x1f"] = x1_d
            D6["s1f"] = s1_d
            emit_ffn1(_Pfx(nc, "g_"), block, S6, st6, D6)

    return nc


def kernel_fused(**inputs):
    inp = {k: np.ascontiguousarray(np.asarray(v)) for k, v in inputs.items()}
    nc = build_fused()
    dec, tabs = _attn1_consts()
    base = {
        "identb": _ident_np(),
        "identf": np.eye(128, dtype=np.float32),
        "onesf": np.ones((1, 128), dtype=np.float32),
        "onesb": np.ones((1, 128), dtype=bf16),
        "qkvwT": np.ascontiguousarray(inp["L0_attn_qkv_w"].T),
        "projwT": np.ascontiguousarray(inp["L0_attn_proj_w"].T),
        "qn_w": inp["L0_attn_q_norm_w"].reshape(1, -1),
        "qn_b": inp["L0_attn_q_norm_b"].reshape(1, -1),
        "kn_w": inp["L0_attn_k_norm_w"].reshape(1, -1),
        "kn_b": inp["L0_attn_k_norm_b"].reshape(1, -1),
        "proj_b": inp["L0_attn_proj_b"].reshape(1, -1),
        "cen": inp["L0_ffn_centroids"],
        "lngR": np.ascontiguousarray(
            inp["L0_ffn_ln_g"].reshape(8, 8, 128).transpose(2, 0, 1
                                                            ).reshape(128, 64)),
        "lnbR": np.ascontiguousarray(
            inp["L0_ffn_ln_b"].reshape(8, 8, 128).transpose(2, 0, 1
                                                            ).reshape(128, 64)),
        "w1": inp["L0_ffn_w1"],
        "b1R": np.ascontiguousarray(
            inp["L0_ffn_b1"].reshape(8, 32, 128).transpose(2, 0, 1
                                                           ).reshape(128, 256)),
        "w2": inp["L0_ffn_w2"],
        "b2": inp["L0_ffn_b2"],
        "WQ": inp["L1_attn_W_Q"], "WK": inp["L1_attn_W_K"],
        "WV": inp["L1_attn_W_V"],
        "dec11": dec,
        "gnwR": np.ascontiguousarray(np.tile(inp["L1_attn_gn_weight"], (128, 1))),
        "gnbR": np.ascontiguousarray(np.tile(inp["L1_attn_gn_bias"], (128, 1))),
    }
    base.update(_ffn1_base(inp))
    xf = inp["L0_attn_x"].reshape(B * N, C)
    in_maps = []
    for c in range(NCORES):
        b = c // 4
        off = (c % 4) * TLOC
        xr = np.ascontiguousarray(np.roll(xf[b * N:(b + 1) * N], -off, axis=0))
        rp = (c % 4) + 1
        valid = (c % 4) != 3
        idx = (np.arange(1024, dtype=np.int32) + (rp % 4) * 1024
               ).reshape(8, 128).T.copy()
        cosT, sinS = tabs[c]
        in_maps.append(dict(
            base, xfull=xr, tailidx=idx,
            tmask=np.full((128, 1), 1.0 if valid else 0.0, np.float32),
            cosT=cosT, sinS=sinS))
    res = run_bass_kernel_spmd(nc, in_maps, core_ids=list(range(NCORES)))
    x2 = np.concatenate([res.results[c]["x2"] for c in range(NCORES)], 0)
    return x2.reshape(B, N, C).astype(np.float32)


def _ident_np():
    return np.eye(128, dtype=bf16)


def run_comb1(x_full, s0_by_core, g0_by_core):
    nc = build_comb1()
    xf = x_full.reshape(-1, C)
    ident = _ident_np()
    in_maps = [{"x": xf[c * TLOC:(c + 1) * TLOC], "s0c": s0_by_core[c],
                "g0c": g0_by_core[c], "identb": ident} for c in range(NCORES)]
    return run_bass_kernel_spmd(nc, in_maps, core_ids=list(range(NCORES))).results


def _attn1_consts():
    GAMMA = 0.96875
    invf = (1.0 / 10000.0 ** (np.arange(0, 64, 2, dtype=np.float32) / 64)
            ).astype(np.float32)
    r = np.arange(128)
    dec = np.zeros((128, 11 * 128), np.float32)
    for m in range(11):
        ex = 128 * (m - 1) + r[:, None] - r[None, :]
        dec[:, m * 128:(m + 1) * 128] = np.where(
            ex >= 0, np.float64(GAMMA) ** ex, 0.0).astype(np.float32)
    sgn = np.where(np.arange(64) % 2 == 0, -1.0, 1.0).astype(np.float32)
    tabs = {}
    for c in range(NCORES):
        base = (c % 4) * TLOC
        t = np.minimum(np.arange(base, base + 3072), N - 1).astype(np.float32)
        fr = np.repeat(t[:, None] * invf[None, :], 2, axis=1)
        tabs[c] = (np.ascontiguousarray(np.cos(fr).T.astype(np.float32)),
                   np.ascontiguousarray(np.sin(fr).T.astype(np.float32)
                                        * sgn[:, None]))
    return dec, tabs


def run_attn1(inp, xn1T_by_core):
    nc = build_attn1()
    dec, tabs = _attn1_consts()
    gnw = np.ascontiguousarray(np.tile(inp["L1_attn_gn_weight"], (128, 1)))
    gnb = np.ascontiguousarray(np.tile(inp["L1_attn_gn_bias"], (128, 1)))
    in_maps = []
    for c in range(NCORES):
        if c % 4 == 3:
            tail = np.zeros((C, 1024), dtype=bf16)
        else:
            tail = xn1T_by_core[c + 1][:, 0:1024]
        xn1Te = np.ascontiguousarray(
            np.concatenate([xn1T_by_core[c], tail], axis=1))
        cosT, sinS = tabs[c]
        in_maps.append({
            "xn1Te": xn1Te, "WQ": inp["L1_attn_W_Q"], "WK": inp["L1_attn_W_K"],
            "WV": inp["L1_attn_W_V"], "cosT": cosT, "sinS": sinS, "dec11": dec,
            "gnwR": gnw, "gnbR": gnb,
        })
    return run_bass_kernel_spmd(nc, in_maps, core_ids=list(range(NCORES))).results


def _ffn1_base(inp):
    return {
        "encw": inp["L1_ffn_enc_w"],
        "encbR": np.ascontiguousarray(inp["L1_ffn_enc_b"].reshape(16, 128).T),
        "s2cw": inp["L1_ffn_s2c_w"],
        "s2cb": inp["L1_ffn_s2c_b"].reshape(64, 1),
        "rw1": inp["L1_ffn_r_w1"], "rb1": inp["L1_ffn_r_b1"].reshape(64, 1),
        "rw2": inp["L1_ffn_r_w2"], "rb2": inp["L1_ffn_r_b2"].reshape(1, 8),
        "ew1": inp["L1_ffn_e_w1"],
        "eb1R": np.ascontiguousarray(
            inp["L1_ffn_e_b1"].reshape(8, 8, 128).transpose(2, 0, 1
                                                            ).reshape(128, 64)),
        "ew2": inp["L1_ffn_e_w2"],
        "eb2r": inp["L1_ffn_e_b2"].reshape(1, 512),
        "c2sw": inp["L1_ffn_c2s_w"],
        "c2sbR": np.ascontiguousarray(inp["L1_ffn_c2s_b"].reshape(16, 128).T),
        "decw": inp["L1_ffn_dec_w"],
        "decb": inp["L1_ffn_dec_b"].reshape(1, 1024),
        "lngR1": np.ascontiguousarray(np.tile(inp["L1_ffn_ln_g"], (128, 1))),
        "lnbR1": np.ascontiguousarray(np.tile(inp["L1_ffn_ln_b"], (128, 1))),
        "identb": _ident_np(),
        "onesf": np.ones((1, 128), dtype=np.float32),
        "onesb": np.ones((1, 128), dtype=bf16),
    }


def run_ffn1(inp, xn1T_by_core, x1_by_core, s1_by_core):
    nc = build_ffn1()
    base = _ffn1_base(inp)
    in_maps = [dict(base, xn1Tf=xn1T_by_core[c], x1f=x1_by_core[c],
                    s1f=s1_by_core[c]) for c in range(NCORES)]
    return run_bass_kernel_spmd(nc, in_maps, core_ids=list(range(NCORES))).results


def kernel(**inputs):
    return kernel_fused(**inputs)


def kernel_unfused(**inputs):
    inp = {k: np.ascontiguousarray(np.asarray(v)) for k, v in inputs.items()}
    x = inp["L0_attn_x"]

    r1 = run_ln1(x)
    xn0 = np.concatenate([r["xn0"] for r in r1], 0)
    xn0T = [r1[c]["xn0T"] for c in range(NCORES)]
    xlnbT = [r1[c]["xlnbT"] for c in range(NCORES)]

    xnT_by_core = []
    for c in range(NCORES):
        b = c // 4
        slab = np.concatenate([xn0T[4 * b + j] for j in range(4)], axis=1)
        xnT_by_core.append(np.ascontiguousarray(
            np.roll(slab, -((c % 4) * TLOC), axis=1)))
    r2 = run_attn0(xnT_by_core, inp)
    s0 = [r2[c]["s0"] for c in range(NCORES)]

    r3 = run_moe0(inp, xn0, xlnbT)
    g0 = [r3[c]["g0"] for c in range(NCORES)]

    r4 = run_comb1(x, s0, g0)
    x1 = [r4[c]["x1"] for c in range(NCORES)]
    xn1T = [r4[c]["xn1T"] for c in range(NCORES)]

    r5 = run_attn1(inp, xn1T)
    s1 = [r5[c]["s1"] for c in range(NCORES)]

    r6 = run_ffn1(inp, xn1T, x1, s1)
    x2 = np.concatenate([r6[c]["x2"] for c in range(NCORES)], 0)
    return x2.reshape(B, N, C).astype(np.float32)


def run_ln1(x_full):
    nc = build_ln1()
    xf = np.ascontiguousarray(x_full.reshape(B * N, C))
    ident = _ident_np()
    in_maps = [{"x": xf[c * TLOC:(c + 1) * TLOC], "ident": ident}
               for c in range(NCORES)]
    res = run_bass_kernel_spmd(nc, in_maps, core_ids=list(range(NCORES)))
    return res.results



